# version 18
# baseline (speedup 1.0000x reference)
import numpy as np
import ml_dtypes
from contextlib import ExitStack
import concourse.bass as bass
import concourse.mybir as mybir
from concourse.bass_utils import run_bass_kernel_spmd

F32 = mybir.dt.float32
BF16 = mybir.dt.bfloat16
AF = mybir.ActivationFunctionType
AX = mybir.AxisListType
OP = mybir.AluOpType

S = 2048
D = 4096
NT = 16
DCH = 32
NH = 32
QR = 1536
KVR = 512
RD = 64
QK_SCALE = float(192.0 ** -0.5)
LN_EPS = 1e-5
RMS_EPS = 1e-6
Q_LO, Q_HI = 611, 2047
ROUTER_CNT = float(Q_HI - Q_LO)

ENGINES = ("sync", "scalar", "vector", "tensor", "gpsimd")
DMA_QUEUES = ("Q", "Adma", "G")
DMA_POOL = 8

PHASE_LIMIT = None
DEV_P5_STRIP = 0
DEV_DUMP = None


class Ctx:
    def __init__(self, nc):
        self.nc = nc
        self.sems = {}
        self.cnt = {}
        self.own = {}

    def sem(self, name, owner=None):
        h = self.nc.semaphore(name).__enter__()
        self.sems[name] = h
        self.cnt[name] = 0
        if owner:
            self.own[name] = owner


class Ph:

    def __init__(self, k, name, barrier=True):
        self.k = k
        self.name = name
        self.q = {e: [] for e in ENGINES}
        self.last_wait = {e: {} for e in ENGINES}
        self.bar = {s: k.cnt[s] for s in k.cnt} if barrier else None

    def do(self, eng, fn, wait=(), inc=()):
        k = self.k
        wait = list(wait)
        inc2 = []
        for s, n in inc:
            if s in DMA_QUEUES:
                j = k.rr[s]
                k.rr[s] = (j + 1) % DMA_POOL
                ps = f"{s}{j}"
                wait.append((ps, k.cnt[ps]))
                inc2.append((ps, n))
            else:
                inc2.append((s, n))
        inc = inc2
        w = []
        for s, v in wait:
            assert v <= k.cnt[s], f"{self.name}: wait {s}>={v} > planned {k.cnt[s]}"
            if self.last_wait[eng].get(s, -1) >= v:
                continue
            self.last_wait[eng][s] = v
            w.append((s, v))
        ii = []
        tok = None
        for s, n in inc:
            k.cnt[s] += n
            ii.append((s, n))
            if tok is None:
                tok = (s, k.cnt[s])
        self.q[eng].append((fn, w, ii))
        return tok

    def w(self, *sems):
        out = []
        for s in sems:
            if s in DMA_QUEUES:
                for j in range(DMA_POOL):
                    ps = f"{s}{j}"
                    out.append((ps, self.k.cnt[ps]))
            else:
                out.append((s, self.k.cnt[s]))
        return out

    def emit(self):
        k = self.k
        with k.nc.Block(self.name) as blk:
            for e in ENGINES:
                steps = self.q[e]
                if not steps:
                    continue

                def run(eng, steps=steps, e_name=e):
                    if self.bar is not None:
                        for s, v in self.bar.items():
                            if v > 0 and self.last_wait[e_name].get(s, -1) < v:
                                self.last_wait[e_name][s] = v
                                eng.wait_ge(k.sems[s], v)
                    for fn, waits, incs in steps:
                        for s, v in waits:
                            eng.wait_ge(k.sems[s], v)
                        ins = fn(eng)
                        for s, n in incs:
                            ins.then_inc(k.sems[s], n)
                getattr(blk, e)(run)


def mm_cols(n):
    out, c = [], 0
    while c < n:
        w = min(512, n - c)
        out.append((c, w))
        c += w
    return out


def build_nc(n_phases=None, dev_dump=None):
    nc = bass.Bass("TRN2", target_bir_lowering=False, debug=False)
    k = Ctx(nc)

    def din(name, shape, dt=F32):
        return nc.dram_tensor(name, shape, dt, kind="ExternalInput").ap()

    x0 = din("x", [S, D])
    wqa = din("wqa", [D, QR])
    gqa = din("gqa", [QR])
    wqb = din("wqb", [QR, NH * 192])
    wkva = din("wkva", [D, KVR + RD])
    gkva = din("gkva", [KVR])
    wkvb = din("wkvb", [KVR, NH * 256])
    wo_a = din("wo_a", [D, D])
    bWqd = din("bWqd", [D, 12])
    bbqd = din("bbqd", [12])
    bWqk = din("bWqk", [12, 128])
    bbqk = din("bbqk", [128])
    bWkvd = din("bWkvd", [D, 4])
    bbkvd = din("bbkvd", [4])
    bWvu = din("bWvu", [4, D])
    bbvu = din("bbvu", [D])
    bWo = din("bWo", [D, D])
    bbo = din("bbo", [D])
    fW = din("fW", [D, D])
    fb = din("fb", [D])
    fA = din("fA", [128, D])
    fB = din("fB", [8, D, 16])
    frW = din("frW", [8, D])
    frb = din("frb", [8])
    c_id_bf = din("c_id_bf", [128, 128], BF16)
    c_id_f32 = din("c_id_f32", [128, 128], F32)
    c_cos = din("c_cos", [128, S])
    c_sinm = din("c_sinm", [128, S])
    c_ones_bf = din("c_ones_bf", [1, 128], BF16)
    c_e8 = din("c_e8", [8, 128], BF16)

    out = nc.dram_tensor("out", [S, D], F32, kind="ExternalOutput").ap()

    xs_a = nc.dram_tensor("xs_a", [S, D], F32, kind="Internal").ap()
    xs_b = nc.dram_tensor("xs_b", [S, D], F32, kind="Internal").ap()
    tls = nc.dram_tensor("tls", [NT, 128, D], BF16, kind="Internal").ap()
    wt_scr = nc.dram_tensor("wt_scr", [D, D], BF16, kind="Internal").ap()

    for q in DMA_QUEUES:
        for j in range(DMA_POOL):
            k.sem(f"{q}{j}")
    for s_name, owner in [("V", "vector"), ("Sc", "scalar"), ("T", "tensor"),
                          ("Gc", "gpsimd")]:
        k.sem(s_name, owner)
    k.rr = {q: 0 for q in DMA_QUEUES}
    nc.clear_and_free_semaphores(list(k.sems.values()))
    nc.all_engine_barrier()

    id_bf = nc.alloc_sbuf_tensor("id_bf", [128, 128], BF16)
    id_f32 = nc.alloc_sbuf_tensor("id_f32", [128, 128], F32)
    ones_bf = nc.alloc_sbuf_tensor("ones_bf", [1, 128], BF16)
    rcpT_a = nc.dram_tensor("rcp_scr", [32, S], BF16, kind="Internal").ap()

    psA = nc.alloc_psum_tensor("psA", [128, 1024], F32)
    psB = nc.alloc_psum_tensor("psB", [128, 1024], F32)
    psC = nc.alloc_psum_tensor("psC", [128, 1024], F32)
    ptA = nc.alloc_psum_tensor("ptA", [128, 1024], BF16)
    ptB = nc.alloc_psum_tensor("ptB", [128, 1024], BF16)

    def tr16(ph, src_bf, n, dst_fn, wait_extra=()):
        for rnd in range((n + 7) // 8):
            pt = ptA if rnd % 2 == 0 else ptB
            lo, hi = rnd * 8, min(rnd * 8 + 8, n)

            def tr(e, lo=lo, hi=hi, pt=pt):
                for j in range(hi - lo):
                    c = lo + j
                    ins = nc.tensor.transpose(pt[:, j * 128:(j + 1) * 128],
                                              src_bf[:, c * 128:(c + 1) * 128], id_bf[:])
                return ins
            ph.do("tensor", tr, wait=ph.w("V", "Sc") + list(wait_extra), inc=[("T", 1)])
            ph.do("scalar", lambda e, rnd=rnd, pt=pt, lo=lo, hi=hi: dst_fn(e, rnd, pt, lo, hi),
                  wait=ph.w("T"), inc=[("Sc", 1)])

    def new_stat(prefix, es):
        names = ["ssq", "sm", "mean", "var", "var2", "std", "rstd", "nmr"]
        return {n: es.enter_context(nc.sbuf_tensor(f"{prefix}_{n}", [128, 1], F32))
                for n in names}

    def emit_ln_tile(ph, x_dram, tt, xt_f32, junk_bf, stat, xln_bf, xlt_dst):
        ph.do("sync", lambda e: e.dma_start(
            out=xt_f32, in_=x_dram[tt * 128:(tt + 1) * 128, :]),
            wait=ph.w("V", "Sc"), inc=[("Q", 16)])
        ph.do("scalar", lambda e: e.activation(junk_bf[:], xt_f32, AF.Square,
                                               accum_out=stat["ssq"][:]),
              wait=ph.w("Q", "V", "T"), inc=[("Sc", 1)])
        ph.do("vector", lambda e: e.reduce_sum(stat["sm"][:], xt_f32, axis=AX.X),
              wait=ph.w("Q", "Sc"), inc=[("V", 1)])

        def s1(e):
            e.tensor_scalar_mul(stat["mean"][:], stat["sm"][:], 1.0 / D)
            return e.tensor_scalar(stat["var2"][:], stat["ssq"][:], 1.0 / D, None,
                                   op0=OP.mult)
        ph.do("vector", s1, wait=ph.w("Sc", "V"), inc=[("V", 1)])
        ph.do("vector", lambda e: e.tensor_mul(stat["var"][:], stat["mean"][:],
                                               stat["mean"][:]),
              wait=ph.w("V"), inc=[("V", 1)])
        ph.do("vector", lambda e: e.tensor_sub(stat["var"][:], stat["var2"][:],
                                               stat["var"][:]),
              wait=ph.w("V"), inc=[("V", 1)])
        ph.do("vector", lambda e: e.tensor_scalar_add(stat["var"][:], stat["var"][:],
                                                      LN_EPS),
              wait=ph.w("V"), inc=[("V", 1)])
        ph.do("scalar", lambda e: e.activation(stat["std"][:], stat["var"][:], AF.Sqrt),
              wait=ph.w("V"), inc=[("Sc", 1)])
        ph.do("vector", lambda e: e.reciprocal(stat["rstd"][:], stat["std"][:]),
              wait=ph.w("Sc", "V"), inc=[("V", 1)])
        ph.do("vector", lambda e: e.tensor_mul(stat["nmr"][:], stat["mean"][:],
                                               stat["rstd"][:]),
              wait=ph.w("V"), inc=[("V", 1)])
        ph.do("vector", lambda e: e.tensor_scalar_mul(stat["nmr"][:], stat["nmr"][:],
                                                      -1.0),
              wait=ph.w("V"), inc=[("V", 1)])
        ph.do("vector", lambda e: e.tensor_scalar(xln_bf[:], xt_f32, stat["rstd"][:],
                                                  stat["nmr"][:], op0=OP.mult,
                                                  op1=OP.add),
              wait=ph.w("V", "T"), inc=[("V", 1)])

        def mk_dst(e, rnd, pt, lo, hi):
            return e.copy(xlt_dst[:, lo * 128:hi * 128], pt[:, 0:(hi - lo) * 128])
        tr16(ph, xln_bf, 32, mk_dst)

    def p0():
        ph = Ph(k, "p0")
        ph.do("sync", lambda e: e.dma_start(out=id_bf[:], in_=c_id_bf), inc=[("Q", 16)])
        ph.do("sync", lambda e: e.dma_start(out=id_f32[:], in_=c_id_f32), inc=[("Q", 16)])
        ph.do("sync", lambda e: e.dma_start(out=ones_bf[:], in_=c_ones_bf), inc=[("Q", 16)])
        ph.emit()

    def wt_gen(ph, es, pt_free):
        wlf = lambda *t: [x for x in t if x]
        wrow_f32 = es.enter_context(nc.sbuf_tensor("p0_wrow", [128, 2 * D], F32))
        wrow_bf = es.enter_context(nc.sbuf_tensor("p0_wrowb", [128, 2 * D], BF16))
        wt_stage = es.enter_context(nc.sbuf_tensor("p0_wts", [128, 2 * D], BF16))
        t_conv = {}; t_tr = {}; t_ev = {}; t_st = {}
        for rb in range(DCH):
            sl = (rb % 2) * D
            tl_ = ph.do("sync", lambda e, rb=rb, sl=sl: e.dma_start(
                out=wrow_f32[:, sl:sl + D], in_=fW[rb * 128:(rb + 1) * 128, :]),
                wait=wlf(t_conv.get(rb - 2)), inc=[("Q", 16)])
            yield
            t_conv[rb] = ph.do("vector", lambda e, sl=sl: e.tensor_copy(
                wrow_bf[:, sl:sl + D], wrow_f32[:, sl:sl + D]),
                wait=wlf(tl_, t_tr.get((rb - 2, 3))), inc=[("V", 1)])
            yield
            for rnd in range(4):
                ptk = rnd % 2
                pt = ptA if ptk == 0 else ptB

                def tr(e, rnd=rnd, pt=pt, sl=sl):
                    for j in range(8):
                        c = rnd * 8 + j
                        ins = nc.tensor.transpose(
                            pt[:, j * 128:(j + 1) * 128],
                            wrow_bf[:, sl + c * 128: sl + (c + 1) * 128],
                            id_bf[:])
                    return ins
                t_tr[(rb, rnd)] = ph.do("tensor", tr, wait=wlf(
                    t_conv[rb], pt_free.get(ptk)), inc=[("T", 1)])
                t_ev[(rb, rnd)] = ph.do(
                    "scalar", lambda e, rnd=rnd, pt=pt, sl=sl:
                    e.copy(wt_stage[:, sl + rnd * 1024: sl + (rnd + 1) * 1024],
                           pt[:]),
                    wait=wlf(t_tr[(rb, rnd)]) + list(t_st.get(rb - 2, [])),
                    inc=[("Sc", 1)])
                pt_free[ptk] = t_ev[(rb, rnd)]
                yield
            dst = wt_scr[:, rb * 128:(rb + 1) * 128].rearrange(
                "(c p) j -> p c j", p=128)
            src3 = wt_stage[:, sl:sl + D].rearrange("p (c j) -> p c j", c=32)
            sts = []
            for q4 in range(4):
                sts.append(ph.do("gpsimd", lambda e, dst=dst, src3=src3, q4=q4:
                                 e.dma_start(out=dst[:, q4 * 8:(q4 + 1) * 8, :],
                                             in_=src3[:, q4 * 8:(q4 + 1) * 8, :]),
                                 wait=[t_ev[(rb, q4)]], inc=[("G", 16)]))
            t_st[rb] = sts
            yield

    def ln_tile_tok(ph, x_dram, tt, xt, stat, xln, dst_fn, pt_free, t_prev):
        wlf = lambda *t: [x for x in t if x]
        tload = ph.do("sync", lambda e: e.dma_start(
            out=xt, in_=x_dram[tt * 128:(tt + 1) * 128, :]),
            wait=wlf(t_prev.get("xt")), inc=[("Adma", 16)])
        tsq = ph.do("scalar", lambda e: e.activation(
            xln[:], xt, AF.Square, accum_out=stat["ssq"][:]),
            wait=wlf(tload, t_prev.get("xln"), t_prev.get("stat")),
            inc=[("Sc", 1)])
        tsum = ph.do("vector", lambda e: e.reduce_sum(stat["sm"][:], xt, axis=AX.X),
                     wait=wlf(tload, t_prev.get("stat")), inc=[("V", 1)])

        def s1(e):
            e.tensor_scalar_mul(stat["mean"][:], stat["sm"][:], 1.0 / D)
            return e.tensor_scalar(stat["var2"][:], stat["ssq"][:], 1.0 / D, None,
                                   op0=OP.mult)
        ts1 = ph.do("vector", s1, wait=wlf(tsum, tsq), inc=[("V", 1)])
        tva = ph.do("vector", lambda e: e.scalar_tensor_tensor(
            stat["var"][:], stat["mean"][:], stat["mean"][:], stat["var2"][:],
            op0=OP.mult, op1=OP.subtract),
            wait=[ts1], inc=[("V", 1)])
        tvb = ph.do("vector", lambda e: e.tensor_scalar(
            stat["var"][:], stat["var"][:], -1.0, LN_EPS, op0=OP.mult, op1=OP.add),
            wait=[tva], inc=[("V", 1)])
        trs = ph.do("scalar", lambda e: e.activation(
            stat["std"][:], stat["var"][:], AF.Sqrt),
            wait=[tvb], inc=[("Sc", 1)])
        trc = ph.do("vector", lambda e: e.reciprocal(
            stat["rstd"][:], stat["std"][:]),
            wait=[trs], inc=[("V", 1)])
        tnm = ph.do("vector", lambda e: e.tensor_scalar(
            stat["nmr"][:], stat["mean"][:], stat["rstd"][:], -1.0,
            op0=OP.mult, op1=OP.mult),
            wait=[trc], inc=[("V", 1)])
        txl = ph.do("vector", lambda e: e.tensor_scalar(
            xln[:], xt, stat["rstd"][:], stat["nmr"][:], op0=OP.mult, op1=OP.add),
            wait=wlf(tnm, tsq), inc=[("V", 1)])
        trlast, evlast = None, None
        for rnd in range(4):
            ptk = rnd % 2
            pt = ptA if ptk == 0 else ptB

            def tr(e, rnd=rnd, pt=pt):
                for j in range(8):
                    c = rnd * 8 + j
                    ins = nc.tensor.transpose(pt[:, j * 128:(j + 1) * 128],
                                              xln[:, c * 128:(c + 1) * 128], id_bf[:])
                return ins
            trlast = ph.do("tensor", tr,
                           wait=wlf(txl, pt_free.get(ptk)), inc=[("T", 1)])
            dfree = t_prev.get("dst") if rnd == 0 else None
            dfree = dfree if isinstance(dfree, list) else ([dfree] if dfree else [])
            evlast = ph.do("scalar", lambda e, rnd=rnd, pt=pt: dst_fn(e, rnd, pt),
                           wait=wlf(trlast) + dfree,
                           inc=[("Sc", 1)])
            pt_free[ptk] = evlast
        return {"load": tload, "xln": trlast, "sq": txl, "evlast": evlast}

    def p1(q_cF, k_cF, krF_bf):
        ph = Ph(k, "p1")
        es_out = ExitStack()
        with es_out:
            krF_f32 = es_out.enter_context(nc.sbuf_tensor("p1_krf", [64, S], BF16))
            gqa_bc = es_out.enter_context(nc.sbuf_tensor("p1_gqa", [128, QR], BF16))
            gkva_bc = es_out.enter_context(nc.sbuf_tensor("p1_gkva", [128, KVR], BF16))
            pha = Ph(k, "p1a")
            with ExitStack() as es:
                grow = es.enter_context(nc.sbuf_tensor("p1_grow", [1, QR], F32))
                grow_bf = es.enter_context(nc.sbuf_tensor("p1_growb", [1, QR], BF16))

                def bcast_gain(vecap, n, dst):
                    pha.do("sync", lambda e: e.dma_start(out=grow[0:1, 0:n],
                                                         in_=vecap[None, :]),
                           wait=pha.w("V"), inc=[("Q", 16)])
                    pha.do("vector", lambda e: e.tensor_copy(grow_bf[0:1, 0:n],
                                                             grow[0:1, 0:n]),
                           wait=pha.w("Q"), inc=[("V", 1)])
                    for c0, w in mm_cols(n):
                        pha.do("tensor", lambda e, c0=c0, w=w: nc.tensor.matmul(
                            psA[:, 0:w], ones_bf[0:1, 0:128], grow_bf[0:1, c0:c0 + w],
                            start=True, stop=True), wait=pha.w("V", "Sc"), inc=[("T", 1)])
                        pha.do("scalar", lambda e, c0=c0, w=w, dst=dst: e.copy(
                            dst[:, c0:c0 + w], psA[:, 0:w]), wait=pha.w("T"),
                            inc=[("Sc", 1)])
                bcast_gain(gqa, QR, gqa_bc)
                bcast_gain(gkva, KVR, gkva_bc)
                pha.emit()
            p1_main(ph, q_cF, k_cF, krF_f32, gqa_bc, gkva_bc)
            ph2 = Ph(k, "p1b")
            with ExitStack() as es:
                c_cs = es.enter_context(nc.sbuf_tensor("p1_cs", [64, 2 * S], F32))
                kr_t1 = es.enter_context(nc.sbuf_tensor("p1_kt1", [64, S], F32))
                kr_t2 = es.enter_context(nc.sbuf_tensor("p1_kt2", [64, S], F32))
                kr_sw = es.enter_context(nc.sbuf_tensor("p1_ksw", [64, S], F32))
                ph2.do("sync", lambda e: e.dma_start(out=c_cs[:, 0:S], in_=c_cos[0:64, :]),
                       inc=[("Q", 16)])
                ph2.do("sync", lambda e: e.dma_start(out=c_cs[:, S:2 * S],
                                                     in_=c_sinm[0:64, :]),
                       inc=[("Q", 16)])

                def kr1(e):
                    e.tensor_copy(kr_sw[0:32, :], krF_f32[32:64, :])
                    e.tensor_copy(kr_sw[32:64, :], krF_f32[0:32, :])
                    return e.tensor_copy(kr_t1[:], krF_f32[:])
                ph2.do("vector", kr1, wait=ph2.w("Q", "Sc", "V"), inc=[("V", 1)])

                def kr2(e):
                    e.tensor_mul(kr_t1[:], kr_t1[:], c_cs[:, 0:S])
                    return e.tensor_mul(kr_t2[:], kr_sw[:], c_cs[:, S:2 * S])
                ph2.do("vector", kr2, wait=ph2.w("V"), inc=[("V", 1)])
                ph2.do("vector", lambda e: e.tensor_add(krF_bf[0:64, :], kr_t1[:],
                                                        kr_t2[:]),
                       wait=ph2.w("V"), inc=[("V", 1)])
                ph2.do("vector", lambda e: e.tensor_copy(krF_bf[64:128, :],
                                                         krF_bf[0:64, :]),
                       wait=ph2.w("V"), inc=[("V", 1)])
                if dev_dump == "p1":
                    dq = nc.dram_tensor("dbg_qcf", [128, 12 * S], BF16,
                                        kind="ExternalOutput").ap()
                    dk = nc.dram_tensor("dbg_kcf", [128, 4 * S], BF16,
                                        kind="ExternalOutput").ap()
                    dr = nc.dram_tensor("dbg_krf", [64, S], BF16,
                                        kind="ExternalOutput").ap()
                    ph2.do("gpsimd", lambda e: e.dma_start(out=dq, in_=q_cF[:]),
                           wait=ph2.w("Sc", "V"), inc=[("G", 16)])
                    ph2.do("gpsimd", lambda e: e.dma_start(out=dk, in_=k_cF[:]),
                           wait=ph2.w("Sc", "V"), inc=[("G", 16)])
                    ph2.do("gpsimd", lambda e: e.dma_start(out=dr, in_=krF_bf[0:64, :]),
                           wait=ph2.w("Sc", "V"), inc=[("G", 16)])
                ph2.emit()

    def p1_main(ph, q_cF, k_cF, krF_f32, gqa_bc, gkva_bc):
        wlf = lambda *t: [x for x in t if x]
        with ExitStack() as es:
            pt_free = {}
            xlh = es.enter_context(nc.sbuf_tensor("p1_xlh", [128, 4 * D], BF16))
            xt_f32 = es.enter_context(nc.sbuf_tensor("p1_xt", [128, 2 * D], F32))
            xln_bf = es.enter_context(nc.sbuf_tensor("p1_xln", [128, 2 * D], BF16))
            wpc_f32 = es.enter_context(nc.sbuf_tensor("p1_wpf", [128, 4 * 384], F32))
            wpc_bf = es.enter_context(nc.sbuf_tensor("p1_wpb", [128, 32 * 384], BF16))
            qstage = es.enter_context(nc.sbuf_tensor("p1_qst", [128, 4 * QR], BF16))
            q_ssq = es.enter_context(nc.sbuf_tensor("p1_qssq", [128, 4], F32))
            sspart = es.enter_context(nc.sbuf_tensor("p1_ssp", [128, 2], F32))
            rms_r = es.enter_context(nc.sbuf_tensor("p1_rr", [128, 8], F32))
            kc_f32 = es.enter_context(nc.sbuf_tensor("p1_kc", [128, 4 * KVR], BF16))
            kc_bf = es.enter_context(nc.sbuf_tensor("p1_kcb", [128, KVR], BF16))
            krtmp = es.enter_context(nc.sbuf_tensor("p1_krt", [128, RD], F32))
            qn_bf = es.enter_context(nc.sbuf_tensor("p1_qnb", [128, QR], BF16))
            stats = [new_stat("p1s%d" % i, es) for i in range(2)]

            ln_prev = [{}, {}]
            dst_free = {}
            mm_tok = {}
            ev_tok = {}
            stat_free = {}
            pieces = [("qa", 0, 384), ("qa", 384, 384), ("qa", 768, 384),
                      ("qa", 1152, 384), ("kv", 0, 288), ("kv", 288, 288)]
            NP = len(pieces)
            ln_done = {}
            kr_done = []
            cv_prev = {}

            def emit_ln(quarter):
                for tl, tt in enumerate(range(4 * quarter, 4 * quarter + 4)):
                    par = tl % 2
                    prev = dict(ln_prev[par])
                    prev["dst"] = dst_free.get(tl)
                    prev["stat"] = stat_free.get(par)
                    prev["xln"] = ev_tok.get(("xln", par))

                    def mk_dst(e, rnd, pt, tl=tl):
                        return e.copy(xlh[:, tl * D + rnd * 1024:
                                          tl * D + (rnd + 1) * 1024], pt[:])
                    r = ln_tile_tok(ph, x0, tt, xt_f32[:, par * D:(par + 1) * D],
                                    stats[par], xln_bf[:, par * D:(par + 1) * D],
                                    mk_dst, pt_free, prev)
                    ln_prev[par] = {"xt": r["sq"]}
                    ev_tok[("xln", par)] = r["xln"]
                    stat_free[par] = r["sq"]
                    ln_done[tl] = r["evlast"]

            emit_ln(0)
            for quarter in range(4):
                for pi, (pname, c0, pw) in enumerate(pieces):
                    wsrc = wqa if pname == "qa" else wkva
                    qname = "Q" if pname == "qa" else "G"
                    prev_pi = pi - 1 if pi > 0 else NP - 1
                    prev_pw = pieces[prev_pi][2]
                    cv = {-4: cv_prev.get(28), -3: cv_prev.get(29),
                          -2: cv_prev.get(30), -1: cv_prev.get(31)}
                    for dc in range(DCH):
                        b0 = (dc % 4) * 384
                        if qname == "Q":
                            eng, qq = (("sync", "Q") if dc % 2 == 0
                                       else ("scalar", "Adma"))
                        else:
                            eng, qq = "gpsimd", "G"
                        tl_ = ph.do(eng,
                                    lambda e, dc=dc, b0=b0, wsrc=wsrc, c0=c0, pw=pw:
                                    e.dma_start(
                                        out=wpc_f32[:, b0:b0 + pw],
                                        in_=wsrc[dc * 128:(dc + 1) * 128,
                                                 c0:c0 + pw]),
                                    wait=wlf(cv.get(dc - 4)), inc=[(qq, 16)])
                        g_req = (dc + 1) * pw + 4 * prev_pw - 1
                        g_req = min(7, max(0, g_req // (4 * prev_pw) - 1))
                        wfree = mm_tok.get((prev_pi, 3))
                        wfree = wfree[g_req] if wfree else None
                        cv[dc] = ph.do("vector",
                                       lambda e, dc=dc, b0=b0, pw=pw: e.tensor_copy(
                                           wpc_bf[:, dc * pw:dc * pw + pw],
                                           wpc_f32[:, b0:b0 + pw]),
                                       wait=wlf(tl_, wfree), inc=[("V", 1)])
                    for tl in range(4):
                        ps = psA if tl % 2 == 0 else psB

                        def mmg(e, tl, pw, g8):
                            pps = psA if tl % 2 == 0 else psB
                            for dc in range(g8 * 4, g8 * 4 + 4):
                                lhs = xlh[:, tl * D + dc * 128:
                                          tl * D + (dc + 1) * 128]
                                ins = nc.tensor.matmul(
                                    pps[:, 0:pw], lhs,
                                    wpc_bf[:, dc * pw:(dc + 1) * pw],
                                    start=(dc == 0), stop=(dc == DCH - 1))
                            return ins
                        subs = []
                        for g8 in range(8):
                            w = wlf(cv[g8 * 4 + 3])
                            if g8 == 0:
                                w += wlf(ln_done.get(tl),
                                         ev_tok.get(("psum", tl % 2)))
                            subs.append(ph.do(
                                "tensor",
                                lambda e, tl=tl, pw=pw, g8=g8: mmg(e, tl, pw, g8),
                                wait=w, inc=[("T", 1)]))
                        tB = subs[-1]
                        mm_tok[(pi, tl)] = subs
                        if tl == 0:
                            cv_prev = cv
                        tt = 4 * quarter + tl
                        if pname == "qa":
                            ppi = c0 // 384
                            tsq = ph.do("scalar", lambda e, ps=ps, pw=pw, tl=tl:
                                        e.activation(
                                            qn_bf[:, 0:pw], ps[:, 0:pw],
                                            AF.Square,
                                            accum_out=sspart[:, tl % 2:tl % 2 + 1]),
                                        wait=wlf(tB, ev_tok.get(("sq", tl % 2)),
                                                 ev_tok.get(("qn",))),
                                        inc=[("Sc", 1)])

                            def qacc(e, tl=tl, ppi=ppi, c0=c0, pw=pw, ps=ps):
                                if ppi == 0:
                                    e.tensor_copy(q_ssq[:, tl:tl + 1],
                                                  sspart[:, tl % 2:tl % 2 + 1])
                                else:
                                    e.tensor_add(q_ssq[:, tl:tl + 1],
                                                 q_ssq[:, tl:tl + 1],
                                                 sspart[:, tl % 2:tl % 2 + 1])
                                return e.tensor_copy(
                                    qstage[:, tl * QR + c0: tl * QR + c0 + pw],
                                    ps[:, 0:pw])
                            tq = ph.do("vector", qacc,
                                       wait=wlf(tsq, ev_tok.get(("qa", tl))),
                                       inc=[("V", 1)])
                            ev_tok[("psum", tl % 2)] = tq
                            ev_tok[("sq", tl % 2)] = tq
                            ev_tok[("qa", tl)] = tq
                        elif c0 == 0:
                            te = ph.do("scalar", lambda e, ps=ps, tl=tl: e.copy(
                                kc_f32[:, tl * KVR:tl * KVR + 288], ps[:, 0:288]),
                                wait=wlf(tB, ev_tok.get(("kc", tl))),
                                inc=[("Sc", 1)])
                            ev_tok[("psum", tl % 2)] = te
                            ev_tok[("kv1", tl)] = te
                        else:
                            def kev0(e, ps=ps, tl=tl):
                                e.copy(kc_f32[:, tl * KVR + 288:(tl + 1) * KVR],
                                       ps[:, 0:KVR - 288])
                                return e.copy(krtmp[:], ps[:, KVR - 288:KVR - 288 + RD])
                            te = ph.do("scalar", kev0,
                                       wait=wlf(tB, ev_tok.get(("kc", tl)),
                                                kr_done[-1] if kr_done else None),
                                       inc=[("Sc", 1)])
                            ev_tok[("psum", tl % 2)] = te
                            ttr = ph.do("tensor", lambda e: nc.tensor.transpose(
                                psC[0:64, 0:128], krtmp[:], id_f32[:]),
                                wait=wlf(te, kr_done[-1] if kr_done else None),
                                inc=[("T", 1)])
                            tkr = ph.do("vector", lambda e, tt=tt: e.tensor_copy(
                                krF_f32[:, tt * 128:(tt + 1) * 128],
                                psC[0:64, 0:128]), wait=[ttr], inc=[("V", 1)])
                            kr_done.append(tkr)
                            tsq = ph.do("scalar", lambda e, tl=tl:
                                        e.activation(
                                            qn_bf[:, 0:KVR],
                                            kc_f32[:, tl * KVR:(tl + 1) * KVR],
                                            AF.Square,
                                            accum_out=sspart[:, tl % 2:tl % 2 + 1]),
                                        wait=wlf(te, ev_tok.get(("kv1", tl)),
                                                 ev_tok.get(("sq", tl % 2)),
                                                 ev_tok.get(("qn",))),
                                        inc=[("Sc", 1)])
                            rr = rms_r[:, 4 + tl:5 + tl]
                            t1 = ph.do("vector", lambda e, rr=rr, tl=tl:
                                       e.tensor_scalar_mul(
                                           rr, sspart[:, tl % 2:tl % 2 + 1],
                                           1.0 / KVR),
                                       wait=[tsq], inc=[("V", 1)])
                            ev_tok[("sq", tl % 2)] = t1
                            t2 = ph.do("scalar", lambda e, rr=rr: e.activation(
                                rr, rr, AF.Sqrt), wait=[t1], inc=[("Sc", 1)])
                            t3 = ph.do("vector", lambda e, rr=rr: e.tensor_scalar(
                                rr, rr, 1.0, RMS_EPS, op0=OP.mult, op1=OP.add),
                                wait=[t2], inc=[("V", 1)])
                            t4 = ph.do("vector", lambda e, rr=rr: e.reciprocal(
                                rr, rr), wait=[t3], inc=[("V", 1)])
                            t5 = ph.do("vector", lambda e, rr=rr, tl=tl:
                                       e.scalar_tensor_tensor(
                                           kc_bf[:], kc_f32[:, tl * KVR:(tl + 1) * KVR],
                                           rr, gkva_bc[:], op0=OP.mult, op1=OP.mult),
                                       wait=wlf(t4, ev_tok.get(("kcb",))),
                                       inc=[("V", 1)])
                            ev_tok[("kc", tl)] = t5
                            ptk = tl % 2
                            pt = ptA if ptk == 0 else ptB

                            def ktr(e, pt=pt):
                                for j in range(4):
                                    ins = nc.tensor.transpose(
                                        pt[:, j * 128:(j + 1) * 128],
                                        kc_bf[:, j * 128:(j + 1) * 128], id_bf[:])
                                return ins
                            trl = ph.do("tensor", ktr,
                                        wait=wlf(t5, pt_free.get(ptk)),
                                        inc=[("T", 1)])
                            ev_tok[("kcb",)] = trl

                            def kdst(e, pt=pt, tt=tt):
                                ins = None
                                for j in range(4):
                                    ins = e.copy(
                                        k_cF[:, j * S + tt * 128:
                                             j * S + (tt + 1) * 128],
                                        pt[:, j * 128:(j + 1) * 128])
                                return ins
                            evl = ph.do("scalar", kdst, wait=[trl], inc=[("Sc", 1)])
                            pt_free[ptk] = evl
                    if pname == "qa" and c0 == 1152:
                        prev_trl = ev_tok.get(("qn",))
                        for tl in range(4):
                            rr = rms_r[:, tl:tl + 1]
                            tq0 = ev_tok[("qa", tl)]
                            t1 = ph.do("vector", lambda e, rr=rr, tl=tl:
                                       e.tensor_scalar_mul(rr, q_ssq[:, tl:tl + 1],
                                                           1.0 / QR),
                                       wait=[tq0], inc=[("V", 1)])
                            t2 = ph.do("scalar", lambda e, rr=rr: e.activation(
                                rr, rr, AF.Sqrt), wait=[t1], inc=[("Sc", 1)])
                            t3 = ph.do("vector", lambda e, rr=rr: e.tensor_scalar(
                                rr, rr, 1.0, RMS_EPS, op0=OP.mult, op1=OP.add),
                                wait=[t2], inc=[("V", 1)])
                            t4 = ph.do("vector", lambda e, rr=rr: e.reciprocal(rr, rr),
                                       wait=[t3], inc=[("V", 1)])
                            t5 = ph.do("vector", lambda e, rr=rr: e.tensor_scalar_mul(
                                rr, rr, QK_SCALE), wait=[t4], inc=[("V", 1)])
                            t6 = ph.do("vector", lambda e, rr=rr, tl=tl:
                                       e.scalar_tensor_tensor(
                                           qn_bf[:], qstage[:, tl * QR:(tl + 1) * QR],
                                           rr, gqa_bc[:], op0=OP.mult, op1=OP.mult),
                                       wait=wlf(t5, prev_trl,
                                                ev_tok.get(("sq", 0)),
                                                ev_tok.get(("sq", 1))),
                                       inc=[("V", 1)])
                            tt = 4 * quarter + tl
                            trl = None
                            for rnd in range(2):
                                ptk = rnd % 2
                                pt = ptA if ptk == 0 else ptB

                                def qtr(e, rnd=rnd, pt=pt):
                                    for j in range(6):
                                        c = rnd * 6 + j
                                        ins = nc.tensor.transpose(
                                            pt[:, j * 128:(j + 1) * 128],
                                            qn_bf[:, c * 128:(c + 1) * 128],
                                            id_bf[:])
                                    return ins
                                trl = ph.do("tensor", qtr,
                                            wait=wlf(t6, pt_free.get(ptk)),
                                            inc=[("T", 1)])

                                def qdst(e, rnd=rnd, pt=pt, tt=tt):
                                    ins = None
                                    for j in range(6):
                                        c = rnd * 6 + j
                                        ins = e.copy(
                                            q_cF[:, c * S + tt * 128:
                                                 c * S + (tt + 1) * 128],
                                            pt[:, j * 128:(j + 1) * 128])
                                    return ins
                                evl = ph.do("scalar", qdst, wait=[trl],
                                            inc=[("Sc", 1)])
                                pt_free[ptk] = evl
                            prev_trl = trl
                            ev_tok[("qn",)] = trl
                    if pname == "kv" and c0 == 288:
                        for tl in range(4):
                            dst_free[tl] = mm_tok[(pi, tl)][-1]
                        ln_done = {}
                        if quarter < 3:
                            emit_ln(quarter + 1)
            ph.emit()

    def emit_attention(ph, h, q_chunks, k_chunks, v_tok, attn_bf2, sc_sb2,
                       attnT, ctx_pc, sm, tk, tile_pos=None, strip=None,
                       prep=None, rcp_all=None):
        t_evm = {}; t_comb = {}; t_exp = {}; t_dd = {}; t_recip = {}
        t_tn = {}; t_atr = {}; t_adst = {}; t_avmm = {}; t_cev = {}
        vt_tok = tk.get("vt")
        qk_list = list(tk.get("qk", []))

        def wl(*toks):
            return [t for t in toks if t]

        def stage_A(qt):
            par = qt % 2
            sc_buf = sc_sb2[:, par * 2048:(par + 1) * 2048]
            t_exp.setdefault(qt, {})
            for kh in range(2):
                ps = psA if kh == 0 else psB

                def smm(e, qt=qt, kh=kh, ps=ps):
                    nchunks = len(q_chunks)
                    for ci, ((qsb, qsel), (ksb, ksel)) in enumerate(
                            zip(q_chunks, k_chunks)):
                        if qsel == "strip":
                            lhs = qsb[strip * 32:(strip + 1) * 32,
                                      qt * 128:(qt + 1) * 128]
                        elif qsel is None:
                            lhs = qsb[:, qt * 128:(qt + 1) * 128]
                        else:
                            lhs = qsb[qsel * 64:(qsel + 1) * 64,
                                      qt * 128:(qt + 1) * 128]
                        for j in range(2):
                            c0 = kh * 1024 + j * 512
                            if ksel == "strip":
                                rhs = ksb[strip * 32:(strip + 1) * 32, c0:c0 + 512]
                            elif ksel is None:
                                rhs = ksb[:, c0:c0 + 512]
                            else:
                                rhs = ksb[ksel * 64:(ksel + 1) * 64, c0:c0 + 512]
                            ins = nc.tensor.matmul(
                                ps[:, j * 512:(j + 1) * 512], lhs, rhs,
                                start=(ci == 0), stop=(ci == nchunks - 1),
                                tile_position=tile_pos)
                    return ins
                pk = "evm%d" % kh
                prev = tk.get(pk) if qt == 0 else t_evm[qt - 1][kh]
                w = wl(prev) + (qk_list if qt == 0 else [])
                tT = ph.do("tensor", smm, wait=w, inc=[("T", 1)])

                def evm(e, ps=ps, kh=kh, sc_buf=sc_buf, par=par):
                    return e.tensor_scalar(
                        sc_buf[:, kh * 1024:(kh + 1) * 1024], ps[:], 0.0, None,
                        op0=OP.add, op1=OP.max,
                        accum_out=sm["mx"][:, 2 * par + kh:2 * par + kh + 1])
                scfree = (t_exp[qt - 2][1] if qt >= 2
                          else tk.get("exp_p%d" % par))
                t_evm.setdefault(qt, {})[kh] = ph.do(
                    "vector", evm, wait=wl(tT, scfree), inc=[("V", 1)])
                tk["evm%d" % kh] = t_evm[qt][kh]

            def comb(e, par=par):
                return e.tensor_scalar(
                    sm["nmx"][:, par:par + 1], sm["mx"][:, 2 * par:2 * par + 1],
                    sm["mx"][:, 2 * par + 1:2 * par + 2], -1.0,
                    op0=OP.max, op1=OP.mult)
            scfree = t_exp[qt - 2][1] if qt >= 2 else tk.get("exp_p%d" % par)
            t_comb[qt] = ph.do("vector", comb,
                               wait=wl(t_evm[qt][0], t_evm[qt][1], scfree),
                               inc=[("V", 1)])

        def stage_B(qt):
            par = qt % 2
            sc_buf = sc_sb2[:, par * 2048:(par + 1) * 2048]
            ab = attn_bf2[:, par * 2048:(par + 1) * 2048]
            abfree = t_atr[qt - 2][1] if qt >= 2 else tk.get("atr_p%d" % par)
            ddfree = t_recip[qt - 2] if qt >= 2 else tk.get("dd_p%d" % par)
            for kh in range(2):
                def expo(e, kh=kh, sc_buf=sc_buf, ab=ab, par=par):
                    return e.activation(
                        ab[:, kh * 1024:(kh + 1) * 1024],
                        sc_buf[:, kh * 1024:(kh + 1) * 1024], AF.Exp,
                        bias=sm["nmx"][:, par:par + 1],
                        accum_out=sm["dd"][:, 2 * par + kh:2 * par + kh + 1])
                t_exp[qt][kh] = ph.do("scalar", expo,
                                      wait=wl(t_comb[qt], abfree, ddfree),
                                      inc=[("Sc", 1)])
            t_dd[qt] = ph.do("vector", lambda e, par=par: e.tensor_tensor(
                sm["dd"][:, 2 * par:2 * par + 1], sm["dd"][:, 2 * par:2 * par + 1],
                sm["dd"][:, 2 * par + 1:2 * par + 2], op=OP.add),
                wait=wl(t_exp[qt][0], t_exp[qt][1]), inc=[("V", 1)])
            rdst = (sm["rcp"][:, par:par + 1] if rcp_all is None
                    else rcp_all[:, h * 16 + qt:h * 16 + qt + 1])
            t_recip[qt] = ph.do("vector", lambda e, rdst=rdst, par=par:
                                e.reciprocal(rdst, sm["dd"][:, 2 * par:2 * par + 1]),
                                wait=wl(t_dd[qt]), inc=[("V", 1)])
            if rcp_all is None:
                t_tn[qt] = ph.do("vector", lambda e, ab=ab, par=par:
                                 e.tensor_scalar_mul(ab[:], ab[:],
                                                     sm["rcp"][:, par:par + 1]),
                                 wait=wl(t_recip[qt]), inc=[("V", 1)])

        def stage_C(qt):
            par = qt % 2
            ab = attn_bf2[:, par * 2048:(par + 1) * 2048]
            t_atr.setdefault(qt, {})
            t_adst.setdefault(qt, {})
            for rnd in range(2):
                pt = ptA if rnd == 0 else ptB

                def atr(e, rnd=rnd, pt=pt, ab=ab):
                    for j in range(8):
                        c = rnd * 8 + j
                        ins = nc.tensor.transpose(pt[:, j * 128:(j + 1) * 128],
                                                  ab[:, c * 128:(c + 1) * 128],
                                                  id_bf[:])
                    return ins
                ptfree = t_adst[qt - 1][rnd] if qt >= 1 else tk.get("adst_r%d" % rnd)
                steal = tk.pop("pt_steal_%d" % rnd, None)
                src_ok = t_tn[qt] if rcp_all is None else t_exp[qt][rnd]
                t_atr[qt][rnd] = ph.do(
                    "tensor", atr,
                    wait=wl(src_ok, ptfree, steal), inc=[("T", 1)])

                aeng = "scalar" if (rnd == 0 or rcp_all is None) else "vector"
                asem = "Sc" if aeng == "scalar" else "V"

                def adst(e, rnd=rnd, pt=pt, qt=qt, aeng=aeng):
                    ql = qt % 4
                    dst = attnT.rearrange("p (t c) -> p t c", t=16)[
                        :, rnd * 8:(rnd + 1) * 8, ql * 128:(ql + 1) * 128]
                    srct = pt.rearrange("p (t c) -> p t c", t=8)
                    return (e.copy(dst, srct) if aeng == "scalar"
                            else e.tensor_copy(dst, srct))
                atfree = t_avmm[qt // 4 - 1] if qt >= 4 else tk.get("attnT")
                t_adst[qt][rnd] = ph.do(
                    aeng, adst, wait=wl(t_atr[qt][rnd], atfree),
                    inc=[(asem, 1)])
                tk["adst_r%d" % rnd] = t_adst[qt][rnd]

        def stage_D(qg):
            sl = psC[:, 0:512]

            def avmm(e, sl=sl):
                for kt in range(16):
                    ins = nc.tensor.matmul(
                        sl, v_tok[:, kt * 128:(kt + 1) * 128],
                        attnT[:, kt * 512:(kt + 1) * 512],
                        start=(kt == 0), stop=(kt == 15))
                return ins
            pcfree = t_cev[qg - 1] if qg >= 1 else tk.get("avc")
            t_avmm[qg] = ph.do(
                "tensor", avmm,
                wait=wl(t_adst[qg * 4 + 3][1], vt_tok, pcfree),
                inc=[("T", 1)])

            def cev(e, sl=sl):
                return e.copy(ctx_pc[:], sl)
            cxfree = tk.get("ctx")
            t_cev[qg] = ph.do("scalar", cev, wait=wl(t_avmm[qg], cxfree),
                              inc=[("Sc", 1)])
            tk["avc"] = t_cev[qg]
            dst = tls[qg * 4:(qg + 1) * 4, :, h * 128:(h + 1) * 128].rearrange(
                "t p j -> p t j")
            srcx = ctx_pc.rearrange("p (t j) -> p t j", t=4)
            tk["ctx"] = ph.do("gpsimd", lambda e, dst=dst, srcx=srcx:
                              e.dma_start(out=dst, in_=srcx),
                              wait=[t_cev[qg]], inc=[("G", 16)])

        for i in range(20):
            if i < 16:
                stage_A(i)
            if 4 <= i and (i - 4) % 4 == 3:
                stage_D((i - 4) // 4)
            if 3 <= i < 19:
                stage_C(i - 3)
            if 1 <= i < 17:
                stage_B(i - 1)
            if prep is not None and i >= 1:
                next(prep, None)
        tk["evm0"] = t_evm[15][0]
        tk["evm1"] = t_evm[15][1]
        tk["psA"] = t_evm[15][0]
        tk["psB"] = t_evm[15][1]
        tk["exp_p0"] = t_exp[14][1]
        tk["exp_p1"] = t_exp[15][1]
        tk["atr_p0"] = t_atr[14][1]
        tk["atr_p1"] = t_atr[15][1]
        tk["adst_r0"] = t_adst[15][0]
        tk["adst_r1"] = t_adst[15][1]
        tk["dd_p0"] = t_recip[14]
        tk["dd_p1"] = t_recip[15]
        tk["attnT"] = t_avmm[3]

    def emit_attention_stream(ph, nheads, head_args, attn_bf2, sc_sb2,
                              attnT, ctx_pc, sm, tk, preps, vdw=128,
                              z_sink=None, rcp_all=None):
        wl = lambda *t: [x for x in t if x]
        t_evm = {}; t_comb = {}; t_exp = {}; t_dd = {}; t_recip = {}
        t_tn = {}; t_atr = {}; t_adst = {}; t_avmm = {}; t_cev = {}
        head_state = {}
        NQT = nheads * 16

        def stage_A(qt):
            h = qt // 16
            if qt % 16 == 0:
                head_state[h] = {"qk": list(tk.get("qk", [])),
                                 "vt": tk.get("vt")}
            q_chunks, k_chunks, v_tok, tile_pos, strip = head_args(h)
            par = qt % 2
            ql = qt % 16
            sc_buf = sc_sb2[:, par * 2048:(par + 1) * 2048]
            t_exp.setdefault(qt, {})
            for kh in range(2):
                ps = psA if kh == 0 else psB

                def smm(e, ql=ql, kh=kh, ps=ps, q_chunks=q_chunks,
                        k_chunks=k_chunks, tile_pos=tile_pos, strip=strip):
                    nchunks = len(q_chunks)
                    for ci, ((qsb, qsel), (ksb, ksel)) in enumerate(
                            zip(q_chunks, k_chunks)):
                        if qsel == "strip":
                            lhs = qsb[strip * 32:(strip + 1) * 32,
                                      ql * 128:(ql + 1) * 128]
                        elif qsel is None:
                            lhs = qsb[:, ql * 128:(ql + 1) * 128]
                        else:
                            lhs = qsb[qsel * 64:(qsel + 1) * 64,
                                      ql * 128:(ql + 1) * 128]
                        for j in range(2):
                            c0 = kh * 1024 + j * 512
                            if ksel == "strip":
                                rhs = ksb[strip * 32:(strip + 1) * 32, c0:c0 + 512]
                            elif ksel is None:
                                rhs = ksb[:, c0:c0 + 512]
                            else:
                                rhs = ksb[ksel * 64:(ksel + 1) * 64, c0:c0 + 512]
                            ins = nc.tensor.matmul(
                                ps[:, j * 512:(j + 1) * 512], lhs, rhs,
                                start=(ci == 0), stop=(ci == nchunks - 1),
                                tile_position=tile_pos)
                    return ins
                prev = t_evm[qt - 1][kh] if qt >= 1 else None
                w = wl(prev) + (head_state[h]["qk"] if ql == 0 else [])
                tT = ph.do("tensor", smm, wait=w, inc=[("T", 1)])

                def evm(e, ps=ps, kh=kh, sc_buf=sc_buf, par=par):
                    return e.tensor_scalar(
                        sc_buf[:, kh * 1024:(kh + 1) * 1024], ps[:], 0.0, None,
                        op0=OP.add, op1=OP.max,
                        accum_out=sm["mx"][:, 2 * par + kh:2 * par + kh + 1])
                scfree = t_exp[qt - 2][1] if qt >= 2 else None
                t_evm.setdefault(qt, {})[kh] = ph.do(
                    "vector", evm, wait=wl(tT, scfree), inc=[("V", 1)])
                tk["evm%d" % kh] = t_evm[qt][kh]

            def comb(e, par=par):
                return e.tensor_scalar(
                    sm["nmx"][:, par:par + 1], sm["mx"][:, 2 * par:2 * par + 1],
                    sm["mx"][:, 2 * par + 1:2 * par + 2], -1.0,
                    op0=OP.max, op1=OP.mult)
            scfree = t_exp[qt - 2][1] if qt >= 2 else None
            t_comb[qt] = ph.do("vector", comb,
                               wait=wl(t_evm[qt][0], t_evm[qt][1], scfree),
                               inc=[("V", 1)])

        def stage_B(qt):
            par = qt % 2
            sc_buf = sc_sb2[:, par * 2048:(par + 1) * 2048]
            ab = attn_bf2[:, par * 2048:(par + 1) * 2048]
            abfree = t_atr[qt - 2][1] if qt >= 2 else None
            ddfree = t_recip[qt - 2] if qt >= 2 else None
            for kh in range(2):
                def expo(e, kh=kh, sc_buf=sc_buf, ab=ab, par=par):
                    return e.activation(
                        ab[:, kh * 1024:(kh + 1) * 1024],
                        sc_buf[:, kh * 1024:(kh + 1) * 1024], AF.Exp,
                        bias=sm["nmx"][:, par:par + 1],
                        accum_out=sm["dd"][:, 2 * par + kh:2 * par + kh + 1])
                t_exp[qt][kh] = ph.do("scalar", expo,
                                      wait=wl(t_comb[qt], abfree, ddfree),
                                      inc=[("Sc", 1)])
            t_dd[qt] = ph.do("vector", lambda e, par=par: e.tensor_tensor(
                sm["dd"][:, 2 * par:2 * par + 1], sm["dd"][:, 2 * par:2 * par + 1],
                sm["dd"][:, 2 * par + 1:2 * par + 2], op=OP.add),
                wait=wl(t_exp[qt][0], t_exp[qt][1]), inc=[("V", 1)])
            rdst = (sm["rcp"][:, par:par + 1] if rcp_all is None
                    else rcp_all[:, qt:qt + 1])
            t_recip[qt] = ph.do("vector", lambda e, rdst=rdst, par=par:
                                e.reciprocal(rdst, sm["dd"][:, 2 * par:2 * par + 1]),
                                wait=wl(t_dd[qt]), inc=[("V", 1)])
            if rcp_all is None:
                t_tn[qt] = ph.do("vector", lambda e, ab=ab, par=par:
                                 e.tensor_scalar_mul(ab[:], ab[:],
                                                     sm["rcp"][:, par:par + 1]),
                                 wait=wl(t_recip[qt]), inc=[("V", 1)])

        def stage_C(qt):
            par = qt % 2
            ab = attn_bf2[:, par * 2048:(par + 1) * 2048]
            t_atr.setdefault(qt, {})
            t_adst.setdefault(qt, {})
            for rnd in range(2):
                pt = ptA if rnd == 0 else ptB

                def atr(e, rnd=rnd, pt=pt, ab=ab):
                    for j in range(8):
                        c = rnd * 8 + j
                        ins = nc.tensor.transpose(pt[:, j * 128:(j + 1) * 128],
                                                  ab[:, c * 128:(c + 1) * 128],
                                                  id_bf[:])
                    return ins
                ptfree = t_adst[qt - 1][rnd] if qt >= 1 else None
                steal = tk.pop("pt_steal_%d" % rnd, None)
                src_ok = t_tn[qt] if rcp_all is None else t_exp[qt][rnd]
                t_atr[qt][rnd] = ph.do(
                    "tensor", atr,
                    wait=wl(src_ok, ptfree, steal), inc=[("T", 1)])

                aeng = "scalar" if (rnd == 0 or rcp_all is None) else "vector"
                asem = "Sc" if aeng == "scalar" else "V"

                def adst(e, rnd=rnd, pt=pt, qt=qt, aeng=aeng):
                    ql = qt % 4
                    dst = attnT.rearrange("p (t c) -> p t c", t=16)[
                        :, rnd * 8:(rnd + 1) * 8, ql * 128:(ql + 1) * 128]
                    srct = pt.rearrange("p (t c) -> p t c", t=8)
                    return (e.copy(dst, srct) if aeng == "scalar"
                            else e.tensor_copy(dst, srct))
                atfree = t_avmm[qt // 4 - 1] if qt >= 4 else None
                t_adst[qt][rnd] = ph.do(
                    aeng, adst, wait=wl(t_atr[qt][rnd], atfree),
                    inc=[(asem, 1)])
                tk["adst_r%d" % rnd] = t_adst[qt][rnd]

        def stage_D(qg):
            h = qg // 4
            v_tok = head_args(h)[2]
            sl = psC[:, 0:512] if vdw == 128 else psC[0:vdw, 0:512]

            def avmm(e, sl=sl, v_tok=v_tok):
                for kt in range(16):
                    ins = nc.tensor.matmul(
                        sl, v_tok[:, kt * vdw:(kt + 1) * vdw],
                        attnT[:, kt * 512:(kt + 1) * 512],
                        start=(kt == 0), stop=(kt == 15))
                return ins
            pcfree = t_cev[qg - 1] if qg >= 1 else None
            t_avmm[qg] = ph.do(
                "tensor", avmm,
                wait=wl(t_adst[qg * 4 + 3][1], head_state[h]["vt"], pcfree),
                inc=[("T", 1)])
            tk["attnT"] = t_avmm[qg]
            gg = qg % 4
            if z_sink is not None:
                t_cev[qg] = z_sink(qg, h, gg, t_avmm[qg])
                return

            def cev(e, sl=sl):
                return e.copy(ctx_pc[:], sl)
            cxfree = tk.get("ctx")
            t_cev[qg] = ph.do("scalar", cev, wait=wl(t_avmm[qg], cxfree),
                              inc=[("Sc", 1)])
            tk["avc"] = t_cev[qg]
            dst = tls[gg * 4:(gg + 1) * 4, :, h * 128:(h + 1) * 128].rearrange(
                "t p j -> p t j")
            srcx = ctx_pc.rearrange("p (t j) -> p t j", t=4)
            tk["ctx"] = ph.do("gpsimd", lambda e, dst=dst, srcx=srcx:
                              e.dma_start(out=dst, in_=srcx),
                              wait=[t_cev[qg]], inc=[("G", 16)])

        for i in range(NQT + 4):
            if i < NQT:
                stage_A(i)
            if 4 <= i and (i - 4) % 4 == 3:
                stage_D((i - 4) // 4)
            if 3 <= i < NQT + 3:
                stage_C(i - 3)
            if 1 <= i < NQT + 1:
                stage_B(i - 1)
            hnext = i // 16 + 1
            if hnext < len(preps) and preps[hnext] is not None:
                next(preps[hnext], None)

    def p2(q_cF, k_cF, krF_bf):
        ph = Ph(k, "p2")
        wlf = lambda *t: [x for x in t if x]
        with ExitStack() as es:
            c_cs2 = es.enter_context(nc.sbuf_tensor("p2_cs", [128, 2 * S], F32))
            wqb_f32 = es.enter_context(nc.sbuf_tensor("p2_wqf", [128, 6 * 384], F32))
            wqb_bf = es.enter_context(nc.sbuf_tensor("p2_wqb", [128, 12 * 384], BF16))
            wqbr_bf = es.enter_context(nc.sbuf_tensor("p2_wqr", [128, 12 * 128], BF16))
            wkvb_f32 = es.enter_context(nc.sbuf_tensor("p2_wkf", [128, 4 * 256], F32))
            wkvb_bf = es.enter_context(nc.sbuf_tensor("p2_wkb", [128, 4 * 256], BF16))
            qn2 = es.enter_context(nc.sbuf_tensor("p2_qn", [128, 2 * S], BF16))
            kn2 = es.enter_context(nc.sbuf_tensor("p2_kn", [128, 2 * S], BF16))
            vF2 = es.enter_context(nc.sbuf_tensor("p2_vF", [128, 2 * S], BF16))
            vt2 = es.enter_context(nc.sbuf_tensor("p2_vt", [128, 2 * S], BF16))
            qr2 = es.enter_context(nc.sbuf_tensor("p2_qr", [128, 2 * S], BF16))
            rp_raw = es.enter_context(nc.sbuf_tensor("p2_rraw", [128, 1024], F32))
            rp_sw = es.enter_context(nc.sbuf_tensor("p2_rsw", [128, 1024], F32))
            attn_bf2 = es.enter_context(nc.sbuf_tensor("p2_at2", [128, 2 * S], BF16))
            sc_sb2 = es.enter_context(nc.sbuf_tensor("p2_scb", [128, 2 * S], F32))
            rstg2 = es.enter_context(nc.sbuf_tensor("p2_rst", [16, 256], BF16))
            rcp_all_a = es.enter_context(nc.sbuf_tensor("p2_rca", [128, 512], F32))
            attnT = es.enter_context(nc.sbuf_tensor("p2_aT", [128, 16 * 512], BF16))
            ctx_pc = es.enter_context(nc.sbuf_tensor("p2_cx", [128, 512], BF16))
            sm = {"mx": es.enter_context(nc.sbuf_tensor("p2_mx", [128, 4], F32)),
                  "nmx": es.enter_context(nc.sbuf_tensor("p2_nmx", [128, 2], F32)),
                  "dd": es.enter_context(nc.sbuf_tensor("p2_dd", [128, 4], F32)),
                  "rcp": es.enter_context(nc.sbuf_tensor("p2_rcp", [128, 2], F32))}
            psP = psC[:, 512:1024]
            ph.do("sync", lambda e: e.dma_start(out=c_cs2[:, 0:S], in_=c_cos),
                  inc=[("Q", 16)])
            ph.do("sync", lambda e: e.dma_start(out=c_cs2[:, S:2 * S], in_=c_sinm),
                  inc=[("Q", 16)])
            tk = {}

            def mk_prep(h):
                hh = h % 2
                nb = (h % 2) * S
                gb = ((h // 2) % 2) * S

                def bguard():
                    return wlf(tk.get("evm1"))

                def gen():
                    qk = []
                    if hh == 0:
                        pair = h // 2
                        srcw = wqb[:, pair * 384:(pair + 1) * 384].rearrange(
                            "(dc p) c -> p dc c", p=128)
                        dstw = wqb_f32.rearrange("p (dc c) -> p dc c", dc=6)
                        tconv = None
                        for hf in range(2):
                            tdma = ph.do("sync", lambda e, srcw=srcw, dstw=dstw,
                                         hf=hf: e.dma_start(
                                             out=dstw,
                                             in_=srcw[:, hf * 6:(hf + 1) * 6, :]),
                                         wait=wlf(tconv if hf else
                                                  tk.get("wqf_rd")),
                                         inc=[("Q", 16)])
                            yield
                            tconv = ph.do("vector", lambda e, hf=hf:
                                          e.tensor_copy(
                                              wqb_bf[:, hf * 2304:(hf + 1) * 2304],
                                              wqb_f32[:]),
                                          wait=[tdma] + wlf(tk.get("wqb_rd")
                                                            if hf == 0 else None),
                                          inc=[("V", 1)])
                        tk["wqf_rd"] = tconv
                        tk["wqbconv"] = tconv
                        yield

                        def rstage(e):
                            for dc in range(12):
                                e.tensor_copy(wqbr_bf[:, dc * 128:dc * 128 + 64],
                                              wqb_bf[:, dc * 384 + 128:
                                                     dc * 384 + 192])
                                ins = e.tensor_copy(
                                    wqbr_bf[:, dc * 128 + 64:dc * 128 + 128],
                                    wqb_bf[:, dc * 384 + 320:dc * 384 + 384])
                            return ins
                        trs = ph.do("vector", rstage,
                                    wait=[tconv] + wlf(tk.get("wqr_rd")),
                                    inc=[("V", 1)])
                        tk["rstage"] = trs
                        yield
                    srck = wkvb[:, h * 256:(h + 1) * 256].rearrange(
                        "(dc p) c -> p dc c", p=128)
                    dstk = wkvb_f32.rearrange("p (dc c) -> p dc c", dc=4)
                    tkv = ph.do("scalar", lambda e, srck=srck, dstk=dstk:
                                e.dma_start(out=dstk, in_=srck),
                                wait=wlf(tk.get("wkf_rd")), inc=[("Adma", 16)])
                    tkc = ph.do("vector", lambda e: e.tensor_copy(wkvb_bf[:],
                                                                  wkvb_f32[:]),
                                wait=[tkv] + wlf(tk.get("wkb_rd")), inc=[("V", 1)])
                    tk["wkf_rd"] = tkc
                    yield
                    if hh == 0:
                        for qh in range(2):
                            evs = []
                            for j in range(2):
                                def qrmm(e, qh=qh, j=j):
                                    for dc in range(12):
                                        ins = nc.tensor.matmul(
                                            psP,
                                            wqbr_bf[:, dc * 128:(dc + 1) * 128],
                                            q_cF[:, dc * S + qh * 1024 + j * 512:
                                                 dc * S + qh * 1024 +
                                                 (j + 1) * 512],
                                            start=(dc == 0), stop=(dc == 11))
                                    return ins
                                tq = ph.do("tensor", qrmm,
                                           wait=[tk["rstage"]] +
                                           wlf(tk.get("psP")), inc=[("T", 1)])
                                tk["wqr_rd"] = tq
                                tev = ph.do("vector", lambda e, j=j:
                                            e.tensor_copy(
                                                rp_raw[:, j * 512:(j + 1) * 512],
                                                psP),
                                            wait=[tq] + wlf(tk.get("rp_rd")
                                                            if j == 0 else None),
                                            inc=[("V", 1)])
                                tk["psP"] = tev
                                evs.append(tev)
                                yield

                            def swp(e):
                                for g, sw_ in ((0, 32), (32, 0), (64, 96),
                                               (96, 64)):
                                    ins = e.tensor_copy(rp_sw[g:g + 32, :],
                                                        rp_raw[sw_:sw_ + 32, :])
                                return ins
                            tsw = ph.do("vector", swp, wait=evs, inc=[("V", 1)])
                            tm1 = ph.do("vector", lambda e, qh=qh: e.tensor_mul(
                                rp_raw[:], rp_raw[:],
                                c_cs2[:, qh * 1024:(qh + 1) * 1024]),
                                wait=[tsw], inc=[("V", 1)])
                            tm2 = ph.do("vector", lambda e, qh=qh: e.tensor_mul(
                                rp_sw[:], rp_sw[:],
                                c_cs2[:, S + qh * 1024: S + (qh + 1) * 1024]),
                                wait=[tsw], inc=[("V", 1)])
                            tro = ph.do("vector", lambda e, qh=qh, gb=gb:
                                        e.tensor_add(
                                            qr2[:, gb + qh * 1024:
                                                gb + (qh + 1) * 1024],
                                            rp_raw[:], rp_sw[:]),
                                        wait=[tm1, tm2] + bguard(),
                                        inc=[("V", 1)])
                            tk["rp_rd"] = tro
                            tk["qr_ready"] = tro
                            yield
                    qk.append(tk["qr_ready"])
                    last_ev = None
                    for qh in range(2):
                        for j in range(2):
                            def qnmm(e, qh=qh, j=j, hh=hh):
                                for dc in range(12):
                                    ins = nc.tensor.matmul(
                                        psP,
                                        wqb_bf[:, dc * 384 + hh * 192:
                                               dc * 384 + hh * 192 + 128],
                                        q_cF[:, dc * S + qh * 1024 + j * 512:
                                             dc * S + qh * 1024 + (j + 1) * 512],
                                        start=(dc == 0), stop=(dc == 11))
                                return ins
                            tqn = ph.do("tensor", qnmm,
                                        wait=[tk["wqbconv"]] + wlf(tk.get("psP")),
                                        inc=[("T", 1)])
                            tk["wqb_rd"] = tqn
                            last_ev = ph.do("scalar", lambda e, qh=qh, j=j, nb=nb:
                                            e.copy(qn2[:, nb + qh * 1024 + j * 512:
                                                       nb + qh * 1024 +
                                                       (j + 1) * 512], psP),
                                            wait=[tqn] + bguard(), inc=[("Sc", 1)])
                            tk["psP"] = last_ev
                            yield
                    for qh in range(2):
                        for j in range(2):
                            def knmm(e, qh=qh, j=j):
                                for dc in range(4):
                                    ins = nc.tensor.matmul(
                                        psP,
                                        wkvb_bf[:, dc * 256:dc * 256 + 128],
                                        k_cF[:, dc * S + qh * 1024 + j * 512:
                                             dc * S + qh * 1024 + (j + 1) * 512],
                                        start=(dc == 0), stop=(dc == 3))
                                return ins
                            tkn = ph.do("tensor", knmm,
                                        wait=[tkc] + wlf(tk.get("psP")),
                                        inc=[("T", 1)])
                            tk["wkb_rd"] = tkn
                            last_ev = ph.do("scalar", lambda e, qh=qh, j=j, nb=nb:
                                            e.copy(kn2[:, nb + qh * 1024 + j * 512:
                                                       nb + qh * 1024 +
                                                       (j + 1) * 512], psP),
                                            wait=[tkn] + bguard(), inc=[("Sc", 1)])
                            tk["psP"] = last_ev
                            yield
                    qk.append(last_ev)
                    vevs = []
                    for qh in range(2):
                        for j in range(2):
                            def vmm(e, qh=qh, j=j):
                                for dc in range(4):
                                    ins = nc.tensor.matmul(
                                        psP,
                                        wkvb_bf[:, dc * 256 + 128:dc * 256 + 256],
                                        k_cF[:, dc * S + qh * 1024 + j * 512:
                                             dc * S + qh * 1024 + (j + 1) * 512],
                                        start=(dc == 0), stop=(dc == 3))
                                return ins
                            tvm = ph.do("tensor", vmm,
                                        wait=[tkc] + wlf(tk.get("psP")),
                                        inc=[("T", 1)])
                            tk["wkb_rd"] = tvm
                            tve = ph.do("vector", lambda e, qh=qh, j=j, nb=nb:
                                        e.tensor_copy(
                                            vF2[:, nb + qh * 1024 + j * 512:
                                                nb + qh * 1024 + (j + 1) * 512],
                                            psP),
                                        wait=[tvm] + bguard(), inc=[("V", 1)])
                            tk["psP"] = tve
                            vevs.append(tve)
                            yield
                    last_vt = None
                    for rnd in range(2):
                        pt = ptA if rnd == 0 else ptB

                        def vtr(e, rnd=rnd, pt=pt, nb=nb):
                            for jj in range(8):
                                c = rnd * 8 + jj
                                ins = nc.tensor.transpose(
                                    pt[:, jj * 128:(jj + 1) * 128],
                                    vF2[:, nb + c * 128: nb + (c + 1) * 128],
                                    id_bf[:])
                            return ins
                        ttr = ph.do("tensor", vtr,
                                    wait=vevs + wlf(tk.get("adst_r%d" % rnd)),
                                    inc=[("T", 1)])
                        tvev = ph.do("scalar", lambda e, rnd=rnd, pt=pt, nb=nb:
                                     e.copy(vt2[:, nb + rnd * 1024:
                                                nb + (rnd + 1) * 1024], pt[:]),
                                     wait=[ttr] + wlf(tk.get("attnT")),
                                     inc=[("Sc", 1)])
                        tk["pt_steal_%d" % rnd] = tvev
                        last_vt = tvev
                        yield
                    tk["vt"] = last_vt
                    tk["qk"] = qk
                return gen()

            g0 = mk_prep(0)
            for _ in g0:
                pass
            for h in range(NH):
                nxt = mk_prep(h + 1) if h + 1 < NH else None
                hh = h % 2
                nb = (h % 2) * S
                gb = ((h // 2) % 2) * S
                emit_attention(ph, h,
                               q_chunks=[(qn2[:, nb:nb + S], None),
                                         (qr2[:, gb:gb + S], hh)],
                               k_chunks=[(kn2[:, nb:nb + S], None),
                                         (krF_bf, hh)],
                               v_tok=vt2[:, nb:nb + S],
                               attn_bf2=attn_bf2, sc_sb2=sc_sb2, attnT=attnT,
                               ctx_pc=ctx_pc, sm=sm, tk=tk, prep=nxt,
                               rcp_all=rcp_all_a)
                if nxt is not None:
                    for _ in nxt:
                        pass
            rv_prev = {}
            rd_prev = {}
            for h2 in range(NH):
                slot = h2 % 2
                ttr = ph.do("tensor", lambda e, h2=h2: nc.tensor.transpose(
                    psC[0:16, 0:128], rcp_all_a[:, h2 * 16:(h2 + 1) * 16],
                    id_f32[:]),
                    wait=(ph.w("V") + [tk["avc"]]) if h2 == 0
                    else [x for x in [rv_prev.get(1 - slot)] if x],
                    inc=[("T", 1)])
                rv = ph.do("vector", lambda e, slot=slot: e.tensor_copy(
                    rstg2[:, slot * 128:(slot + 1) * 128], psC[0:16, 0:128]),
                    wait=[ttr] + rd_prev.get(slot, []), inc=[("V", 1)])
                rv_prev[slot] = rv
                rd_prev[slot] = [ph.do("gpsimd", lambda e, h2=h2, slot=slot:
                                       e.dma_start(
                    out=rcpT_a[h2:h2 + 1, :].rearrange("q (c p) -> q c p", c=16),
                    in_=rstg2[:, slot * 128:(slot + 1) * 128]),
                    wait=[rv], inc=[("G", 16)])]
            ph.emit()

    def emit_epilogue(name, w_dram, w_is_bf, x_in, x_out, bias_src=None,
                      lora=None, pre=None, act_scale=None):
        ph = Ph(k, name)
        with ExitStack() as es:
            act_sb = es.enter_context(nc.sbuf_tensor(name + "_act", [128, NT * D], BF16))
            wbf = es.enter_context(nc.sbuf_tensor(name + "_wbf", [128, 2 * 32 * 256], BF16))
            wst = None if w_is_bf else es.enter_context(
                nc.sbuf_tensor(name + "_wst", [128, 4 * 1024], F32))
            xsl = es.enter_context(nc.sbuf_tensor(name + "_xsl", [128, 4 * 256], F32))
            osl = es.enter_context(nc.sbuf_tensor(name + "_osl", [128, 4 * 256], F32))
            brow = es.enter_context(nc.sbuf_tensor(name + "_brow", [1, D], BF16))
            brf = es.enter_context(nc.sbuf_tensor(name + "_brf", [1, 1024], F32))
            btok = None
            if bias_src is not None:
                for c4 in range(4):
                    tl_ = ph.do("sync", lambda e, c4=c4: e.dma_start(
                        out=brf[:], in_=bias_src[None, c4 * 1024:(c4 + 1) * 1024]),
                        wait=[btok] if btok else [], inc=[("Q", 16)])
                    btok = ph.do("vector", lambda e, c4=c4: e.tensor_copy(
                        brow[0:1, c4 * 1024:(c4 + 1) * 1024], brf[:]),
                        wait=[tl_], inc=[("V", 1)])
            pre_tok = pre(ph) if pre is not None else None
            act_toks = []
            for tt in range(NT):
                act_toks.append(ph.do("gpsimd", lambda e, tt=tt: e.dma_start(
                    out=act_sb[:, tt * D:(tt + 1) * D], in_=tls[tt]),
                    wait=ph.w("G") if tt == 0 else [], inc=[("G", 16)]))
            act_tok = act_toks[-1]
            act_first = list(act_toks)
            if act_scale is not None:
                rrow = es.enter_context(
                    nc.sbuf_tensor(name + "_rrow", [1, D], BF16))
                rtk = {}
                mlast = {}

                def scale_act(tt):
                    trd = ph.do("sync", lambda e, tt=tt: e.dma_start(
                        out=rrow[0:1, :].rearrange("q (h j) -> q h j", h=32),
                        in_=act_scale[:, tt * 128:(tt + 1) * 128]),
                        wait=[x for x in [rtk.get(tt - 1)] if x],
                        inc=[("Q", 16)])
                    psx = psC[:, 512:1024]
                    for i in range(8):
                        def bmm(e, i=i, psx=psx):
                            return nc.tensor.matmul(
                                psx, ones_bf[0:1, 0:128],
                                rrow[0:1, i * 512:(i + 1) * 512],
                                start=True, stop=True)
                        tb = ph.do("tensor", bmm,
                                   wait=[trd] + [x for x in
                                                 [mlast.get(0)] if x],
                                   inc=[("T", 1)])

                        def bml(e, tt=tt, i=i, psx=psx):
                            a = act_sb[:, tt * D + i * 512:
                                       tt * D + (i + 1) * 512]
                            return e.tensor_tensor(a, a, psx, op=OP.mult)
                        mlast[0] = ph.do("vector", bml,
                                         wait=[tb, act_toks[tt]],
                                         inc=[("V", 1)])
                    rtk[tt] = mlast[0]
                    return mlast[0]
            psl = [psA[:, 0:256], psA[:, 512:768], psB[:, 0:256], psB[:, 512:768],
                   psC[:, 0:256], psC[:, 512:768]]
            if act_scale is not None:
                psl = psl[:5]
            nps = len(psl)
            psl_free = [None] * nps
            osl_free = [None] * 4
            xtok = {}
            mm_last = {}
            pii = 0
            for nb in range(16):
                wb = (nb % 2) * 8192
                wfree = mm_last.get(nb - 2)
                load_toks = []
                if w_is_bf:
                    for sc8 in range(8):
                        r0 = sc8 * 512
                        srcw = w_dram[r0:r0 + 512, nb * 256:(nb + 1) * 256].rearrange(
                            "(dc p) c -> p dc c", p=128)
                        dstw = wbf[:, wb + sc8 * 1024: wb + (sc8 + 1) * 1024].rearrange(
                            "p (dc c) -> p dc c", dc=4)
                        load_toks.append(ph.do("sync", lambda e, srcw=srcw, dstw=dstw:
                                               e.dma_start(out=dstw, in_=srcw),
                                               wait=[wfree] if wfree else [],
                                               inc=[("Q", 16)]))
                    wready = load_toks
                else:
                    cv = None
                    for sc8 in range(8):
                        r0 = sc8 * 512
                        sb4 = (sc8 % 4) * 1024
                        srcw = w_dram[r0:r0 + 512, nb * 256:(nb + 1) * 256].rearrange(
                            "(dc p) c -> p dc c", p=128)
                        dstw = wst[:, sb4:sb4 + 1024].rearrange(
                            "p (dc c) -> p dc c", dc=4)
                        wld = [t for t in (xtok.get(("w", nb - 1, sc8 + 4)),
                                           xtok.get(("w", nb, sc8 - 4)),
                                           wfree if sc8 == 0 else None) if t]
                        tl_ = ph.do("sync", lambda e, srcw=srcw, dstw=dstw: e.dma_start(
                            out=dstw, in_=srcw), wait=wld, inc=[("Q", 16)])
                        cv = ph.do("vector", lambda e, wb=wb, sc8=sc8, sb4=sb4:
                                   e.tensor_copy(
                                       wbf[:, wb + sc8 * 1024: wb + (sc8 + 1) * 1024],
                                       wst[:, sb4:sb4 + 1024]),
                                   wait=[tl_] + ([wfree] if wfree else []),
                                   inc=[("V", 1)])
                        xtok[("w", nb, sc8)] = cv
                    wready = [cv]
                for tt in range(NT):
                    sl = psl[pii % nps]
                    xb = (pii % 4) * 256
                    ofree = osl_free[pii % 4]
                    tx = ph.do("scalar", lambda e, tt=tt, nb=nb, xb=xb: e.dma_start(
                        out=xsl[:, xb:xb + 256],
                        in_=x_in[tt * 128:(tt + 1) * 128, nb * 256:(nb + 1) * 256]),
                        wait=[t for t in (xtok.get(("x", pii - 4)),) if t],
                        inc=[("Adma", 16)])

                    def big(e, tt=tt, nb=nb, wb=wb, sl=sl):
                        n_ops = DCH + (1 if lora else 0) + (1 if bias_src is not None
                                                            else 0)
                        io = 0
                        for dc in range(DCH):
                            ins = nc.tensor.matmul(
                                sl, act_sb[:, tt * D + dc * 128: tt * D + (dc + 1) * 128],
                                wbf[:, wb + dc * 256: wb + (dc + 1) * 256],
                                start=(io == 0), stop=(io == n_ops - 1))
                            io += 1
                        if lora:
                            t_bf_, bt_sb_ = lora
                            ins = nc.tensor.matmul(
                                sl, t_bf_[:, tt * 128:(tt + 1) * 128],
                                bt_sb_[:, nb * 256:(nb + 1) * 256],
                                start=False, stop=(io == n_ops - 1))
                            io += 1
                        if bias_src is not None:
                            ins = nc.tensor.matmul(
                                sl, ones_bf[0:1, 0:128],
                                brow[0:1, nb * 256:(nb + 1) * 256],
                                start=False, stop=True)
                        return ins
                    if act_scale is not None and nb == 0:
                        act_first[tt] = scale_act(tt)
                    w = [act_tok if (tt == NT - 1 and act_scale is None)
                         else act_first[tt]] + wready + \
                        [t for t in (psl_free[pii % nps], btok, pre_tok) if t]
                    bt_ = ph.do("tensor", big, wait=w, inc=[("T", 1)])
                    mm_last[nb] = bt_
                    at_ = ph.do("vector", lambda e, xb=xb, sl=sl: e.tensor_add(
                        osl[:, xb:xb + 256], sl, xsl[:, xb:xb + 256]),
                        wait=[bt_, tx] + ([ofree] if ofree else []), inc=[("V", 1)])
                    psl_free[pii % nps] = at_
                    xtok[("x", pii)] = at_
                    st_ = ph.do("gpsimd", lambda e, tt=tt, nb=nb, xb=xb: e.dma_start(
                        out=x_out[tt * 128:(tt + 1) * 128, nb * 256:(nb + 1) * 256],
                        in_=osl[:, xb:xb + 256]), wait=[at_], inc=[("G", 16)])
                    osl_free[pii % 4] = st_
                    pii += 1
            ph.emit()

    def emit_mini_epilogue(name, zn_bf, M_bf, brow2_bf, x_in, x_out):
        ph = Ph(k, name)
        wlf = lambda *t: [x for x in t if x]
        with ExitStack() as es:
            xsl = es.enter_context(nc.sbuf_tensor(name + "_xsl", [128, 4 * 512], F32))
            osl = es.enter_context(nc.sbuf_tensor(name + "_osl", [128, 4 * 512], F32))
            psl = [psA[:, 0:512], psA[:, 512:1024], psB[:, 0:512],
                   psB[:, 512:1024], psC[:, 0:512], psC[:, 512:1024]]
            psl_free = [None] * 6
            osl_free = [None] * 4
            xtok = {}
            pii = 0
            for tt in range(NT):
                for nb in range(8):
                    sl = psl[pii % 6]
                    xb = (pii % 4) * 512
                    eng, q = (("sync", "Q") if pii % 2 == 0
                              else ("scalar", "Adma"))
                    tx = ph.do(eng, lambda e, tt=tt, nb=nb, xb=xb: e.dma_start(
                        out=xsl[:, xb:xb + 512],
                        in_=x_in[tt * 128:(tt + 1) * 128,
                                 nb * 512:(nb + 1) * 512]),
                        wait=wlf(xtok.get(pii - 4)), inc=[(q, 16)])

                    def big(e, tt=tt, nb=nb, sl=sl):
                        nc.tensor.matmul(sl, zn_bf[:, tt * 128:(tt + 1) * 128],
                                         M_bf[:, nb * 512:(nb + 1) * 512],
                                         start=True, stop=False)
                        return nc.tensor.matmul(
                            sl, ones_bf[0:1, 0:128],
                            brow2_bf[0:1, nb * 512:(nb + 1) * 512],
                            start=False, stop=True)
                    bt_ = ph.do("tensor", big, wait=wlf(psl_free[pii % 6]),
                                inc=[("T", 1)])
                    at_ = ph.do("vector", lambda e, xb=xb, sl=sl: e.tensor_add(
                        osl[:, xb:xb + 512], sl, xsl[:, xb:xb + 512]),
                        wait=wlf(bt_, tx, osl_free[pii % 4]), inc=[("V", 1)])
                    psl_free[pii % 6] = at_
                    xtok[pii] = at_
                    st_ = ph.do("gpsimd", lambda e, tt=tt, nb=nb, xb=xb:
                                e.dma_start(
                                    out=x_out[tt * 128:(tt + 1) * 128,
                                              nb * 512:(nb + 1) * 512],
                                    in_=osl[:, xb:xb + 512]),
                                wait=[at_], inc=[("G", 16)])
                    osl_free[pii % 4] = st_
                    pii += 1
            ph.emit()

    def make_f1(tag, x_in, t_bf, routing_vec):
        def f1():
            ph = Ph(k, tag)
            with ExitStack() as es:
                xt2 = es.enter_context(nc.sbuf_tensor(tag + "_xt", [128, 2 * D], F32))

                abuf = es.enter_context(nc.sbuf_tensor(tag + "_abuf", [128, D], BF16))
                a_f32 = es.enter_context(nc.sbuf_tensor(tag + "_af", [128, D], F32))
                at_sb = es.enter_context(nc.sbuf_tensor(tag + "_at", [128, D], BF16))
                xln_bf = es.enter_context(nc.sbuf_tensor(tag + "_xln", [128, 2 * D],
                                                          BF16))
                junk_bf = xln_bf
                xlt2 = es.enter_context(nc.sbuf_tensor(tag + "_xlt", [128, 2 * D], BF16))
                t_f32 = es.enter_context(nc.sbuf_tensor(tag + "_tf32", [128, S], F32))
                abuf2_dbg = es.enter_context(nc.sbuf_tensor(tag + "_ab2", [128, D],
                                                            BF16)) if dev_dump == "p4" \
                    else None
                rsum = es.enter_context(nc.sbuf_tensor(tag + "_rsum", [128, 32], F32))
                rtmp = es.enter_context(nc.sbuf_tensor(tag + "_rtmp", [128, 32], F32))
                rm_bf = es.enter_context(nc.sbuf_tensor(tag + "_rmbf", [128, 32], BF16))
                rw_f32 = es.enter_context(nc.sbuf_tensor(tag + "_rw", [8, D], F32))
                rw_bf = es.enter_context(nc.sbuf_tensor(tag + "_rwb", [8, D], BF16))
                rwt = es.enter_context(nc.sbuf_tensor(tag + "_rwt", [128, 256], BF16))
                sm8 = es.enter_context(nc.sbuf_tensor(tag + "_sm8", [1, 8], F32))
                r8col = es.enter_context(nc.sbuf_tensor(tag + "_r8", [8, 1], F32))
                r8bf = es.enter_context(nc.sbuf_tensor(tag + "_r8b", [8, 1], BF16))
                e8_sb = es.enter_context(nc.sbuf_tensor(tag + "_e8", [8, 128], BF16))
                nm8 = es.enter_context(nc.sbuf_tensor(tag + "_nm8", [1, 1], F32))
                dd8 = es.enter_context(nc.sbuf_tensor(tag + "_dd8", [1, 1], F32))
                rb8 = es.enter_context(nc.sbuf_tensor(tag + "_rb8", [1, 8], F32))
                stat = new_stat(tag, es)
                ph.do("sync", lambda e: e.dma_start(out=a_f32[:], in_=fA),
                      inc=[("Q", 16)])
                if dev_dump == "p4" and tag == "p4":
                    ph.do("vector", lambda e: e.memset(abuf[:], 777.0), inc=[("V", 1)])
                ph.do("vector", lambda e: e.tensor_copy(abuf[:], a_f32[:]),
                      wait=ph.w("Q", "V"), inc=[("V", 1)])
                if dev_dump == "p4" and tag == "p4":
                    d10 = nc.dram_tensor("dbg_abuf0", [128, D], BF16,
                                         kind="ExternalOutput").ap()
                    ph.do("gpsimd", lambda e: e.dma_start(out=d10, in_=abuf[:]),
                          wait=ph.w("V"), inc=[("G", 16)])
                    d11 = nc.dram_tensor("dbg_abuf2", [128, D], BF16,
                                         kind="ExternalOutput").ap()
                    def spacer(e):
                        for _ in range(10):
                            ins = e.tensor_scalar_mul(junk_bf[:, 0:2048],
                                                      junk_bf[:, 0:2048], 1.0)
                        return ins
                    ph.do("vector", spacer, wait=ph.w("V"), inc=[("V", 1)])
                    ph.do("vector", lambda e: e.tensor_copy(junk_bf[:, 0:D] if False
                                                            else abuf2_dbg[:], a_f32[:]),
                          wait=ph.w("V"), inc=[("V", 1)])
                    ph.do("gpsimd", lambda e: e.dma_start(out=d11, in_=abuf2_dbg[:]),
                          wait=ph.w("V"), inc=[("G", 16)])

                def adst(e, rnd, pt, lo, hi):
                    return e.copy(at_sb[:, lo * 128:hi * 128], pt[:, 0:(hi - lo) * 128])
                tr16(ph, abuf, 32, adst)
                if dev_dump == "p4" and tag == "p4":
                    d6 = nc.dram_tensor("dbg_at_early", [128, D], BF16,
                                        kind="ExternalOutput").ap()
                    ph.do("gpsimd", lambda e: e.dma_start(out=d6, in_=at_sb[:]),
                          wait=ph.w("Sc", "V", "T"), inc=[("G", 16)])
                    d7 = nc.dram_tensor("dbg_abuf", [128, D], BF16,
                                        kind="ExternalOutput").ap()
                    ph.do("gpsimd", lambda e: e.dma_start(out=d7, in_=abuf[:]),
                          wait=ph.w("Sc", "V", "T"), inc=[("G", 16)])
                    d8 = nc.dram_tensor("dbg_af32", [128, D], F32,
                                        kind="ExternalOutput").ap()
                    ph.do("gpsimd", lambda e: e.dma_start(out=d8, in_=a_f32[:]),
                          wait=ph.w("Sc", "V", "T"), inc=[("G", 16)])
                    d9 = nc.dram_tensor("dbg_af32b", [128, D], F32,
                                        kind="ExternalOutput").ap()
                    ph.do("gpsimd", lambda e: e.dma_start(out=d9, in_=fA),
                          wait=ph.w("Sc", "V", "T"), inc=[("G", 16)])
                ph.do("vector", lambda e: e.memset(rsum[:], 0.0), inc=[("V", 1)])
                ph.do("sync", lambda e: e.dma_start(out=e8_sb[:], in_=c_e8), inc=[("Q", 16)])
                ph.do("sync", lambda e: e.dma_start(out=rw_f32[:], in_=frW), inc=[("Q", 16)])
                ph.do("sync", lambda e: e.dma_start(out=rb8[:], in_=frb[None, :]),
                      inc=[("Q", 16)])
                ph.do("vector", lambda e: e.tensor_copy(rw_bf[:], rw_f32[:]),
                      wait=ph.w("Q"), inc=[("V", 1)])
                for rnd in range(4):
                    pt = ptA if rnd % 2 == 0 else ptB

                    def tr8(e, rnd=rnd, pt=pt):
                        for j in range(8):
                            c = rnd * 8 + j
                            ins = nc.tensor.transpose(pt[:, j * 8:(j + 1) * 8],
                                                      rw_bf[:, c * 128:(c + 1) * 128],
                                                      id_bf[0:8, 0:8])
                        return ins
                    ph.do("tensor", tr8, wait=ph.w("V", "Sc"), inc=[("T", 1)])
                    ph.do("scalar", lambda e, rnd=rnd, pt=pt: e.copy(
                        rwt[:, rnd * 64:(rnd + 1) * 64], pt[:, 0:64]),
                        wait=ph.w("T"), inc=[("Sc", 1)])
                at_ready = ph.w("Sc", "T", "V")
                f1_stats = [new_stat(tag + "s0", es), new_stat(tag + "s1", es)]
                pt_free = {0: at_ready[0], 1: at_ready[0]}
                wtg = wt_gen(ph, es, pt_free) if tag == "p4" else None

                def adv(n):
                    if wtg is not None:
                        for _ in range(n):
                            next(wtg, None)
                ln_prev = [{}, {}]
                slot_free = {}
                tev_prev = None
                for tt in range(NT):
                    tb = (tt % 2) * D
                    par = tt % 2
                    prev = dict(ln_prev[par])
                    prev["dst"] = slot_free.get(par)
                    prev["xln"] = slot_free.get(("xln", par))
                    prev["stat"] = slot_free.get(("stat", par))

                    def mk_dst(e, rnd, pt, tb=tb):
                        return e.copy(xlt2[:, tb + rnd * 1024: tb + (rnd + 1) * 1024],
                                      pt[:])
                    r = ln_tile_tok(ph, x_in, tt, xt2[:, tb:tb + D], f1_stats[par],
                                    xln_bf[:, tb:tb + D], mk_dst, pt_free, prev)
                    ln_prev[par] = {"xt": r["sq"]}
                    slot_free[("xln", par)] = r["xln"]
                    slot_free[("stat", par)] = r["sq"]
                    tg = ph.do("gpsimd", lambda e, tt=tt, tb=tb: e.dma_start(
                        out=tls[tt], in_=xlt2[:, tb:tb + D]), wait=[r["evlast"]],
                        inc=[("G", 16)])

                    def tmm(e, tb=tb):
                        for dc in range(DCH):
                            ins = nc.tensor.matmul(
                                psC[:, 0:128], at_sb[:, dc * 128:(dc + 1) * 128],
                                xlt2[:, tb + dc * 128: tb + (dc + 1) * 128],
                                start=(dc == 0), stop=(dc == DCH - 1))
                        return ins
                    w = [r["evlast"]] + ([tev_prev] if tev_prev else []) + \
                        (at_ready if tt == 0 else [])
                    tmt = ph.do("tensor", tmm, wait=w, inc=[("T", 1)])
                    tev = ph.do("vector", lambda e, tt=tt: e.tensor_copy(
                        t_f32[:, tt * 128:(tt + 1) * 128], psC[:, 0:128]),
                        wait=[tmt], inc=[("V", 1)])
                    tev_prev = tev
                    slots = [tg, tmt]
                    lo = max(Q_LO, tt * 128) - tt * 128
                    hi = min(Q_HI, (tt + 1) * 128) - tt * 128
                    if hi > lo:
                        def rsel(e, tb=tb, lo=lo, hi=hi):
                            src = xlt2[:, tb:tb + D].rearrange(
                                "p (dc j) -> p dc j", dc=DCH)[:, :, lo:hi]
                            return e.reduce_sum(rtmp[:], src, axis=AX.X)
                        trs_ = ph.do("vector", rsel, wait=[r["evlast"]],
                                     inc=[("V", 1)])
                        ph.do("vector", lambda e: e.tensor_add(rsum[:], rsum[:],
                                                               rtmp[:]),
                              wait=[trs_], inc=[("V", 1)])
                        slots.append(trs_)
                    slot_free[par] = slots
                    adv(14)
                if wtg is not None:
                    for _ in wtg:
                        pass
                ph.do("vector", lambda e: e.tensor_scalar_mul(
                    rtmp[:], rsum[:], 1.0 / ROUTER_CNT), wait=ph.w("V"), inc=[("V", 1)])
                ph.do("vector", lambda e: e.tensor_copy(rm_bf[:], rtmp[:]),
                      wait=ph.w("V"), inc=[("V", 1)])

                def lmm(e):
                    for dc in range(DCH):
                        ins = nc.tensor.matmul(psC[0:8, 0:1], rwt[:, dc * 8:(dc + 1) * 8],
                                               rm_bf[:, dc:dc + 1],
                                               start=(dc == 0), stop=(dc == DCH - 1))
                    return ins
                ph.do("tensor", lmm, wait=ph.w("V", "Sc"), inc=[("T", 1)])
                ph.do("vector", lambda e: e.tensor_copy(r8col[:], psC[0:8, 0:1]),
                      wait=ph.w("T"), inc=[("V", 1)])
                ph.do("tensor", lambda e: nc.tensor.transpose(psC[0:1, 512:520], r8col[:],
                                                              id_f32[0:8, 0:8]),
                      wait=ph.w("V"), inc=[("T", 1)])

                ph.do("vector", lambda e: e.tensor_add(sm8[:], psC[0:1, 512:520],
                                                       rb8[:]),
                      wait=ph.w("T", "Q", "V"), inc=[("V", 1)])
                ph.do("vector", lambda e: e.reduce_max(nm8[:], sm8[:], axis=AX.X,
                                                       negate=True),
                      wait=ph.w("V"), inc=[("V", 1)])
                ph.do("scalar", lambda e: e.activation(sm8[:], sm8[:], AF.Exp, bias=nm8[:],
                                                       accum_out=dd8[:]),
                      wait=ph.w("V"), inc=[("Sc", 1)])

                ph.do("vector", lambda e: e.reciprocal(dd8[:], dd8[:]),
                      wait=ph.w("Sc", "V"), inc=[("V", 1)])
                ph.do("vector", lambda e: e.tensor_scalar(sm8[:], sm8[:], dd8[:], 2.0,
                                                          op0=OP.mult, op1=OP.mult),
                      wait=ph.w("V"), inc=[("V", 1)])
                ph.do("tensor", lambda e: nc.tensor.transpose(psC[0:8, 640:641], sm8[:],
                                                              id_f32[0:1, 0:1]),
                      wait=ph.w("V"), inc=[("T", 1)])
                ph.do("vector", lambda e: e.tensor_copy(r8bf[:], psC[0:8, 640:641]),
                      wait=ph.w("T"), inc=[("V", 1)])
                ph.do("tensor", lambda e: nc.tensor.matmul(psC[:, 896:897], e8_sb[:],
                                                           r8bf[:], start=True, stop=True),
                      wait=ph.w("V"), inc=[("T", 1)])
                ph.do("vector", lambda e: e.tensor_copy(routing_vec[:], psC[:, 896:897]),
                      wait=ph.w("T"), inc=[("V", 1)])
                ph.do("vector", lambda e: e.tensor_scalar_mul(t_bf[:], t_f32[:],
                                                              routing_vec[:]),
                      wait=ph.w("V"), inc=[("V", 1)])
                if dev_dump == "p4" and tag == "p4":
                    d1 = nc.dram_tensor("dbg_t", [128, S], F32, kind="ExternalOutput").ap()
                    d2 = nc.dram_tensor("dbg_tbf", [128, S], BF16,
                                        kind="ExternalOutput").ap()
                    d3 = nc.dram_tensor("dbg_rv", [128, 1], F32,
                                        kind="ExternalOutput").ap()
                    d4 = nc.dram_tensor("dbg_rsum", [128, 32], F32,
                                        kind="ExternalOutput").ap()
                    ph.do("gpsimd", lambda e: e.dma_start(out=d1, in_=t_f32[:]),
                          wait=ph.w("V", "Sc"), inc=[("G", 16)])
                    ph.do("gpsimd", lambda e: e.dma_start(out=d2, in_=t_bf[:]),
                          wait=ph.w("V", "Sc"), inc=[("G", 16)])
                    ph.do("gpsimd", lambda e: e.dma_start(out=d3, in_=routing_vec[:]),
                          wait=ph.w("V", "Sc"), inc=[("G", 16)])
                    ph.do("gpsimd", lambda e: e.dma_start(out=d4, in_=rsum[:]),
                          wait=ph.w("V", "Sc"), inc=[("G", 16)])
                    d5 = nc.dram_tensor("dbg_at", [128, D], BF16,
                                        kind="ExternalOutput").ap()
                    ph.do("gpsimd", lambda e: e.dma_start(out=d5, in_=at_sb[:]),
                          wait=ph.w("V", "Sc"), inc=[("G", 16)])
                ph.emit()
        return f1

    def make_f2_pre(bt_sb, uniq=""):
        def f2_pre(ph):
            with ExitStack() as es:
                b_f32 = es.enter_context(nc.sbuf_tensor("btb_f32" + uniq, [128, 512], F32))
                b_bf = es.enter_context(nc.sbuf_tensor("btb_bf" + uniq, [128, 2 * 512], BF16))
                btst = es.enter_context(nc.sbuf_tensor("btb_st" + uniq, [128, 32], BF16))
                for pg in range(4):
                    for ei in range(2):
                        e_idx = pg * 2 + ei
                        src = fB[e_idx].rearrange("(ob p) r -> p ob r", p=128)
                        dstb = b_f32.rearrange("p (ob r) -> p ob r", ob=32)
                        for q4 in range(4):
                            ph.do("sync", lambda e, src=src, dstb=dstb, q4=q4:
                                  e.dma_start(out=dstb[:, q4 * 8:(q4 + 1) * 8, :],
                                              in_=src[:, q4 * 8:(q4 + 1) * 8, :]),
                                  wait=ph.w("V"), inc=[("Q", 16)])
                        ph.do("vector", lambda e, ei=ei: e.tensor_copy(
                            b_bf[:, ei * 512:(ei + 1) * 512], b_f32[:]),
                            wait=ph.w("Q", "T"), inc=[("V", 1)])
                    for rnd in range(4):
                        pt = ptA if rnd % 2 == 0 else ptB
                        for j in range(8):
                            ob = rnd * 8 + j

                            def st2(e, ob=ob):
                                e.tensor_copy(btst[:, 0:16],
                                              b_bf[:, ob * 16:(ob + 1) * 16])
                                return e.tensor_copy(
                                    btst[:, 16:32],
                                    b_bf[:, 512 + ob * 16: 512 + (ob + 1) * 16])
                            ph.do("vector", st2, wait=ph.w("T", "V"), inc=[("V", 1)])
                            ph.do("tensor", lambda e, pt=pt, j=j: nc.tensor.transpose(
                                pt[0:32, j * 128:(j + 1) * 128], btst[:], id_bf[:]),
                                wait=ph.w("V", "Sc"), inc=[("T", 1)])

                        def bev(e, pg=pg, rnd=rnd, pt=pt):
                            return e.copy(bt_sb[pg * 32:(pg + 1) * 32,
                                                rnd * 1024:(rnd + 1) * 1024], pt[0:32, :])
                        bev_tok = ph.do("scalar", bev, wait=ph.w("T"), inc=[("Sc", 1)])
                if dev_dump == "p4":
                    db = nc.dram_tensor("dbg_bt", [128, D], BF16,
                                        kind="ExternalOutput").ap()
                    ph.do("gpsimd", lambda e: e.dma_start(out=db, in_=bt_sb[:]),
                          wait=ph.w("Sc", "V"), inc=[("G", 16)])
                return bev_tok
        return f2_pre

    def p6(qpack, ckv_pack, cq_bf, ckv_bf, bvu_sb, wvu5_all, ckv_tok):
        ph = Ph(k, "p6")
        with ExitStack() as es:
            xt2 = es.enter_context(nc.sbuf_tensor("p6_xt", [128, 2 * D], F32))
            xln_bf = es.enter_context(nc.sbuf_tensor("p6_xln", [128, 2 * D], BF16))
            junk_bf = xln_bf
            xlt2 = es.enter_context(nc.sbuf_tensor("p6_xlt", [128, 2 * D], BF16))
            wq_f32 = es.enter_context(nc.sbuf_tensor("p6_wqf", [128, 32 * 12], F32))
            wq_bf = es.enter_context(nc.sbuf_tensor("p6_wqb", [128, 32 * 12], BF16))
            wk_f32 = es.enter_context(nc.sbuf_tensor("p6_wkf", [128, 32 * 4], F32))
            wk_bf = es.enter_context(nc.sbuf_tensor("p6_wkb", [128, 32 * 4], BF16))
            b12 = es.enter_context(nc.sbuf_tensor("p6_b12", [1, 12], F32))
            b4 = es.enter_context(nc.sbuf_tensor("p6_b4", [1, 4], F32))
            bqd_col = es.enter_context(nc.sbuf_tensor("p6_bqd", [12, 1], F32))
            bkvd_col = es.enter_context(nc.sbuf_tensor("p6_bkvd", [4, 1], F32))
            wqk_f32 = es.enter_context(nc.sbuf_tensor("p6_wqk", [12, 128], F32))
            wqk_bf = es.enter_context(nc.sbuf_tensor("p6_wqkb", [12, 128], BF16))
            wqk_pack = es.enter_context(nc.sbuf_tensor("p6_wqkp", [12, 8 * 128], BF16))
            bqk_row = es.enter_context(nc.sbuf_tensor("p6_bqk", [1, 128], F32))
            bqk_col = es.enter_context(nc.sbuf_tensor("p6_bqkc", [128, 1], F32))
            bqk_packed = es.enter_context(nc.sbuf_tensor("p6_bqkp", [128, 1], F32))
            bvu_rows = es.enter_context(nc.sbuf_tensor("p6_bvur", [32, 128], F32))
            stat = new_stat("p6", es)
            for dc in range(DCH):
                ph.do("sync", lambda e, dc=dc: e.dma_start(
                    out=wq_f32[:, dc * 12:(dc + 1) * 12],
                    in_=bWqd[dc * 128:(dc + 1) * 128, :]), inc=[("Q", 16)])
                ph.do("sync", lambda e, dc=dc: e.dma_start(
                    out=wk_f32[:, dc * 4:(dc + 1) * 4],
                    in_=bWkvd[dc * 128:(dc + 1) * 128, :]), inc=[("Q", 16)])
            ph.do("vector", lambda e: e.tensor_copy(wq_bf[:], wq_f32[:]),
                  wait=ph.w("Q"), inc=[("V", 1)])
            ph.do("vector", lambda e: e.tensor_copy(wk_bf[:], wk_f32[:]), inc=[("V", 1)])
            ph.do("sync", lambda e: e.dma_start(out=b12[:], in_=bbqd[None, :]),
                  inc=[("Q", 16)])
            ph.do("sync", lambda e: e.dma_start(out=b4[:], in_=bbkvd[None, :]),
                  inc=[("Q", 16)])
            ph.do("sync", lambda e: e.dma_start(out=wqk_f32[:], in_=bWqk), inc=[("Q", 16)])
            ph.do("sync", lambda e: e.dma_start(out=bqk_row[:], in_=bbqk[None, :]),
                  inc=[("Q", 16)])
            ph.do("sync", lambda e: e.dma_start(
                out=bvu_rows[:], in_=bbvu.rearrange("(h j) -> h j", h=32)),
                inc=[("Q", 16)])
            ph.do("vector", lambda e: e.tensor_copy(wqk_bf[:], wqk_f32[:]),
                  wait=ph.w("Q"), inc=[("V", 1)])
            ph.do("tensor", lambda e: nc.tensor.transpose(psC[0:12, 0:1], b12[:],
                                                          id_f32[0:1, 0:1]),
                  wait=ph.w("Q"), inc=[("T", 1)])
            ph.do("vector", lambda e: e.tensor_copy(bqd_col[:], psC[0:12, 0:1]),
                  wait=ph.w("T"), inc=[("V", 1)])
            ph.do("tensor", lambda e: nc.tensor.transpose(psC[0:4, 512:513], b4[:],
                                                          id_f32[0:1, 0:1]),
                  wait=ph.w("V"), inc=[("T", 1)])
            ph.do("vector", lambda e: e.tensor_copy(bkvd_col[:], psC[0:4, 512:513]),
                  wait=ph.w("T"), inc=[("V", 1)])
            ph.do("tensor", lambda e: nc.tensor.transpose(psC[:, 640:641], bqk_row[:],
                                                          id_f32[0:1, 0:1]),
                  wait=ph.w("V"), inc=[("T", 1)])
            ph.do("vector", lambda e: e.tensor_scalar(bqk_col[:], psC[:, 640:641], 0.5,
                                                      None, op0=OP.mult),
                  wait=ph.w("T"), inc=[("V", 1)])
            ph.do("tensor", lambda e: nc.tensor.transpose(psC[:, 768:800], bvu_rows[:],
                                                          id_f32[0:32, 0:32]),
                  wait=ph.w("V"), inc=[("T", 1)])
            tbvu = ph.do("vector", lambda e: e.tensor_copy(bvu_sb[:],
                                                           psC[:, 768:800]),
                         wait=ph.w("T"), inc=[("V", 1)])
            ph.do("gpsimd", lambda e: e.memset(wqk_pack[:], 0.0), inc=[("Gc", 1)])
            ph.do("gpsimd", lambda e: e.memset(ckv_pack[:], 0.0), inc=[("Gc", 1)])
            ph.do("gpsimd", lambda e: e.memset(bqk_packed[:], 0.0), inc=[("Gc", 1)])

            def wqkp(e):
                for g in range(8):
                    for s_ in range(4):
                        hsrc = (4 * g + s_) * 4
                        ins = e.tensor_copy(
                            wqk_pack[:, g * 128 + 32 * s_: g * 128 + 32 * s_ + 4],
                            wqk_bf[:, hsrc:hsrc + 4])
                return ins
            ph.do("vector", wqkp, wait=ph.w("Gc", "V"), inc=[("V", 1)])
            w_ready = ph.w("Sc", "V", "T")
            p6_stats = [new_stat("p6s0", es), new_stat("p6s1", es)]
            pt_free = {}
            ln_prev = [{}, {}]
            slot_free = {}
            cev_tok = {}
            for tt in range(NT):
                tb = (tt % 2) * D
                par = tt % 2
                prev = dict(ln_prev[par])
                prev["dst"] = slot_free.get(par)
                prev["xln"] = slot_free.get(("xln", par))
                prev["stat"] = slot_free.get(("stat", par))

                def mk_dst(e, rnd, pt, tb=tb):
                    return e.copy(xlt2[:, tb + rnd * 1024: tb + (rnd + 1) * 1024],
                                  pt[:])
                r = ln_tile_tok(ph, xs_b, tt, xt2[:, tb:tb + D], p6_stats[par],
                                xln_bf[:, tb:tb + D], mk_dst, pt_free, prev)
                ln_prev[par] = {"xt": r["sq"]}
                slot_free[("xln", par)] = r["xln"]
                slot_free[("stat", par)] = r["sq"]
                ps = psA if tt % 2 == 0 else psB

                def cmm(e, tb=tb, ps=ps):
                    for dc in range(DCH):
                        nc.tensor.matmul(ps[0:12, 0:128], wq_bf[:, dc * 12:(dc + 1) * 12],
                                         xlt2[:, tb + dc * 128: tb + (dc + 1) * 128],
                                         start=(dc == 0), stop=(dc == DCH - 1))
                    for dc in range(DCH):
                        ins = nc.tensor.matmul(ps[0:4, 512:640],
                                               wk_bf[:, dc * 4:(dc + 1) * 4],
                                               xlt2[:, tb + dc * 128: tb + (dc + 1) * 128],
                                               start=(dc == 0), stop=(dc == DCH - 1))
                    return ins
                w = [r["evlast"]] + ([cev_tok[tt - 2]] if tt >= 2 else w_ready)
                tmc = ph.do("tensor", cmm, wait=w, inc=[("T", 1)])

                def cev(e, tt=tt, ps=ps):
                    e.tensor_scalar(cq_bf[:, tt * 128:(tt + 1) * 128], ps[0:12, 0:128],
                                    1.0, bqd_col[0:12, :], op0=OP.mult, op1=OP.add)
                    return e.tensor_scalar(ckv_bf[:, tt * 128:(tt + 1) * 128],
                                           ps[0:4, 512:640], 1.0, bkvd_col[0:4, :],
                                           op0=OP.mult, op1=OP.add)
                cev_tok[tt] = ph.do("vector", cev, wait=[tmc], inc=[("V", 1)])
                slot_free[par] = [tmc]

            def ckvp(e):
                for s_ in range(4):
                    ins = e.tensor_copy(ckv_pack[32 * s_:32 * s_ + 4, :], ckv_bf[:])
                return ins
            ph.do("vector", ckvp, wait=ph.w("Gc", "V"), inc=[("V", 1)])
            cq_ready = ph.w("V")
            vev = None
            for g in range(8):
                gts = []
                for s_ in range(4):
                    src_p = 4 * (4 * g + s_)
                    gts.append(ph.do("gpsimd", lambda e, s_=s_, src_p=src_p:
                                     e.dma_start(
                                         out=bqk_packed[32 * s_:32 * s_ + 4, :],
                                         in_=bqk_col[src_p:src_p + 4, :]),
                                     wait=([vev] if vev else ph.w("V")),
                                     inc=[("G", 16)]))
                for qh in range(2):
                    tmq = None
                    for j in range(2):
                        tmq = ph.do("tensor", lambda e, g=g, qh=qh, j=j:
                                    nc.tensor.matmul(
                                        psC[:, j * 512:(j + 1) * 512],
                                        wqk_pack[:, g * 128:(g + 1) * 128],
                                        cq_bf[:, qh * 1024 + j * 512:
                                              qh * 1024 + (j + 1) * 512],
                                        start=True, stop=True),
                                    wait=([vev] if vev else cq_ready) if j == 0
                                    else [], inc=[("T", 1)])
                    vev = ph.do("vector", lambda e, g=g, qh=qh: e.tensor_scalar(
                        qpack[:, g * S + qh * 1024: g * S + (qh + 1) * 1024],
                        psC[:, 0:1024], 0.5, bqk_packed[:], op0=OP.mult, op1=OP.add),
                        wait=[tmq] + gts, inc=[("V", 1)])

            wvu_f32 = es.enter_context(nc.sbuf_tensor("p6_wvf", [4, D], F32))
            wvu_bf = es.enter_context(nc.sbuf_tensor("p6_wvb", [4, D], BF16))
            wlf2 = lambda *t: [x for x in t if x]
            tw = ph.do("sync", lambda e: e.dma_start(out=wvu_f32[:], in_=bWvu),
                       inc=[("Q", 16)])
            twc = ph.do("vector", lambda e: e.tensor_copy(wvu_bf[:], wvu_f32[:]),
                        wait=[tw], inc=[("V", 1)])
            pt_ev = {}
            for r in range(4):
                pt = ptA if r % 2 == 0 else ptB

                def wtr(e, r=r, pt=pt):
                    for j in range(8):
                        h_ = r * 8 + j
                        ins = nc.tensor.transpose(
                            pt[:, j * 128:j * 128 + 4],
                            wvu_bf[0:4, h_ * 128:(h_ + 1) * 128],
                            id_bf[0:4, 0:4])
                    return ins
                ttr = ph.do("tensor", wtr, wait=wlf2(twc, pt_ev.get(r - 2)),
                            inc=[("T", 1)])

                def wev(e, r=r, pt=pt):
                    dst = wvu5_all.rearrange("p (h f) -> p h f", f=5)[
                        :, r * 8:(r + 1) * 8, 0:4]
                    src = pt.rearrange("p (h f) -> p h f", f=128)[:, 0:8, 0:4]
                    return e.tensor_copy(dst, src)
                pt_ev[r] = ph.do("vector", wev, wait=[ttr], inc=[("V", 1)])
            ph.do("vector", lambda e: e.tensor_copy(
                wvu5_all.rearrange("p (h f) -> p h f", f=5)[:, :, 4:5],
                bvu_sb.rearrange("p (h f) -> p h f", f=1)),
                wait=[tbvu], inc=[("V", 1)])
            for r in range(2):
                pt = ptA if r % 2 == 0 else ptB

                def ctr(e, r=r, pt=pt):
                    for j in range(8):
                        c = r * 8 + j
                        ins = nc.tensor.transpose(
                            pt[:, j * 128:j * 128 + 4],
                            ckv_bf[0:4, c * 128:(c + 1) * 128],
                            id_bf[0:4, 0:4])
                    return ins
                tct = ph.do("tensor", ctr,
                            wait=wlf2(cev_tok[r * 8 + 7], pt_ev.get(r + 2)),
                            inc=[("T", 1)])

                def cevk(e, r=r, pt=pt):
                    dst = ckv_tok.rearrange("p (h f) -> p h f", f=4)[
                        :, r * 8:(r + 1) * 8, :]
                    src = pt.rearrange("p (h f) -> p h f", f=128)[:, 0:8, 0:4]
                    return e.tensor_copy(dst, src)
                ph.do("vector", cevk, wait=[tct], inc=[("V", 1)])
            ph.emit()

    def p7(qpack, ckv_pack, wvu5_all, ckv_tok, zn_bf, M_bf, bvw_parts,
           brow2_bf):
        ph = Ph(k, "p7")
        wlf = lambda *t: [x for x in t if x]
        with ExitStack() as es:
            wo_f32 = es.enter_context(nc.sbuf_tensor("p7_wof", [128, 2048], F32))
            wo_bf = es.enter_context(nc.sbuf_tensor("p7_wob", [128, 2 * D], BF16))
            mstg = es.enter_context(nc.sbuf_tensor("p7_mst", [64, 1024], BF16))
            zstg = es.enter_context(nc.sbuf_tensor("p7_zst", [4, 1024], F32))
            zn_f32 = es.enter_context(nc.sbuf_tensor("p7_znf", [128, S], F32))
            rcp_all = es.enter_context(nc.sbuf_tensor("p7_rca", [128, 512], F32))
            RCPE = es.enter_context(nc.sbuf_tensor("p7_rce", [128, S], F32))
            rstg = es.enter_context(nc.sbuf_tensor("p7_rst", [16, 256], F32))
            bo_f32 = es.enter_context(nc.sbuf_tensor("p7_bo", [1, D], F32))
            brow2_f32 = es.enter_context(nc.sbuf_tensor("p7_br2", [1, D], F32))
            ones32 = es.enter_context(nc.sbuf_tensor("p7_o32", [32, 1], BF16))
            attn_bf2 = es.enter_context(nc.sbuf_tensor("p7_at2", [128, 2 * S], BF16))
            sc_sb2 = es.enter_context(nc.sbuf_tensor("p7_scb", [128, 2 * S], F32))
            attnT = es.enter_context(nc.sbuf_tensor("p7_aT", [128, 16 * 512], BF16))
            sm = {"mx": es.enter_context(nc.sbuf_tensor("p7_mx", [128, 4], F32)),
                  "nmx": es.enter_context(nc.sbuf_tensor("p7_nmx", [128, 2], F32)),
                  "dd": es.enter_context(nc.sbuf_tensor("p7_dd", [128, 4], F32)),
                  "rcp": es.enter_context(nc.sbuf_tensor("p7_rcp", [128, 2], F32))}
            tk = {}

            def mk_prep(h):
                wb = (h % 2) * D

                def gen():
                    casts = []
                    for c4 in range(4):
                        slot = c4 % 2

                        def wld(e, h=h, c4=c4, slot=slot):
                            return e.dma_start(
                                out=wo_f32[:, slot * 1024:(slot + 1) * 1024],
                                in_=bWo[h * 128:(h + 1) * 128,
                                        c4 * 1024:(c4 + 1) * 1024])
                        tl_ = ph.do("sync", wld,
                                    wait=wlf(casts[c4 - 2] if c4 >= 2 else
                                             tk.get("wof%d" % slot)),
                                    inc=[("Q", 16)])
                        tc = ph.do("vector", lambda e, c4=c4, slot=slot, wb=wb:
                                   e.tensor_copy(
                                       wo_bf[:, wb + c4 * 1024:
                                             wb + (c4 + 1) * 1024],
                                       wo_f32[:, slot * 1024:(slot + 1) * 1024]),
                                   wait=wlf(tl_, tk.get("wo_used_%d" % (h % 2))
                                            if c4 == 0 else None),
                                   inc=[("V", 1)])
                        casts.append(tc)
                        tk["wof%d" % slot] = tc
                        yield
                    mm_last = None
                    ev_prev = None
                    dma_prev = {}
                    ev_hist = {}
                    for nb in range(8):
                        slot = nb % 2
                        pb = slot * 32

                        def mmm(e, h=h, nb=nb, wb=wb, pb=pb):
                            return nc.tensor.matmul(
                                psC[pb:pb + 5, 512:1024],
                                wvu5_all[:, h * 5:(h + 1) * 5],
                                wo_bf[:, wb + nb * 512: wb + (nb + 1) * 512],
                                start=True, stop=True)
                        tm = ph.do("tensor", mmm,
                                   wait=wlf(casts[nb // 2],
                                            ev_hist.get(slot,
                                                        tk.get("mps%d" % slot))),
                                   inc=[("T", 1)])
                        mm_last = tm

                        def mev(e, slot=slot, pb=pb):
                            return e.tensor_copy(
                                mstg[pb:pb + 5, slot * 512:(slot + 1) * 512],
                                psC[pb:pb + 5, 512:1024])
                        ev = ph.do("vector", mev,
                                   wait=wlf(tm) + dma_prev.get(slot, []),
                                   inc=[("V", 1)])
                        ev_hist[slot] = ev
                        tk["mps%d" % slot] = ev
                        ev_prev = ev
                        d1 = ph.do("scalar", lambda e, h=h, nb=nb, slot=slot,
                                   pb=pb: e.dma_start(
                                       out=M_bf[h * 4:(h + 1) * 4,
                                                nb * 512:(nb + 1) * 512],
                                       in_=mstg[pb:pb + 4,
                                                slot * 512:(slot + 1) * 512]),
                                   wait=[ev], inc=[("Adma", 16)])
                        d2 = ph.do("scalar", lambda e, h=h, nb=nb, slot=slot,
                                   pb=pb: e.dma_start(
                                       out=bvw_parts[h:h + 1,
                                                     nb * 512:(nb + 1) * 512],
                                       in_=mstg[pb + 4:pb + 5,
                                                slot * 512:(slot + 1) * 512]),
                                   wait=[ev], inc=[("Adma", 16)])
                        dma_prev[slot] = [d1, d2]
                        if nb % 2 == 1:
                            yield
                    tk["wo_used_%d" % (h % 2)] = mm_last
                    tk["mev_last"] = ev_prev
                    tk["vt"] = None
                    tk["qk"] = []
                return gen()

            preps = [mk_prep(h) for h in range(NH)] + [None]
            for _ in preps[0]:
                pass

            zdma = {}

            def z_sink(qg, h, gg, tmm):
                slot = qg % 2
                tz = ph.do("scalar", lambda e, slot=slot: e.copy(
                    zstg[0:4, slot * 512:(slot + 1) * 512], psC[0:4, 0:512]),
                    wait=wlf(tmm, zdma.get(slot)), inc=[("Sc", 1)])
                td = ph.do("gpsimd", lambda e, h=h, gg=gg, slot=slot:
                           e.dma_start(
                               out=zn_f32[h * 4:(h + 1) * 4,
                                          gg * 512:(gg + 1) * 512],
                               in_=zstg[0:4, slot * 512:(slot + 1) * 512]),
                           wait=[tz], inc=[("G", 16)])
                zdma[slot] = td
                tk["z_last"] = tz
                return tz

            def head_args(h):
                g, s_ = h // 4, h % 4
                return ([(qpack[:, g * S:(g + 1) * S], "strip")],
                        [(ckv_pack, "strip")], ckv_tok,
                        (32 * s_, 0), s_)
            emit_attention_stream(ph, NH, head_args, attn_bf2, sc_sb2, attnT,
                                  None, sm, tk, preps, vdw=4, z_sink=z_sink,
                                  rcp_all=rcp_all)

            rv_prev = {}
            rd_prev = {}
            for h in range(NH):
                slot = h % 2
                ttr = ph.do("tensor", lambda e, h=h: nc.tensor.transpose(
                    psC[0:16, 0:128], rcp_all[:, h * 16:(h + 1) * 16],
                    id_f32[:]),
                    wait=(ph.w("V") + wlf(tk.get("z_last"))) if h == 0
                    else wlf(rv_prev.get(1 - slot)), inc=[("T", 1)])
                rv = ph.do("vector", lambda e, slot=slot: e.tensor_copy(
                    rstg[:, slot * 128:(slot + 1) * 128], psC[0:16, 0:128]),
                    wait=[ttr] + rd_prev.get(slot, []), inc=[("V", 1)])
                rv_prev[slot] = rv
                rd_prev[slot] = []
                for j in range(4):
                    eng, q = (("gpsimd", "G"), ("sync", "Q"))[j % 2]
                    rd_prev[slot].append(ph.do(
                        eng, lambda e, h=h, j=j, slot=slot: e.dma_start(
                            out=RCPE[h * 4 + j:h * 4 + j + 1, :].rearrange(
                                "q (c p) -> q c p", c=16),
                            in_=rstg[:, slot * 128:(slot + 1) * 128]),
                        wait=[rv], inc=[(q, 16)]))
            ph.do("vector", lambda e: e.tensor_tensor(
                zn_bf[:], zn_f32[:], RCPE[:], op=OP.mult),
                wait=ph.w("Q", "G"), inc=[("V", 1)])

            tone = ph.do("gpsimd", lambda e: e.memset(ones32[:], 1.0),
                         inc=[("Gc", 1)])
            tbo = ph.do("sync", lambda e: e.dma_start(out=bo_f32[:],
                                                      in_=bbo[None, :]),
                        inc=[("Q", 16)])
            tprev = None
            for nb in range(8):
                w = wlf(tprev)
                if nb == 0:
                    w = ph.w("Adma", "Gc") + wlf(tk.get("mev_last"))
                tm = ph.do("tensor", lambda e, nb=nb: nc.tensor.matmul(
                    psC[0:1, 512:1024], ones32[:],
                    bvw_parts[:, nb * 512:(nb + 1) * 512],
                    start=True, stop=True), wait=w, inc=[("T", 1)])
                tprev = ph.do("vector", lambda e, nb=nb: e.tensor_tensor(
                    brow2_f32[0:1, nb * 512:(nb + 1) * 512],
                    psC[0:1, 512:1024],
                    bo_f32[0:1, nb * 512:(nb + 1) * 512], op=OP.add),
                    wait=wlf(tm, tbo if nb == 0 else None), inc=[("V", 1)])
            ph.do("vector", lambda e: e.tensor_copy(brow2_bf[:], brow2_f32[:]),
                  wait=[tprev], inc=[("V", 1)])
            ph.emit()

    names = ["p0", "p1", "p2", "p3", "p4", "p5", "p6", "p7", "p8", "p9", "p10"]
    n_run = len(names) if n_phases is None else n_phases
    redir = names[n_run - 1] if (dev_dump == "redir" and n_run < len(names)) else None

    def xout(name, default):
        return out if name == redir else default

    if n_run >= 1:
        p0()
    if n_run >= 2:
        with (
            nc.sbuf_tensor("q_cF", [128, 12 * S], BF16) as q_cF,
            nc.sbuf_tensor("k_cF", [128, 4 * S], BF16) as k_cF,
            nc.sbuf_tensor("krF_bf", [128, S], BF16) as krF_bf,
        ):
            p1(q_cF, k_cF, krF_bf)
            if n_run >= 3:
                p2(q_cF, k_cF, krF_bf)
    if n_run >= 4:
        emit_epilogue("p3", wo_a, False, x0, xout("p3", xs_a),
                      act_scale=rcpT_a)
    if n_run >= 5:
        with (
            nc.sbuf_tensor("t_bf", [128, S], BF16) as t_bf,
            nc.sbuf_tensor("routing_vec", [128, 1], F32) as routing_vec,
        ):
            make_f1("p4", xs_a, t_bf, routing_vec)()
            if n_run >= 6:
                with nc.sbuf_tensor("bt_sb", [128, D], BF16) as bt_sb:
                    if DEV_P5_STRIP == 1:
                        emit_epilogue("p5", wt_scr, True, xs_a, xout("p5", xs_b))
                    elif DEV_P5_STRIP == 2:
                        emit_epilogue("p5", wt_scr, True, xs_a, xout("p5", xs_b),
                                      pre=make_f2_pre(bt_sb))
                    elif DEV_P5_STRIP == 3:
                        emit_epilogue("p5", wt_scr, True, xs_a, xout("p5", xs_b),
                                      bias_src=fb, pre=make_f2_pre(bt_sb))
                    else:
                        emit_epilogue("p5", wt_scr, True, xs_a, xout("p5", xs_b),
                                      bias_src=fb, lora=(t_bf, bt_sb),
                                      pre=make_f2_pre(bt_sb))
    if n_run >= 7:
        with (
            nc.sbuf_tensor("qpack", [128, 8 * S], BF16) as qpack,
            nc.sbuf_tensor("ckv_pack", [128, S], BF16) as ckv_pack,
            nc.sbuf_tensor("cq_bf", [12, S], BF16) as cq_bf,
            nc.sbuf_tensor("ckv_bf", [4, S], BF16) as ckv_bf,
            nc.sbuf_tensor("bvu_sb", [128, 32], F32) as bvu_sb,
            nc.sbuf_tensor("wvu5_all", [128, 160], BF16) as wvu5_all,
            nc.sbuf_tensor("ckv_tok", [128, 64], BF16) as ckv_tok,
            nc.sbuf_tensor("zn_bf", [128, S], BF16) as zn_bf,
            nc.sbuf_tensor("M_bf", [128, D], BF16) as M_bf,
            nc.sbuf_tensor("bvw_parts", [32, D], BF16) as bvw_parts,
            nc.sbuf_tensor("brow2_bf", [1, D], BF16) as brow2_bf,
        ):
            p6(qpack, ckv_pack, cq_bf, ckv_bf, bvu_sb, wvu5_all, ckv_tok)
            if n_run >= 8:
                p7(qpack, ckv_pack, wvu5_all, ckv_tok, zn_bf, M_bf,
                   bvw_parts, brow2_bf)
            if n_run >= 9:
                emit_mini_epilogue("p8", zn_bf, M_bf, brow2_bf, xs_b,
                                   xout("p8", xs_a))
    if n_run >= 10:
        with (
            nc.sbuf_tensor("t_bfb", [128, S], BF16) as t_bf,
            nc.sbuf_tensor("routing_vecb", [128, 1], F32) as routing_vec,
        ):
            make_f1("p9", xs_a, t_bf, routing_vec)()
            if n_run >= 11:
                with nc.sbuf_tensor("bt_sbb", [128, D], BF16) as bt_sb:
                    emit_epilogue("p10", wt_scr, True, xs_a, out,
                                  bias_src=fb, lora=(t_bf, bt_sb),
                                  pre=make_f2_pre(bt_sb, "b"))

    if dev_dump == "victim":
        vbig = nc.dram_tensor("vbig", [128, 32768], F32, kind="ExternalInput").ap()
        d_v = nc.dram_tensor("d_v", [128, 2048], F32, kind="ExternalOutput").ap()
        phv = Ph(k, "victim")
        with ExitStack() as es:
            vb = es.enter_context(nc.sbuf_tensor("v_b", [128, 32768], F32))
            vo = es.enter_context(nc.sbuf_tensor("v_o", [128, 2048], F32))
            slack = 0
            phv.do("sync", lambda e: e.dma_start(out=vb[:], in_=vbig), inc=[("Q", 16)])
            if slack:
                phv.do("sync", lambda e: e.dma_start(out=vo[:], in_=vbig[:, 0:2048]),
                       inc=[("Q", 16)])
            wv = phv.w("Q")
            phv.do("vector", lambda e: e.tensor_copy(vo[:], vb[:, 30720:32768]),
                   wait=wv, inc=[("V", 1)])
            phv.do("gpsimd", lambda e: e.dma_start(out=d_v, in_=vo[:]),
                   wait=phv.w("V"), inc=[("G", 16)])
            phv.emit()

    if dev_dump == "tls":
        ph = Ph(k, "dump_tls")
        dt_ = nc.dram_tensor("dbg_tls", [NT, 128, D], BF16, kind="ExternalOutput").ap()
        for tt in range(NT):
            ph.do("gpsimd", lambda e, tt=tt: e.dma_start(out=dt_[tt], in_=tls[tt]),
                  wait=ph.w("G", "Sc", "V", "T"), inc=[("G", 16)])
        ph.emit()

    ph = Ph(k, "fin")
    ph.do("sync", lambda e: e.nop(),
          wait=ph.w("Q", "Adma", "G", "V", "Sc", "T", "Gc"))
    ph.do("gpsimd", lambda e: e.nop(),
          wait=ph.w("Q", "Adma", "G", "V", "Sc", "T", "Gc"))
    ph.emit()
    return nc


def _consts():
    t = np.arange(S, dtype=np.float64)
    inv = 1.0 / (10000.0 ** (np.arange(0, RD, 2, dtype=np.float64) / RD))
    f = np.concatenate([inv, inv])
    ang = np.outer(f, t)
    cos64 = np.cos(ang)
    sin64 = np.sin(ang)
    sinm = np.concatenate([-sin64[0:32], sin64[32:64]], axis=0)
    cos128 = np.concatenate([cos64, cos64], axis=0).astype(np.float32)
    sinm128 = np.concatenate([sinm, sinm], axis=0).astype(np.float32)
    e8 = np.zeros((8, 128), dtype=np.float32)
    for e in range(8):
        e8[e, e * 16:(e + 1) * 16] = 1.0
    return {
        "c_id_bf": np.eye(128, dtype=np.float32).astype(ml_dtypes.bfloat16),
        "c_id_f32": np.eye(128, dtype=np.float32),
        "c_cos": cos128,
        "c_sinm": sinm128,
        "c_ones_bf": np.ones((1, 128), dtype=np.float32).astype(ml_dtypes.bfloat16),
        "c_e8": e8.astype(ml_dtypes.bfloat16),
    }


def host_inputs(inputs):
    consts = _consts()
    common = dict(
        wqa=inputs["L0_attn_a_w_qa"], gqa=inputs["L0_attn_a_g_qa"],
        wqb=inputs["L0_attn_a_w_qb"], wkva=inputs["L0_attn_a_w_kva"],
        gkva=inputs["L0_attn_a_g_kva"], wkvb=inputs["L0_attn_a_w_kvb"],
        wo_a=inputs["L0_attn_a_w_o"],
        bWqd=inputs["L0_attn_b_Wq_d"], bbqd=inputs["L0_attn_b_bq_d"],
        bWqk=inputs["L0_attn_b_W_qk"], bbqk=inputs["L0_attn_b_b_qk"],
        bWkvd=inputs["L0_attn_b_Wkv_d"], bbkvd=inputs["L0_attn_b_bkv_d"],
        bWvu=inputs["L0_attn_b_Wv_u"], bbvu=inputs["L0_attn_b_bv_u"],
        bWo=inputs["L0_attn_b_Wo"], bbo=inputs["L0_attn_b_bo"],
        fW=inputs["L0_ffn_W"], fb=inputs["L0_ffn_b"],
        fA=np.ascontiguousarray(inputs["L0_ffn_A"], dtype=np.float32).reshape(128, D),
        fB=inputs["L0_ffn_B"], frW=inputs["L0_ffn_router_W"],
        frb=inputs["L0_ffn_router_b"],
        **consts,
    )
    return {kk: np.ascontiguousarray(v) for kk, v in common.items()}


def kernel(**inputs):
    x = np.ascontiguousarray(inputs["L0_attn_a_hidden_states"], dtype=np.float32)
    common = host_inputs(inputs)
    nc = build_nc(PHASE_LIMIT, DEV_DUMP)
    in_maps = [dict(common, x=np.ascontiguousarray(x[b])) for b in range(8)]
    res = run_bass_kernel_spmd(nc, in_maps, core_ids=list(range(8)))
    return np.stack([res.results[b]["out"] for b in range(8)], axis=0)


# version 20
# speedup vs baseline: 1.1438x; 1.1438x over previous
import numpy as np
import ml_dtypes
from contextlib import ExitStack
import concourse.bass as bass
import concourse.mybir as mybir
from concourse.bass_utils import run_bass_kernel_spmd

F32 = mybir.dt.float32
BF16 = mybir.dt.bfloat16
AF = mybir.ActivationFunctionType
AX = mybir.AxisListType
OP = mybir.AluOpType

S = 2048
D = 4096
NT = 16
DCH = 32
NH = 32
QR = 1536
KVR = 512
RD = 64
QK_SCALE = float(192.0 ** -0.5)
LN_EPS = 1e-5
RMS_EPS = 1e-6
Q_LO, Q_HI = 611, 2047
ROUTER_CNT = float(Q_HI - Q_LO)

ENGINES = ("sync", "scalar", "vector", "tensor", "gpsimd")
DMA_QUEUES = ("Q", "Adma", "G")
DMA_POOL = 8

PHASE_LIMIT = None
DEV_P5_STRIP = 0
DEV_DUMP = None


class Ctx:
    def __init__(self, nc):
        self.nc = nc
        self.sems = {}
        self.cnt = {}
        self.own = {}

    def sem(self, name, owner=None):
        h = self.nc.semaphore(name).__enter__()
        self.sems[name] = h
        self.cnt[name] = 0
        if owner:
            self.own[name] = owner


class Ph:

    def __init__(self, k, name, barrier=True):
        self.k = k
        self.name = name
        self.q = {e: [] for e in ENGINES}
        self.last_wait = {e: {} for e in ENGINES}
        self.bar = {s: k.cnt[s] for s in k.cnt} if barrier else None

    def do(self, eng, fn, wait=(), inc=()):
        k = self.k
        wait = list(wait)
        inc2 = []
        for s, n in inc:
            if s in DMA_QUEUES:
                j = k.rr[s]
                k.rr[s] = (j + 1) % DMA_POOL
                ps = f"{s}{j}"
                wait.append((ps, k.cnt[ps]))
                inc2.append((ps, n))
            else:
                inc2.append((s, n))
        inc = inc2
        w = []
        for s, v in wait:
            assert v <= k.cnt[s], f"{self.name}: wait {s}>={v} > planned {k.cnt[s]}"
            if self.last_wait[eng].get(s, -1) >= v:
                continue
            self.last_wait[eng][s] = v
            w.append((s, v))
        ii = []
        tok = None
        for s, n in inc:
            k.cnt[s] += n
            ii.append((s, n))
            if tok is None:
                tok = (s, k.cnt[s])
        self.q[eng].append((fn, w, ii))
        return tok

    def w(self, *sems):
        out = []
        for s in sems:
            if s in DMA_QUEUES:
                for j in range(DMA_POOL):
                    ps = f"{s}{j}"
                    out.append((ps, self.k.cnt[ps]))
            else:
                out.append((s, self.k.cnt[s]))
        return out

    def emit(self):
        k = self.k
        with k.nc.Block(self.name) as blk:
            for e in ENGINES:
                steps = self.q[e]
                if not steps:
                    continue

                def run(eng, steps=steps, e_name=e):
                    if self.bar is not None:
                        for s, v in self.bar.items():
                            if v > 0 and self.last_wait[e_name].get(s, -1) < v:
                                self.last_wait[e_name][s] = v
                                eng.wait_ge(k.sems[s], v)
                    for fn, waits, incs in steps:
                        for s, v in waits:
                            eng.wait_ge(k.sems[s], v)
                        ins = fn(eng)
                        for s, n in incs:
                            ins.then_inc(k.sems[s], n)
                getattr(blk, e)(run)


def mm_cols(n):
    out, c = [], 0
    while c < n:
        w = min(512, n - c)
        out.append((c, w))
        c += w
    return out


def build_nc(n_phases=None, dev_dump=None):
    nc = bass.Bass("TRN2", target_bir_lowering=False, debug=False)
    k = Ctx(nc)

    def din(name, shape, dt=F32):
        return nc.dram_tensor(name, shape, dt, kind="ExternalInput").ap()

    x0 = din("x", [S, D])
    wqa = din("wqa", [D, QR])
    gqa = din("gqa", [QR])
    wqb = din("wqb", [QR, NH * 192])
    wkva = din("wkva", [D, KVR + RD])
    gkva = din("gkva", [KVR])
    wkvb = din("wkvb", [KVR, NH * 256])
    wo_a = din("wo_a", [D, D])
    bWqd = din("bWqd", [D, 12])
    bbqd = din("bbqd", [12])
    bWqk = din("bWqk", [12, 128])
    bbqk = din("bbqk", [128])
    bWkvd = din("bWkvd", [D, 4])
    bbkvd = din("bbkvd", [4])
    bWvu = din("bWvu", [4, D])
    bbvu = din("bbvu", [D])
    bWo = din("bWo", [D, D])
    bbo = din("bbo", [D])
    fW = din("fW", [D, D])
    fb = din("fb", [D])
    fA = din("fA", [128, D])
    fB = din("fB", [8, D, 16])
    frW = din("frW", [8, D])
    frb = din("frb", [8])
    c_id_bf = din("c_id_bf", [128, 128], BF16)
    c_id_f32 = din("c_id_f32", [128, 128], F32)
    c_cos = din("c_cos", [128, S])
    c_sinm = din("c_sinm", [128, S])
    c_ones_bf = din("c_ones_bf", [1, 128], BF16)
    c_e8 = din("c_e8", [8, 128], BF16)

    out = nc.dram_tensor("out", [S, D], F32, kind="ExternalOutput").ap()

    xs_a = nc.dram_tensor("xs_a", [S, D], F32, kind="Internal").ap()
    xs_b = nc.dram_tensor("xs_b", [S, D], F32, kind="Internal").ap()
    tls = nc.dram_tensor("tls", [NT, 128, D], BF16, kind="Internal").ap()
    wt_scr = nc.dram_tensor("wt_scr", [D, D], BF16, kind="Internal").ap()

    for q in DMA_QUEUES:
        for j in range(DMA_POOL):
            k.sem(f"{q}{j}")
    for s_name, owner in [("V", "vector"), ("Sc", "scalar"), ("T", "tensor"),
                          ("Gc", "gpsimd")]:
        k.sem(s_name, owner)
    k.rr = {q: 0 for q in DMA_QUEUES}
    nc.clear_and_free_semaphores(list(k.sems.values()))
    nc.all_engine_barrier()

    id_bf = nc.alloc_sbuf_tensor("id_bf", [128, 128], BF16)
    id_f32 = nc.alloc_sbuf_tensor("id_f32", [128, 128], F32)
    ones_bf = nc.alloc_sbuf_tensor("ones_bf", [1, 128], BF16)
    rcpT_a = nc.dram_tensor("rcp_scr", [32, S], F32, kind="Internal").ap()

    psA = nc.alloc_psum_tensor("psA", [128, 1024], F32)
    psB = nc.alloc_psum_tensor("psB", [128, 1024], F32)
    psC = nc.alloc_psum_tensor("psC", [128, 1024], F32)
    ptA = nc.alloc_psum_tensor("ptA", [128, 1024], BF16)
    ptB = nc.alloc_psum_tensor("ptB", [128, 1024], BF16)

    def tr16(ph, src_bf, n, dst_fn, wait_extra=()):
        for rnd in range((n + 7) // 8):
            pt = ptA if rnd % 2 == 0 else ptB
            lo, hi = rnd * 8, min(rnd * 8 + 8, n)

            def tr(e, lo=lo, hi=hi, pt=pt):
                for j in range(hi - lo):
                    c = lo + j
                    ins = nc.tensor.transpose(pt[:, j * 128:(j + 1) * 128],
                                              src_bf[:, c * 128:(c + 1) * 128], id_bf[:])
                return ins
            ph.do("tensor", tr, wait=ph.w("V", "Sc") + list(wait_extra), inc=[("T", 1)])
            ph.do("scalar", lambda e, rnd=rnd, pt=pt, lo=lo, hi=hi: dst_fn(e, rnd, pt, lo, hi),
                  wait=ph.w("T"), inc=[("Sc", 1)])

    def new_stat(prefix, es):
        names = ["ssq", "sm", "mean", "var", "var2", "std", "rstd", "nmr"]
        return {n: es.enter_context(nc.sbuf_tensor(f"{prefix}_{n}", [128, 1], F32))
                for n in names}

    def emit_ln_tile(ph, x_dram, tt, xt_f32, junk_bf, stat, xln_bf, xlt_dst):
        ph.do("sync", lambda e: e.dma_start(
            out=xt_f32, in_=x_dram[tt * 128:(tt + 1) * 128, :]),
            wait=ph.w("V", "Sc"), inc=[("Q", 16)])
        ph.do("scalar", lambda e: e.activation(junk_bf[:], xt_f32, AF.Square,
                                               accum_out=stat["ssq"][:]),
              wait=ph.w("Q", "V", "T"), inc=[("Sc", 1)])
        ph.do("vector", lambda e: e.reduce_sum(stat["sm"][:], xt_f32, axis=AX.X),
              wait=ph.w("Q", "Sc"), inc=[("V", 1)])

        def s1(e):
            e.tensor_scalar_mul(stat["mean"][:], stat["sm"][:], 1.0 / D)
            return e.tensor_scalar(stat["var2"][:], stat["ssq"][:], 1.0 / D, None,
                                   op0=OP.mult)
        ph.do("vector", s1, wait=ph.w("Sc", "V"), inc=[("V", 1)])
        ph.do("vector", lambda e: e.tensor_mul(stat["var"][:], stat["mean"][:],
                                               stat["mean"][:]),
              wait=ph.w("V"), inc=[("V", 1)])
        ph.do("vector", lambda e: e.tensor_sub(stat["var"][:], stat["var2"][:],
                                               stat["var"][:]),
              wait=ph.w("V"), inc=[("V", 1)])
        ph.do("vector", lambda e: e.tensor_scalar_add(stat["var"][:], stat["var"][:],
                                                      LN_EPS),
              wait=ph.w("V"), inc=[("V", 1)])
        ph.do("scalar", lambda e: e.activation(stat["std"][:], stat["var"][:], AF.Sqrt),
              wait=ph.w("V"), inc=[("Sc", 1)])
        ph.do("vector", lambda e: e.reciprocal(stat["rstd"][:], stat["std"][:]),
              wait=ph.w("Sc", "V"), inc=[("V", 1)])
        ph.do("vector", lambda e: e.tensor_mul(stat["nmr"][:], stat["mean"][:],
                                               stat["rstd"][:]),
              wait=ph.w("V"), inc=[("V", 1)])
        ph.do("vector", lambda e: e.tensor_scalar_mul(stat["nmr"][:], stat["nmr"][:],
                                                      -1.0),
              wait=ph.w("V"), inc=[("V", 1)])
        ph.do("vector", lambda e: e.tensor_scalar(xln_bf[:], xt_f32, stat["rstd"][:],
                                                  stat["nmr"][:], op0=OP.mult,
                                                  op1=OP.add),
              wait=ph.w("V", "T"), inc=[("V", 1)])

        def mk_dst(e, rnd, pt, lo, hi):
            return e.copy(xlt_dst[:, lo * 128:hi * 128], pt[:, 0:(hi - lo) * 128])
        tr16(ph, xln_bf, 32, mk_dst)

    def p0():
        ph = Ph(k, "p0")
        ph.do("sync", lambda e: e.dma_start(out=id_bf[:], in_=c_id_bf), inc=[("Q", 16)])
        ph.do("sync", lambda e: e.dma_start(out=id_f32[:], in_=c_id_f32), inc=[("Q", 16)])
        ph.do("sync", lambda e: e.dma_start(out=ones_bf[:], in_=c_ones_bf), inc=[("Q", 16)])
        ph.emit()

    def wt_gen(ph, es, pt_free):
        wlf = lambda *t: [x for x in t if x]
        wrow_f32 = es.enter_context(nc.sbuf_tensor("p0_wrow", [128, 2 * D], F32))
        wrow_bf = es.enter_context(nc.sbuf_tensor("p0_wrowb", [128, 2 * D], BF16))
        wt_stage = es.enter_context(nc.sbuf_tensor("p0_wts", [128, 2 * D], BF16))
        t_conv = {}; t_tr = {}; t_ev = {}; t_st = {}
        for rb in range(DCH):
            sl = (rb % 2) * D
            tl_ = ph.do("sync", lambda e, rb=rb, sl=sl: e.dma_start(
                out=wrow_f32[:, sl:sl + D], in_=fW[rb * 128:(rb + 1) * 128, :]),
                wait=wlf(t_conv.get(rb - 2)), inc=[("Q", 16)])
            yield
            t_conv[rb] = ph.do("vector", lambda e, sl=sl: e.tensor_copy(
                wrow_bf[:, sl:sl + D], wrow_f32[:, sl:sl + D]),
                wait=wlf(tl_, t_tr.get((rb - 2, 3))), inc=[("V", 1)])
            yield
            for rnd in range(4):
                ptk = rnd % 2
                pt = ptA if ptk == 0 else ptB

                def tr(e, rnd=rnd, pt=pt, sl=sl):
                    for j in range(8):
                        c = rnd * 8 + j
                        ins = nc.tensor.transpose(
                            pt[:, j * 128:(j + 1) * 128],
                            wrow_bf[:, sl + c * 128: sl + (c + 1) * 128],
                            id_bf[:])
                    return ins
                t_tr[(rb, rnd)] = ph.do("tensor", tr, wait=wlf(
                    t_conv[rb], pt_free.get(ptk)), inc=[("T", 1)])
                t_ev[(rb, rnd)] = ph.do(
                    "scalar", lambda e, rnd=rnd, pt=pt, sl=sl:
                    e.copy(wt_stage[:, sl + rnd * 1024: sl + (rnd + 1) * 1024],
                           pt[:]),
                    wait=wlf(t_tr[(rb, rnd)]) + list(t_st.get(rb - 2, [])),
                    inc=[("Sc", 1)])
                pt_free[ptk] = t_ev[(rb, rnd)]
                yield
            dst = wt_scr[:, rb * 128:(rb + 1) * 128].rearrange(
                "(c p) j -> p c j", p=128)
            src3 = wt_stage[:, sl:sl + D].rearrange("p (c j) -> p c j", c=32)
            sts = []
            for q4 in range(4):
                sts.append(ph.do("gpsimd", lambda e, dst=dst, src3=src3, q4=q4:
                                 e.dma_start(out=dst[:, q4 * 8:(q4 + 1) * 8, :],
                                             in_=src3[:, q4 * 8:(q4 + 1) * 8, :]),
                                 wait=[t_ev[(rb, q4)]], inc=[("G", 16)]))
            t_st[rb] = sts
            yield

    def ln_tile_tok(ph, x_dram, tt, xt, stat, xln, dst_fn, pt_free, t_prev):
        wlf = lambda *t: [x for x in t if x]
        tload = ph.do("sync", lambda e: e.dma_start(
            out=xt, in_=x_dram[tt * 128:(tt + 1) * 128, :]),
            wait=wlf(t_prev.get("xt")), inc=[("Adma", 16)])
        tsq = ph.do("scalar", lambda e: e.activation(
            xln[:], xt, AF.Square, accum_out=stat["ssq"][:]),
            wait=wlf(tload, t_prev.get("xln"), t_prev.get("stat")),
            inc=[("Sc", 1)])
        tsum = ph.do("vector", lambda e: e.reduce_sum(stat["sm"][:], xt, axis=AX.X),
                     wait=wlf(tload, t_prev.get("stat")), inc=[("V", 1)])

        def s1(e):
            e.tensor_scalar_mul(stat["mean"][:], stat["sm"][:], 1.0 / D)
            return e.tensor_scalar(stat["var2"][:], stat["ssq"][:], 1.0 / D, None,
                                   op0=OP.mult)
        ts1 = ph.do("vector", s1, wait=wlf(tsum, tsq), inc=[("V", 1)])
        tva = ph.do("vector", lambda e: e.scalar_tensor_tensor(
            stat["var"][:], stat["mean"][:], stat["mean"][:], stat["var2"][:],
            op0=OP.mult, op1=OP.subtract),
            wait=[ts1], inc=[("V", 1)])
        tvb = ph.do("vector", lambda e: e.tensor_scalar(
            stat["var"][:], stat["var"][:], -1.0, LN_EPS, op0=OP.mult, op1=OP.add),
            wait=[tva], inc=[("V", 1)])
        trs = ph.do("scalar", lambda e: e.activation(
            stat["std"][:], stat["var"][:], AF.Sqrt),
            wait=[tvb], inc=[("Sc", 1)])
        trc = ph.do("vector", lambda e: e.reciprocal(
            stat["rstd"][:], stat["std"][:]),
            wait=[trs], inc=[("V", 1)])
        tnm = ph.do("vector", lambda e: e.tensor_scalar(
            stat["nmr"][:], stat["mean"][:], stat["rstd"][:], -1.0,
            op0=OP.mult, op1=OP.mult),
            wait=[trc], inc=[("V", 1)])
        txl = ph.do("vector", lambda e: e.tensor_scalar(
            xln[:], xt, stat["rstd"][:], stat["nmr"][:], op0=OP.mult, op1=OP.add),
            wait=wlf(tnm, tsq), inc=[("V", 1)])
        trlast, evlast = None, None
        for rnd in range(4):
            ptk = rnd % 2
            pt = ptA if ptk == 0 else ptB

            def tr(e, rnd=rnd, pt=pt):
                for j in range(8):
                    c = rnd * 8 + j
                    ins = nc.tensor.transpose(pt[:, j * 128:(j + 1) * 128],
                                              xln[:, c * 128:(c + 1) * 128], id_bf[:])
                return ins
            trlast = ph.do("tensor", tr,
                           wait=wlf(txl, pt_free.get(ptk)), inc=[("T", 1)])
            dfree = t_prev.get("dst") if rnd == 0 else None
            dfree = dfree if isinstance(dfree, list) else ([dfree] if dfree else [])
            evlast = ph.do("scalar", lambda e, rnd=rnd, pt=pt: dst_fn(e, rnd, pt),
                           wait=wlf(trlast) + dfree,
                           inc=[("Sc", 1)])
            pt_free[ptk] = evlast
        return {"load": tload, "xln": trlast, "sq": txl, "evlast": evlast}

    def p1(q_cF, k_cF, krF_bf):
        ph = Ph(k, "p1")
        es_out = ExitStack()
        with es_out:
            krF_f32 = es_out.enter_context(nc.sbuf_tensor("p1_krf", [64, S], BF16))
            gqa_bc = es_out.enter_context(nc.sbuf_tensor("p1_gqa", [128, QR], BF16))
            gkva_bc = es_out.enter_context(nc.sbuf_tensor("p1_gkva", [128, KVR], BF16))
            pha = Ph(k, "p1a")
            with ExitStack() as es:
                grow = es.enter_context(nc.sbuf_tensor("p1_grow", [1, QR], F32))
                grow_bf = es.enter_context(nc.sbuf_tensor("p1_growb", [1, QR], BF16))

                def bcast_gain(vecap, n, dst):
                    pha.do("sync", lambda e: e.dma_start(out=grow[0:1, 0:n],
                                                         in_=vecap[None, :]),
                           wait=pha.w("V"), inc=[("Q", 16)])
                    pha.do("vector", lambda e: e.tensor_copy(grow_bf[0:1, 0:n],
                                                             grow[0:1, 0:n]),
                           wait=pha.w("Q"), inc=[("V", 1)])
                    for c0, w in mm_cols(n):
                        pha.do("tensor", lambda e, c0=c0, w=w: nc.tensor.matmul(
                            psA[:, 0:w], ones_bf[0:1, 0:128], grow_bf[0:1, c0:c0 + w],
                            start=True, stop=True), wait=pha.w("V", "Sc"), inc=[("T", 1)])
                        pha.do("scalar", lambda e, c0=c0, w=w, dst=dst: e.copy(
                            dst[:, c0:c0 + w], psA[:, 0:w]), wait=pha.w("T"),
                            inc=[("Sc", 1)])
                bcast_gain(gqa, QR, gqa_bc)
                bcast_gain(gkva, KVR, gkva_bc)
                pha.emit()
            p1_main(ph, q_cF, k_cF, krF_f32, gqa_bc, gkva_bc)
            ph2 = Ph(k, "p1b")
            with ExitStack() as es:
                c_cs = es.enter_context(nc.sbuf_tensor("p1_cs", [64, 2 * S], F32))
                kr_t1 = es.enter_context(nc.sbuf_tensor("p1_kt1", [64, S], F32))
                kr_t2 = es.enter_context(nc.sbuf_tensor("p1_kt2", [64, S], F32))
                kr_sw = es.enter_context(nc.sbuf_tensor("p1_ksw", [64, S], F32))
                ph2.do("sync", lambda e: e.dma_start(out=c_cs[:, 0:S], in_=c_cos[0:64, :]),
                       inc=[("Q", 16)])
                ph2.do("sync", lambda e: e.dma_start(out=c_cs[:, S:2 * S],
                                                     in_=c_sinm[0:64, :]),
                       inc=[("Q", 16)])

                def kr1(e):
                    e.tensor_copy(kr_sw[0:32, :], krF_f32[32:64, :])
                    e.tensor_copy(kr_sw[32:64, :], krF_f32[0:32, :])
                    return e.tensor_copy(kr_t1[:], krF_f32[:])
                ph2.do("vector", kr1, wait=ph2.w("Q", "Sc", "V"), inc=[("V", 1)])

                def kr2(e):
                    e.tensor_mul(kr_t1[:], kr_t1[:], c_cs[:, 0:S])
                    return e.tensor_mul(kr_t2[:], kr_sw[:], c_cs[:, S:2 * S])
                ph2.do("vector", kr2, wait=ph2.w("V"), inc=[("V", 1)])
                ph2.do("vector", lambda e: e.tensor_add(krF_bf[0:64, :], kr_t1[:],
                                                        kr_t2[:]),
                       wait=ph2.w("V"), inc=[("V", 1)])
                ph2.do("vector", lambda e: e.tensor_copy(krF_bf[64:128, :],
                                                         krF_bf[0:64, :]),
                       wait=ph2.w("V"), inc=[("V", 1)])
                if dev_dump == "p1":
                    dq = nc.dram_tensor("dbg_qcf", [128, 12 * S], BF16,
                                        kind="ExternalOutput").ap()
                    dk = nc.dram_tensor("dbg_kcf", [128, 4 * S], BF16,
                                        kind="ExternalOutput").ap()
                    dr = nc.dram_tensor("dbg_krf", [64, S], BF16,
                                        kind="ExternalOutput").ap()
                    ph2.do("gpsimd", lambda e: e.dma_start(out=dq, in_=q_cF[:]),
                           wait=ph2.w("Sc", "V"), inc=[("G", 16)])
                    ph2.do("gpsimd", lambda e: e.dma_start(out=dk, in_=k_cF[:]),
                           wait=ph2.w("Sc", "V"), inc=[("G", 16)])
                    ph2.do("gpsimd", lambda e: e.dma_start(out=dr, in_=krF_bf[0:64, :]),
                           wait=ph2.w("Sc", "V"), inc=[("G", 16)])
                ph2.emit()

    def p1_main(ph, q_cF, k_cF, krF_f32, gqa_bc, gkva_bc):
        wlf = lambda *t: [x for x in t if x]
        with ExitStack() as es:
            pt_free = {}
            xlh = es.enter_context(nc.sbuf_tensor("p1_xlh", [128, 4 * D], BF16))
            xt_f32 = es.enter_context(nc.sbuf_tensor("p1_xt", [128, 2 * D], F32))
            xln_bf = es.enter_context(nc.sbuf_tensor("p1_xln", [128, 2 * D], BF16))
            wpc_f32 = es.enter_context(nc.sbuf_tensor("p1_wpf", [128, 4 * 384], F32))
            wpc_bf = es.enter_context(nc.sbuf_tensor("p1_wpb", [128, 32 * 384], BF16))
            qstage = es.enter_context(nc.sbuf_tensor("p1_qst", [128, 4 * QR], BF16))
            q_ssq = es.enter_context(nc.sbuf_tensor("p1_qssq", [128, 4], F32))
            sspart = es.enter_context(nc.sbuf_tensor("p1_ssp", [128, 2], F32))
            rms_r = es.enter_context(nc.sbuf_tensor("p1_rr", [128, 8], F32))
            kc_f32 = es.enter_context(nc.sbuf_tensor("p1_kc", [128, 4 * KVR], BF16))
            kc_bf = es.enter_context(nc.sbuf_tensor("p1_kcb", [128, KVR], BF16))
            krtmp = es.enter_context(nc.sbuf_tensor("p1_krt", [128, RD], F32))
            qn_bf = es.enter_context(nc.sbuf_tensor("p1_qnb", [128, QR], BF16))
            stats = [new_stat("p1s%d" % i, es) for i in range(2)]

            ln_prev = [{}, {}]
            dst_free = {}
            mm_tok = {}
            ev_tok = {}
            stat_free = {}
            pieces = [("qa", 0, 384), ("qa", 384, 384), ("qa", 768, 384),
                      ("qa", 1152, 384), ("kv", 0, 288), ("kv", 288, 288)]
            NP = len(pieces)
            ln_done = {}
            kr_done = []
            cv_prev = {}

            def emit_ln(quarter):
                for tl, tt in enumerate(range(4 * quarter, 4 * quarter + 4)):
                    par = tl % 2
                    prev = dict(ln_prev[par])
                    prev["dst"] = dst_free.get(tl)
                    prev["stat"] = stat_free.get(par)
                    prev["xln"] = ev_tok.get(("xln", par))

                    def mk_dst(e, rnd, pt, tl=tl):
                        return e.copy(xlh[:, tl * D + rnd * 1024:
                                          tl * D + (rnd + 1) * 1024], pt[:])
                    r = ln_tile_tok(ph, x0, tt, xt_f32[:, par * D:(par + 1) * D],
                                    stats[par], xln_bf[:, par * D:(par + 1) * D],
                                    mk_dst, pt_free, prev)
                    ln_prev[par] = {"xt": r["sq"]}
                    ev_tok[("xln", par)] = r["xln"]
                    stat_free[par] = r["sq"]
                    ln_done[tl] = r["evlast"]

            emit_ln(0)
            for quarter in range(4):
                for pi, (pname, c0, pw) in enumerate(pieces):
                    wsrc = wqa if pname == "qa" else wkva
                    qname = "Q" if pname == "qa" else "G"
                    prev_pi = pi - 1 if pi > 0 else NP - 1
                    prev_pw = pieces[prev_pi][2]
                    cv = {-4: cv_prev.get(28), -3: cv_prev.get(29),
                          -2: cv_prev.get(30), -1: cv_prev.get(31)}
                    for dc in range(DCH):
                        b0 = (dc % 4) * 384
                        if qname == "Q":
                            eng, qq = (("sync", "Q") if dc % 2 == 0
                                       else ("scalar", "Adma"))
                        else:
                            eng, qq = "gpsimd", "G"
                        tl_ = ph.do(eng,
                                    lambda e, dc=dc, b0=b0, wsrc=wsrc, c0=c0, pw=pw:
                                    e.dma_start(
                                        out=wpc_f32[:, b0:b0 + pw],
                                        in_=wsrc[dc * 128:(dc + 1) * 128,
                                                 c0:c0 + pw]),
                                    wait=wlf(cv.get(dc - 4)), inc=[(qq, 16)])
                        g_req = (dc + 1) * pw + 4 * prev_pw - 1
                        g_req = min(7, max(0, g_req // (4 * prev_pw) - 1))
                        wfree = mm_tok.get((prev_pi, 3))
                        wfree = wfree[g_req] if wfree else None
                        cv[dc] = ph.do("vector",
                                       lambda e, dc=dc, b0=b0, pw=pw: e.tensor_copy(
                                           wpc_bf[:, dc * pw:dc * pw + pw],
                                           wpc_f32[:, b0:b0 + pw]),
                                       wait=wlf(tl_, wfree), inc=[("V", 1)])
                    for tl in range(4):
                        ps = psA if tl % 2 == 0 else psB

                        def mmg(e, tl, pw, g8):
                            pps = psA if tl % 2 == 0 else psB
                            for dc in range(g8 * 4, g8 * 4 + 4):
                                lhs = xlh[:, tl * D + dc * 128:
                                          tl * D + (dc + 1) * 128]
                                ins = nc.tensor.matmul(
                                    pps[:, 0:pw], lhs,
                                    wpc_bf[:, dc * pw:(dc + 1) * pw],
                                    start=(dc == 0), stop=(dc == DCH - 1))
                            return ins
                        subs = []
                        for g8 in range(8):
                            w = wlf(cv[g8 * 4 + 3])
                            if g8 == 0:
                                w += wlf(ln_done.get(tl),
                                         ev_tok.get(("psum", tl % 2)))
                            subs.append(ph.do(
                                "tensor",
                                lambda e, tl=tl, pw=pw, g8=g8: mmg(e, tl, pw, g8),
                                wait=w, inc=[("T", 1)]))
                        tB = subs[-1]
                        mm_tok[(pi, tl)] = subs
                        if tl == 0:
                            cv_prev = cv
                        tt = 4 * quarter + tl
                        if pname == "qa":
                            ppi = c0 // 384
                            tsq = ph.do("scalar", lambda e, ps=ps, pw=pw, tl=tl:
                                        e.activation(
                                            qn_bf[:, 0:pw], ps[:, 0:pw],
                                            AF.Square,
                                            accum_out=sspart[:, tl % 2:tl % 2 + 1]),
                                        wait=wlf(tB, ev_tok.get(("sq", tl % 2)),
                                                 ev_tok.get(("qn",))),
                                        inc=[("Sc", 1)])

                            def qacc(e, tl=tl, ppi=ppi, c0=c0, pw=pw, ps=ps):
                                if ppi == 0:
                                    e.tensor_copy(q_ssq[:, tl:tl + 1],
                                                  sspart[:, tl % 2:tl % 2 + 1])
                                else:
                                    e.tensor_add(q_ssq[:, tl:tl + 1],
                                                 q_ssq[:, tl:tl + 1],
                                                 sspart[:, tl % 2:tl % 2 + 1])
                                return e.tensor_copy(
                                    qstage[:, tl * QR + c0: tl * QR + c0 + pw],
                                    ps[:, 0:pw])
                            tq = ph.do("vector", qacc,
                                       wait=wlf(tsq, ev_tok.get(("qa", tl))),
                                       inc=[("V", 1)])
                            ev_tok[("psum", tl % 2)] = tq
                            ev_tok[("sq", tl % 2)] = tq
                            ev_tok[("qa", tl)] = tq
                        elif c0 == 0:
                            te = ph.do("scalar", lambda e, ps=ps, tl=tl: e.copy(
                                kc_f32[:, tl * KVR:tl * KVR + 288], ps[:, 0:288]),
                                wait=wlf(tB, ev_tok.get(("kc", tl))),
                                inc=[("Sc", 1)])
                            ev_tok[("psum", tl % 2)] = te
                            ev_tok[("kv1", tl)] = te
                        else:
                            def kev0(e, ps=ps, tl=tl):
                                e.copy(kc_f32[:, tl * KVR + 288:(tl + 1) * KVR],
                                       ps[:, 0:KVR - 288])
                                return e.copy(krtmp[:], ps[:, KVR - 288:KVR - 288 + RD])
                            te = ph.do("scalar", kev0,
                                       wait=wlf(tB, ev_tok.get(("kc", tl)),
                                                kr_done[-1] if kr_done else None),
                                       inc=[("Sc", 1)])
                            ev_tok[("psum", tl % 2)] = te
                            ttr = ph.do("tensor", lambda e: nc.tensor.transpose(
                                psC[0:64, 0:128], krtmp[:], id_f32[:]),
                                wait=wlf(te, kr_done[-1] if kr_done else None),
                                inc=[("T", 1)])
                            tkr = ph.do("vector", lambda e, tt=tt: e.tensor_copy(
                                krF_f32[:, tt * 128:(tt + 1) * 128],
                                psC[0:64, 0:128]), wait=[ttr], inc=[("V", 1)])
                            kr_done.append(tkr)
                            tsq = ph.do("scalar", lambda e, tl=tl:
                                        e.activation(
                                            qn_bf[:, 0:KVR],
                                            kc_f32[:, tl * KVR:(tl + 1) * KVR],
                                            AF.Square,
                                            accum_out=sspart[:, tl % 2:tl % 2 + 1]),
                                        wait=wlf(te, ev_tok.get(("kv1", tl)),
                                                 ev_tok.get(("sq", tl % 2)),
                                                 ev_tok.get(("qn",))),
                                        inc=[("Sc", 1)])
                            rr = rms_r[:, 4 + tl:5 + tl]
                            t1 = ph.do("vector", lambda e, rr=rr, tl=tl:
                                       e.tensor_scalar_mul(
                                           rr, sspart[:, tl % 2:tl % 2 + 1],
                                           1.0 / KVR),
                                       wait=[tsq], inc=[("V", 1)])
                            ev_tok[("sq", tl % 2)] = t1
                            t2 = ph.do("scalar", lambda e, rr=rr: e.activation(
                                rr, rr, AF.Sqrt), wait=[t1], inc=[("Sc", 1)])
                            t3 = ph.do("vector", lambda e, rr=rr: e.tensor_scalar(
                                rr, rr, 1.0, RMS_EPS, op0=OP.mult, op1=OP.add),
                                wait=[t2], inc=[("V", 1)])
                            t4 = ph.do("vector", lambda e, rr=rr: e.reciprocal(
                                rr, rr), wait=[t3], inc=[("V", 1)])
                            t5 = ph.do("vector", lambda e, rr=rr, tl=tl:
                                       e.scalar_tensor_tensor(
                                           kc_bf[:], kc_f32[:, tl * KVR:(tl + 1) * KVR],
                                           rr, gkva_bc[:], op0=OP.mult, op1=OP.mult),
                                       wait=wlf(t4, ev_tok.get(("kcb",))),
                                       inc=[("V", 1)])
                            ev_tok[("kc", tl)] = t5
                            ptk = tl % 2
                            pt = ptA if ptk == 0 else ptB

                            def ktr(e, pt=pt):
                                for j in range(4):
                                    ins = nc.tensor.transpose(
                                        pt[:, j * 128:(j + 1) * 128],
                                        kc_bf[:, j * 128:(j + 1) * 128], id_bf[:])
                                return ins
                            trl = ph.do("tensor", ktr,
                                        wait=wlf(t5, pt_free.get(ptk)),
                                        inc=[("T", 1)])
                            ev_tok[("kcb",)] = trl

                            def kdst(e, pt=pt, tt=tt):
                                ins = None
                                for j in range(4):
                                    ins = e.copy(
                                        k_cF[:, j * S + tt * 128:
                                             j * S + (tt + 1) * 128],
                                        pt[:, j * 128:(j + 1) * 128])
                                return ins
                            evl = ph.do("scalar", kdst, wait=[trl], inc=[("Sc", 1)])
                            pt_free[ptk] = evl
                    if pname == "qa" and c0 == 1152:
                        prev_trl = ev_tok.get(("qn",))
                        for tl in range(4):
                            rr = rms_r[:, tl:tl + 1]
                            tq0 = ev_tok[("qa", tl)]
                            t1 = ph.do("vector", lambda e, rr=rr, tl=tl:
                                       e.tensor_scalar_mul(rr, q_ssq[:, tl:tl + 1],
                                                           1.0 / QR),
                                       wait=[tq0], inc=[("V", 1)])
                            t2 = ph.do("scalar", lambda e, rr=rr: e.activation(
                                rr, rr, AF.Sqrt), wait=[t1], inc=[("Sc", 1)])
                            t3 = ph.do("vector", lambda e, rr=rr: e.tensor_scalar(
                                rr, rr, 1.0, RMS_EPS, op0=OP.mult, op1=OP.add),
                                wait=[t2], inc=[("V", 1)])
                            t4 = ph.do("vector", lambda e, rr=rr: e.reciprocal(rr, rr),
                                       wait=[t3], inc=[("V", 1)])
                            t5 = ph.do("vector", lambda e, rr=rr: e.tensor_scalar_mul(
                                rr, rr, QK_SCALE), wait=[t4], inc=[("V", 1)])
                            t6 = ph.do("vector", lambda e, rr=rr, tl=tl:
                                       e.scalar_tensor_tensor(
                                           qn_bf[:], qstage[:, tl * QR:(tl + 1) * QR],
                                           rr, gqa_bc[:], op0=OP.mult, op1=OP.mult),
                                       wait=wlf(t5, prev_trl,
                                                ev_tok.get(("sq", 0)),
                                                ev_tok.get(("sq", 1))),
                                       inc=[("V", 1)])
                            tt = 4 * quarter + tl
                            trl = None
                            for rnd in range(2):
                                ptk = rnd % 2
                                pt = ptA if ptk == 0 else ptB

                                def qtr(e, rnd=rnd, pt=pt):
                                    for j in range(6):
                                        c = rnd * 6 + j
                                        ins = nc.tensor.transpose(
                                            pt[:, j * 128:(j + 1) * 128],
                                            qn_bf[:, c * 128:(c + 1) * 128],
                                            id_bf[:])
                                    return ins
                                trl = ph.do("tensor", qtr,
                                            wait=wlf(t6, pt_free.get(ptk)),
                                            inc=[("T", 1)])

                                def qdst(e, rnd=rnd, pt=pt, tt=tt):
                                    ins = None
                                    for j in range(6):
                                        c = rnd * 6 + j
                                        ins = e.copy(
                                            q_cF[:, c * S + tt * 128:
                                                 c * S + (tt + 1) * 128],
                                            pt[:, j * 128:(j + 1) * 128])
                                    return ins
                                evl = ph.do("scalar", qdst, wait=[trl],
                                            inc=[("Sc", 1)])
                                pt_free[ptk] = evl
                            prev_trl = trl
                            ev_tok[("qn",)] = trl
                    if pname == "kv" and c0 == 288:
                        for tl in range(4):
                            dst_free[tl] = mm_tok[(pi, tl)][-1]
                        ln_done = {}
                        if quarter < 3:
                            emit_ln(quarter + 1)
            ph.emit()

    def emit_attention(ph, h, q_chunks, k_chunks, v_tok, attn_bf2, sc_sb2,
                       attnT, ctx_pc, sm, tk, tile_pos=None, strip=None,
                       prep=None, rcp_all=None):
        t_evm = {}; t_comb = {}; t_exp = {}; t_dd = {}; t_recip = {}
        t_tn = {}; t_atr = {}; t_adst = {}; t_avmm = {}; t_cev = {}
        vt_tok = tk.get("vt")
        qk_list = list(tk.get("qk", []))

        def wl(*toks):
            return [t for t in toks if t]

        def stage_A(qt):
            par = qt % 2
            sc_buf = sc_sb2[:, par * 2048:(par + 1) * 2048]
            t_exp.setdefault(qt, {})
            for kh in range(2):
                ps = psA if kh == 0 else psB

                def smm(e, qt=qt, kh=kh, ps=ps):
                    nchunks = len(q_chunks)
                    for ci, ((qsb, qsel), (ksb, ksel)) in enumerate(
                            zip(q_chunks, k_chunks)):
                        if qsel == "strip":
                            lhs = qsb[strip * 32:(strip + 1) * 32,
                                      qt * 128:(qt + 1) * 128]
                        elif qsel is None:
                            lhs = qsb[:, qt * 128:(qt + 1) * 128]
                        else:
                            lhs = qsb[qsel * 64:(qsel + 1) * 64,
                                      qt * 128:(qt + 1) * 128]
                        for j in range(2):
                            c0 = kh * 1024 + j * 512
                            if ksel == "strip":
                                rhs = ksb[strip * 32:(strip + 1) * 32, c0:c0 + 512]
                            elif ksel is None:
                                rhs = ksb[:, c0:c0 + 512]
                            else:
                                rhs = ksb[ksel * 64:(ksel + 1) * 64, c0:c0 + 512]
                            ins = nc.tensor.matmul(
                                ps[:, j * 512:(j + 1) * 512], lhs, rhs,
                                start=(ci == 0), stop=(ci == nchunks - 1),
                                tile_position=tile_pos)
                    return ins
                pk = "evm%d" % kh
                prev = tk.get(pk) if qt == 0 else t_evm[qt - 1][kh]
                w = wl(prev) + (qk_list if qt == 0 else [])
                tT = ph.do("tensor", smm, wait=w, inc=[("T", 1)])

                def evm(e, ps=ps, kh=kh, sc_buf=sc_buf, par=par):
                    return e.tensor_scalar(
                        sc_buf[:, kh * 1024:(kh + 1) * 1024], ps[:], 0.0, None,
                        op0=OP.add, op1=OP.max,
                        accum_out=sm["mx"][:, 2 * par + kh:2 * par + kh + 1])
                scfree = (t_exp[qt - 2][1] if qt >= 2
                          else tk.get("exp_p%d" % par))
                t_evm.setdefault(qt, {})[kh] = ph.do(
                    "vector", evm, wait=wl(tT, scfree), inc=[("V", 1)])
                tk["evm%d" % kh] = t_evm[qt][kh]

            def comb(e, par=par):
                return e.tensor_scalar(
                    sm["nmx"][:, par:par + 1], sm["mx"][:, 2 * par:2 * par + 1],
                    sm["mx"][:, 2 * par + 1:2 * par + 2], -1.0,
                    op0=OP.max, op1=OP.mult)
            scfree = t_exp[qt - 2][1] if qt >= 2 else tk.get("exp_p%d" % par)
            t_comb[qt] = ph.do("vector", comb,
                               wait=wl(t_evm[qt][0], t_evm[qt][1], scfree),
                               inc=[("V", 1)])

        def stage_B(qt):
            par = qt % 2
            sc_buf = sc_sb2[:, par * 2048:(par + 1) * 2048]
            ab = attn_bf2[:, par * 2048:(par + 1) * 2048]
            abfree = t_atr[qt - 2][1] if qt >= 2 else tk.get("atr_p%d" % par)
            ddfree = t_recip[qt - 2] if qt >= 2 else tk.get("dd_p%d" % par)
            for kh in range(2):
                def expo(e, kh=kh, sc_buf=sc_buf, ab=ab, par=par):
                    return e.activation(
                        ab[:, kh * 1024:(kh + 1) * 1024],
                        sc_buf[:, kh * 1024:(kh + 1) * 1024], AF.Exp,
                        bias=sm["nmx"][:, par:par + 1],
                        accum_out=sm["dd"][:, 2 * par + kh:2 * par + kh + 1])
                t_exp[qt][kh] = ph.do("scalar", expo,
                                      wait=wl(t_comb[qt], abfree, ddfree),
                                      inc=[("Sc", 1)])
            t_dd[qt] = ph.do("vector", lambda e, par=par: e.tensor_tensor(
                sm["dd"][:, 2 * par:2 * par + 1], sm["dd"][:, 2 * par:2 * par + 1],
                sm["dd"][:, 2 * par + 1:2 * par + 2], op=OP.add),
                wait=wl(t_exp[qt][0], t_exp[qt][1]), inc=[("V", 1)])
            rdst = (sm["rcp"][:, par:par + 1] if rcp_all is None
                    else rcp_all[:, h * 16 + qt:h * 16 + qt + 1])
            t_recip[qt] = ph.do("vector", lambda e, rdst=rdst, par=par:
                                e.reciprocal(rdst, sm["dd"][:, 2 * par:2 * par + 1]),
                                wait=wl(t_dd[qt]), inc=[("V", 1)])
            if rcp_all is None:
                t_tn[qt] = ph.do("vector", lambda e, ab=ab, par=par:
                                 e.tensor_scalar_mul(ab[:], ab[:],
                                                     sm["rcp"][:, par:par + 1]),
                                 wait=wl(t_recip[qt]), inc=[("V", 1)])

        def stage_C(qt):
            par = qt % 2
            ab = attn_bf2[:, par * 2048:(par + 1) * 2048]
            t_atr.setdefault(qt, {})
            t_adst.setdefault(qt, {})
            for rnd in range(2):
                pt = ptA if rnd == 0 else ptB

                def atr(e, rnd=rnd, pt=pt, ab=ab):
                    for j in range(8):
                        c = rnd * 8 + j
                        ins = nc.tensor.transpose(pt[:, j * 128:(j + 1) * 128],
                                                  ab[:, c * 128:(c + 1) * 128],
                                                  id_bf[:])
                    return ins
                ptfree = t_adst[qt - 1][rnd] if qt >= 1 else tk.get("adst_r%d" % rnd)
                steal = tk.pop("pt_steal_%d" % rnd, None)
                src_ok = t_tn[qt] if rcp_all is None else t_exp[qt][rnd]
                t_atr[qt][rnd] = ph.do(
                    "tensor", atr,
                    wait=wl(src_ok, ptfree, steal), inc=[("T", 1)])

                aeng = "scalar" if (rnd == 0 or rcp_all is None) else "vector"
                asem = "Sc" if aeng == "scalar" else "V"

                def adst(e, rnd=rnd, pt=pt, qt=qt, aeng=aeng):
                    ql = qt % 4
                    dst = attnT.rearrange("p (t c) -> p t c", t=16)[
                        :, rnd * 8:(rnd + 1) * 8, ql * 128:(ql + 1) * 128]
                    srct = pt.rearrange("p (t c) -> p t c", t=8)
                    return (e.copy(dst, srct) if aeng == "scalar"
                            else e.tensor_copy(dst, srct))
                atfree = t_avmm[qt // 4 - 1] if qt >= 4 else tk.get("attnT")
                t_adst[qt][rnd] = ph.do(
                    aeng, adst, wait=wl(t_atr[qt][rnd], atfree),
                    inc=[(asem, 1)])
                tk["adst_r%d" % rnd] = t_adst[qt][rnd]

        def stage_D(qg):
            sl = psC[:, 0:512]

            def avmm(e, sl=sl):
                for kt in range(16):
                    ins = nc.tensor.matmul(
                        sl, v_tok[:, kt * 128:(kt + 1) * 128],
                        attnT[:, kt * 512:(kt + 1) * 512],
                        start=(kt == 0), stop=(kt == 15))
                return ins
            pcfree = t_cev[qg - 1] if qg >= 1 else tk.get("avc")
            t_avmm[qg] = ph.do(
                "tensor", avmm,
                wait=wl(t_adst[qg * 4 + 3][1], vt_tok, pcfree),
                inc=[("T", 1)])

            def cev(e, sl=sl):
                return e.copy(ctx_pc[:], sl)
            cxfree = tk.get("ctx")
            t_cev[qg] = ph.do("scalar", cev, wait=wl(t_avmm[qg], cxfree),
                              inc=[("Sc", 1)])
            tk["avc"] = t_cev[qg]
            dst = tls[qg * 4:(qg + 1) * 4, :, h * 128:(h + 1) * 128].rearrange(
                "t p j -> p t j")
            srcx = ctx_pc.rearrange("p (t j) -> p t j", t=4)
            tk["ctx"] = ph.do("gpsimd", lambda e, dst=dst, srcx=srcx:
                              e.dma_start(out=dst, in_=srcx),
                              wait=[t_cev[qg]], inc=[("G", 16)])

        for i in range(20):
            if i < 16:
                stage_A(i)
            if 4 <= i and (i - 4) % 4 == 3:
                stage_D((i - 4) // 4)
            if 3 <= i < 19:
                stage_C(i - 3)
            if 1 <= i < 17:
                stage_B(i - 1)
            if prep is not None and i >= 1:
                next(prep, None)
        tk["evm0"] = t_evm[15][0]
        tk["evm1"] = t_evm[15][1]
        tk["psA"] = t_evm[15][0]
        tk["psB"] = t_evm[15][1]
        tk["exp_p0"] = t_exp[14][1]
        tk["exp_p1"] = t_exp[15][1]
        tk["atr_p0"] = t_atr[14][1]
        tk["atr_p1"] = t_atr[15][1]
        tk["adst_r0"] = t_adst[15][0]
        tk["adst_r1"] = t_adst[15][1]
        tk["dd_p0"] = t_recip[14]
        tk["dd_p1"] = t_recip[15]
        tk["attnT"] = t_avmm[3]

    def emit_attention_stream(ph, nheads, head_args, attn_bf2, sc_sb2,
                              attnT, ctx_pc, sm, tk, preps, vdw=128,
                              z_sink=None, rcp_all=None):
        wl = lambda *t: [x for x in t if x]
        t_evm = {}; t_comb = {}; t_exp = {}; t_dd = {}; t_recip = {}
        t_tn = {}; t_atr = {}; t_adst = {}; t_avmm = {}; t_cev = {}
        head_state = {}
        NQT = nheads * 16

        def stage_A(qt):
            h = qt // 16
            if qt % 16 == 0:
                head_state[h] = {"qk": list(tk.get("qk", [])),
                                 "vt": tk.get("vt")}
            q_chunks, k_chunks, v_tok, tile_pos, strip = head_args(h)
            par = qt % 2
            ql = qt % 16
            sc_buf = sc_sb2[:, par * 2048:(par + 1) * 2048]
            t_exp.setdefault(qt, {})
            for kh in range(2):
                ps = psA if kh == 0 else psB

                def smm(e, ql=ql, kh=kh, ps=ps, q_chunks=q_chunks,
                        k_chunks=k_chunks, tile_pos=tile_pos, strip=strip):
                    nchunks = len(q_chunks)
                    for ci, ((qsb, qsel), (ksb, ksel)) in enumerate(
                            zip(q_chunks, k_chunks)):
                        if qsel == "strip":
                            lhs = qsb[strip * 32:(strip + 1) * 32,
                                      ql * 128:(ql + 1) * 128]
                        elif qsel is None:
                            lhs = qsb[:, ql * 128:(ql + 1) * 128]
                        else:
                            lhs = qsb[qsel * 64:(qsel + 1) * 64,
                                      ql * 128:(ql + 1) * 128]
                        for j in range(2):
                            c0 = kh * 1024 + j * 512
                            if ksel == "strip":
                                rhs = ksb[strip * 32:(strip + 1) * 32, c0:c0 + 512]
                            elif ksel is None:
                                rhs = ksb[:, c0:c0 + 512]
                            else:
                                rhs = ksb[ksel * 64:(ksel + 1) * 64, c0:c0 + 512]
                            ins = nc.tensor.matmul(
                                ps[:, j * 512:(j + 1) * 512], lhs, rhs,
                                start=(ci == 0), stop=(ci == nchunks - 1),
                                tile_position=tile_pos)
                    return ins
                prev = t_evm[qt - 1][kh] if qt >= 1 else None
                w = wl(prev) + (head_state[h]["qk"] if ql == 0 else [])
                tT = ph.do("tensor", smm, wait=w, inc=[("T", 1)])

                def evm(e, ps=ps, kh=kh, sc_buf=sc_buf, par=par):
                    return e.tensor_scalar(
                        sc_buf[:, kh * 1024:(kh + 1) * 1024], ps[:], 0.0, None,
                        op0=OP.add, op1=OP.max,
                        accum_out=sm["mx"][:, 2 * par + kh:2 * par + kh + 1])
                scfree = t_exp[qt - 2][1] if qt >= 2 else None
                t_evm.setdefault(qt, {})[kh] = ph.do(
                    "vector", evm, wait=wl(tT, scfree), inc=[("V", 1)])
                tk["evm%d" % kh] = t_evm[qt][kh]

            def comb(e, par=par):
                return e.tensor_scalar(
                    sm["nmx"][:, par:par + 1], sm["mx"][:, 2 * par:2 * par + 1],
                    sm["mx"][:, 2 * par + 1:2 * par + 2], -1.0,
                    op0=OP.max, op1=OP.mult)
            scfree = t_exp[qt - 2][1] if qt >= 2 else None
            t_comb[qt] = ph.do("vector", comb,
                               wait=wl(t_evm[qt][0], t_evm[qt][1], scfree),
                               inc=[("V", 1)])

        def stage_B(qt):
            par = qt % 2
            sc_buf = sc_sb2[:, par * 2048:(par + 1) * 2048]
            ab = attn_bf2[:, par * 2048:(par + 1) * 2048]
            abfree = t_atr[qt - 2][1] if qt >= 2 else None
            ddfree = t_recip[qt - 2] if qt >= 2 else None
            for kh in range(2):
                def expo(e, kh=kh, sc_buf=sc_buf, ab=ab, par=par):
                    return e.activation(
                        ab[:, kh * 1024:(kh + 1) * 1024],
                        sc_buf[:, kh * 1024:(kh + 1) * 1024], AF.Exp,
                        bias=sm["nmx"][:, par:par + 1],
                        accum_out=sm["dd"][:, 2 * par + kh:2 * par + kh + 1])
                t_exp[qt][kh] = ph.do("scalar", expo,
                                      wait=wl(t_comb[qt], abfree, ddfree),
                                      inc=[("Sc", 1)])
            t_dd[qt] = ph.do("vector", lambda e, par=par: e.tensor_tensor(
                sm["dd"][:, 2 * par:2 * par + 1], sm["dd"][:, 2 * par:2 * par + 1],
                sm["dd"][:, 2 * par + 1:2 * par + 2], op=OP.add),
                wait=wl(t_exp[qt][0], t_exp[qt][1]), inc=[("V", 1)])
            rdst = (sm["rcp"][:, par:par + 1] if rcp_all is None
                    else rcp_all[:, qt:qt + 1])
            t_recip[qt] = ph.do("vector", lambda e, rdst=rdst, par=par:
                                e.reciprocal(rdst, sm["dd"][:, 2 * par:2 * par + 1]),
                                wait=wl(t_dd[qt]), inc=[("V", 1)])
            if rcp_all is None:
                t_tn[qt] = ph.do("vector", lambda e, ab=ab, par=par:
                                 e.tensor_scalar_mul(ab[:], ab[:],
                                                     sm["rcp"][:, par:par + 1]),
                                 wait=wl(t_recip[qt]), inc=[("V", 1)])

        def stage_C(qt):
            par = qt % 2
            ab = attn_bf2[:, par * 2048:(par + 1) * 2048]
            t_atr.setdefault(qt, {})
            t_adst.setdefault(qt, {})
            for rnd in range(2):
                pt = ptA if rnd == 0 else ptB

                def atr(e, rnd=rnd, pt=pt, ab=ab):
                    for j in range(8):
                        c = rnd * 8 + j
                        ins = nc.tensor.transpose(pt[:, j * 128:(j + 1) * 128],
                                                  ab[:, c * 128:(c + 1) * 128],
                                                  id_bf[:])
                    return ins
                ptfree = t_adst[qt - 1][rnd] if qt >= 1 else None
                steal = tk.pop("pt_steal_%d" % rnd, None)
                src_ok = t_tn[qt] if rcp_all is None else t_exp[qt][rnd]
                t_atr[qt][rnd] = ph.do(
                    "tensor", atr,
                    wait=wl(src_ok, ptfree, steal), inc=[("T", 1)])

                aeng = "scalar" if (rnd == 0 or rcp_all is None) else "vector"
                asem = "Sc" if aeng == "scalar" else "V"

                def adst(e, rnd=rnd, pt=pt, qt=qt, aeng=aeng):
                    ql = qt % 4
                    dst = attnT.rearrange("p (t c) -> p t c", t=16)[
                        :, rnd * 8:(rnd + 1) * 8, ql * 128:(ql + 1) * 128]
                    srct = pt.rearrange("p (t c) -> p t c", t=8)
                    return (e.copy(dst, srct) if aeng == "scalar"
                            else e.tensor_copy(dst, srct))
                atfree = t_avmm[qt // 4 - 1] if qt >= 4 else None
                t_adst[qt][rnd] = ph.do(
                    aeng, adst, wait=wl(t_atr[qt][rnd], atfree),
                    inc=[(asem, 1)])
                tk["adst_r%d" % rnd] = t_adst[qt][rnd]

        def stage_D(qg):
            h = qg // 4
            v_tok = head_args(h)[2]
            sl = psC[:, 0:512] if vdw == 128 else psC[0:vdw, 0:512]

            def avmm(e, sl=sl, v_tok=v_tok):
                for kt in range(16):
                    ins = nc.tensor.matmul(
                        sl, v_tok[:, kt * vdw:(kt + 1) * vdw],
                        attnT[:, kt * 512:(kt + 1) * 512],
                        start=(kt == 0), stop=(kt == 15))
                return ins
            pcfree = t_cev[qg - 1] if qg >= 1 else None
            t_avmm[qg] = ph.do(
                "tensor", avmm,
                wait=wl(t_adst[qg * 4 + 3][1], head_state[h]["vt"], pcfree),
                inc=[("T", 1)])
            tk["attnT"] = t_avmm[qg]
            gg = qg % 4
            if z_sink is not None:
                t_cev[qg] = z_sink(qg, h, gg, t_avmm[qg])
                return

            def cev(e, sl=sl):
                return e.copy(ctx_pc[:], sl)
            cxfree = tk.get("ctx")
            t_cev[qg] = ph.do("scalar", cev, wait=wl(t_avmm[qg], cxfree),
                              inc=[("Sc", 1)])
            tk["avc"] = t_cev[qg]
            dst = tls[gg * 4:(gg + 1) * 4, :, h * 128:(h + 1) * 128].rearrange(
                "t p j -> p t j")
            srcx = ctx_pc.rearrange("p (t j) -> p t j", t=4)
            tk["ctx"] = ph.do("gpsimd", lambda e, dst=dst, srcx=srcx:
                              e.dma_start(out=dst, in_=srcx),
                              wait=[t_cev[qg]], inc=[("G", 16)])

        for i in range(NQT + 4):
            if i < NQT:
                stage_A(i)
            if 4 <= i and (i - 4) % 4 == 3:
                stage_D((i - 4) // 4)
            if 3 <= i < NQT + 3:
                stage_C(i - 3)
            if 1 <= i < NQT + 1:
                stage_B(i - 1)
            hnext = i // 16 + 1
            if hnext < len(preps) and preps[hnext] is not None:
                next(preps[hnext], None)

    def p2(q_cF, k_cF, krF_bf):
        ph = Ph(k, "p2")
        wlf = lambda *t: [x for x in t if x]
        with ExitStack() as es:
            c_cs2 = es.enter_context(nc.sbuf_tensor("p2_cs", [128, 2 * S], F32))
            wqb_f32 = es.enter_context(nc.sbuf_tensor("p2_wqf", [128, 6 * 384], F32))
            wqb_bf = es.enter_context(nc.sbuf_tensor("p2_wqb", [128, 12 * 384], BF16))
            wqbr_bf = es.enter_context(nc.sbuf_tensor("p2_wqr", [128, 12 * 128], BF16))
            wkvb_f32 = es.enter_context(nc.sbuf_tensor("p2_wkf", [128, 4 * 256], F32))
            wkvb_bf = es.enter_context(nc.sbuf_tensor("p2_wkb", [128, 4 * 256], BF16))
            qn2 = es.enter_context(nc.sbuf_tensor("p2_qn", [128, 2 * S], BF16))
            kn2 = es.enter_context(nc.sbuf_tensor("p2_kn", [128, 2 * S], BF16))
            vF2 = es.enter_context(nc.sbuf_tensor("p2_vF", [128, 2 * S], BF16))
            vt2 = es.enter_context(nc.sbuf_tensor("p2_vt", [128, 2 * S], BF16))
            qr2 = es.enter_context(nc.sbuf_tensor("p2_qr", [128, 2 * S], BF16))
            rp_raw = es.enter_context(nc.sbuf_tensor("p2_rraw", [128, 1024], F32))
            rp_sw = es.enter_context(nc.sbuf_tensor("p2_rsw", [128, 1024], F32))
            attn_bf2 = es.enter_context(nc.sbuf_tensor("p2_at2", [128, 2 * S], BF16))
            sc_sb2 = es.enter_context(nc.sbuf_tensor("p2_scb", [128, 2 * S], F32))
            rstg2 = es.enter_context(nc.sbuf_tensor("p2_rst", [16, 256], F32))
            rcp_all_a = es.enter_context(nc.sbuf_tensor("p2_rca", [128, 512], F32))
            attnT = es.enter_context(nc.sbuf_tensor("p2_aT", [128, 16 * 512], BF16))
            ctx_pc = es.enter_context(nc.sbuf_tensor("p2_cx", [128, 512], BF16))
            sm = {"mx": es.enter_context(nc.sbuf_tensor("p2_mx", [128, 4], F32)),
                  "nmx": es.enter_context(nc.sbuf_tensor("p2_nmx", [128, 2], F32)),
                  "dd": es.enter_context(nc.sbuf_tensor("p2_dd", [128, 4], F32)),
                  "rcp": es.enter_context(nc.sbuf_tensor("p2_rcp", [128, 2], F32))}
            psP = psC[:, 512:1024]
            ph.do("sync", lambda e: e.dma_start(out=c_cs2[:, 0:S], in_=c_cos),
                  inc=[("Q", 16)])
            ph.do("sync", lambda e: e.dma_start(out=c_cs2[:, S:2 * S], in_=c_sinm),
                  inc=[("Q", 16)])
            tk = {}

            def mk_prep(h):
                hh = h % 2
                nb = (h % 2) * S
                gb = ((h // 2) % 2) * S

                def bguard():
                    return wlf(tk.get("evm1"))

                def gen():
                    qk = []
                    if hh == 0:
                        pair = h // 2
                        srcw = wqb[:, pair * 384:(pair + 1) * 384].rearrange(
                            "(dc p) c -> p dc c", p=128)
                        dstw = wqb_f32.rearrange("p (dc c) -> p dc c", dc=6)
                        tconv = None
                        for hf in range(2):
                            tdma = ph.do("sync", lambda e, srcw=srcw, dstw=dstw,
                                         hf=hf: e.dma_start(
                                             out=dstw,
                                             in_=srcw[:, hf * 6:(hf + 1) * 6, :]),
                                         wait=wlf(tconv if hf else
                                                  tk.get("wqf_rd")),
                                         inc=[("Q", 16)])
                            yield
                            tconv = ph.do("vector", lambda e, hf=hf:
                                          e.tensor_copy(
                                              wqb_bf[:, hf * 2304:(hf + 1) * 2304],
                                              wqb_f32[:]),
                                          wait=[tdma] + wlf(tk.get("wqb_rd")
                                                            if hf == 0 else None),
                                          inc=[("V", 1)])
                        tk["wqf_rd"] = tconv
                        tk["wqbconv"] = tconv
                        yield

                        def rstage(e):
                            for dc in range(12):
                                e.tensor_copy(wqbr_bf[:, dc * 128:dc * 128 + 64],
                                              wqb_bf[:, dc * 384 + 128:
                                                     dc * 384 + 192])
                                ins = e.tensor_copy(
                                    wqbr_bf[:, dc * 128 + 64:dc * 128 + 128],
                                    wqb_bf[:, dc * 384 + 320:dc * 384 + 384])
                            return ins
                        trs = ph.do("vector", rstage,
                                    wait=[tconv] + wlf(tk.get("wqr_rd")),
                                    inc=[("V", 1)])
                        tk["rstage"] = trs
                        yield
                    srck = wkvb[:, h * 256:(h + 1) * 256].rearrange(
                        "(dc p) c -> p dc c", p=128)
                    dstk = wkvb_f32.rearrange("p (dc c) -> p dc c", dc=4)
                    tkv = ph.do("scalar", lambda e, srck=srck, dstk=dstk:
                                e.dma_start(out=dstk, in_=srck),
                                wait=wlf(tk.get("wkf_rd")), inc=[("Adma", 16)])
                    tkc = ph.do("vector", lambda e: e.tensor_copy(wkvb_bf[:],
                                                                  wkvb_f32[:]),
                                wait=[tkv] + wlf(tk.get("wkb_rd")), inc=[("V", 1)])
                    tk["wkf_rd"] = tkc
                    yield
                    if hh == 0:
                        for qh in range(2):
                            evs = []
                            for j in range(2):
                                def qrmm(e, qh=qh, j=j):
                                    for dc in range(12):
                                        ins = nc.tensor.matmul(
                                            psP,
                                            wqbr_bf[:, dc * 128:(dc + 1) * 128],
                                            q_cF[:, dc * S + qh * 1024 + j * 512:
                                                 dc * S + qh * 1024 +
                                                 (j + 1) * 512],
                                            start=(dc == 0), stop=(dc == 11))
                                    return ins
                                tq = ph.do("tensor", qrmm,
                                           wait=[tk["rstage"]] +
                                           wlf(tk.get("psP")), inc=[("T", 1)])
                                tk["wqr_rd"] = tq
                                tev = ph.do("vector", lambda e, j=j:
                                            e.tensor_copy(
                                                rp_raw[:, j * 512:(j + 1) * 512],
                                                psP),
                                            wait=[tq] + wlf(tk.get("rp_rd")
                                                            if j == 0 else None),
                                            inc=[("V", 1)])
                                tk["psP"] = tev
                                evs.append(tev)
                                yield

                            def swp(e):
                                for g, sw_ in ((0, 32), (32, 0), (64, 96),
                                               (96, 64)):
                                    ins = e.tensor_copy(rp_sw[g:g + 32, :],
                                                        rp_raw[sw_:sw_ + 32, :])
                                return ins
                            tsw = ph.do("vector", swp, wait=evs, inc=[("V", 1)])
                            tm1 = ph.do("vector", lambda e, qh=qh: e.tensor_mul(
                                rp_raw[:], rp_raw[:],
                                c_cs2[:, qh * 1024:(qh + 1) * 1024]),
                                wait=[tsw], inc=[("V", 1)])
                            tm2 = ph.do("vector", lambda e, qh=qh: e.tensor_mul(
                                rp_sw[:], rp_sw[:],
                                c_cs2[:, S + qh * 1024: S + (qh + 1) * 1024]),
                                wait=[tsw], inc=[("V", 1)])
                            tro = ph.do("vector", lambda e, qh=qh, gb=gb:
                                        e.tensor_add(
                                            qr2[:, gb + qh * 1024:
                                                gb + (qh + 1) * 1024],
                                            rp_raw[:], rp_sw[:]),
                                        wait=[tm1, tm2] + bguard(),
                                        inc=[("V", 1)])
                            tk["rp_rd"] = tro
                            tk["qr_ready"] = tro
                            yield
                    qk.append(tk["qr_ready"])
                    last_ev = None
                    for qh in range(2):
                        for j in range(2):
                            def qnmm(e, qh=qh, j=j, hh=hh):
                                for dc in range(12):
                                    ins = nc.tensor.matmul(
                                        psP,
                                        wqb_bf[:, dc * 384 + hh * 192:
                                               dc * 384 + hh * 192 + 128],
                                        q_cF[:, dc * S + qh * 1024 + j * 512:
                                             dc * S + qh * 1024 + (j + 1) * 512],
                                        start=(dc == 0), stop=(dc == 11))
                                return ins
                            tqn = ph.do("tensor", qnmm,
                                        wait=[tk["wqbconv"]] + wlf(tk.get("psP")),
                                        inc=[("T", 1)])
                            tk["wqb_rd"] = tqn
                            last_ev = ph.do("scalar", lambda e, qh=qh, j=j, nb=nb:
                                            e.copy(qn2[:, nb + qh * 1024 + j * 512:
                                                       nb + qh * 1024 +
                                                       (j + 1) * 512], psP),
                                            wait=[tqn] + bguard(), inc=[("Sc", 1)])
                            tk["psP"] = last_ev
                            yield
                    for qh in range(2):
                        for j in range(2):
                            def knmm(e, qh=qh, j=j):
                                for dc in range(4):
                                    ins = nc.tensor.matmul(
                                        psP,
                                        wkvb_bf[:, dc * 256:dc * 256 + 128],
                                        k_cF[:, dc * S + qh * 1024 + j * 512:
                                             dc * S + qh * 1024 + (j + 1) * 512],
                                        start=(dc == 0), stop=(dc == 3))
                                return ins
                            tkn = ph.do("tensor", knmm,
                                        wait=[tkc] + wlf(tk.get("psP")),
                                        inc=[("T", 1)])
                            tk["wkb_rd"] = tkn
                            last_ev = ph.do("scalar", lambda e, qh=qh, j=j, nb=nb:
                                            e.copy(kn2[:, nb + qh * 1024 + j * 512:
                                                       nb + qh * 1024 +
                                                       (j + 1) * 512], psP),
                                            wait=[tkn] + bguard(), inc=[("Sc", 1)])
                            tk["psP"] = last_ev
                            yield
                    qk.append(last_ev)
                    vevs = []
                    for qh in range(2):
                        for j in range(2):
                            def vmm(e, qh=qh, j=j):
                                for dc in range(4):
                                    ins = nc.tensor.matmul(
                                        psP,
                                        wkvb_bf[:, dc * 256 + 128:dc * 256 + 256],
                                        k_cF[:, dc * S + qh * 1024 + j * 512:
                                             dc * S + qh * 1024 + (j + 1) * 512],
                                        start=(dc == 0), stop=(dc == 3))
                                return ins
                            tvm = ph.do("tensor", vmm,
                                        wait=[tkc] + wlf(tk.get("psP")),
                                        inc=[("T", 1)])
                            tk["wkb_rd"] = tvm
                            tve = ph.do("vector", lambda e, qh=qh, j=j, nb=nb:
                                        e.tensor_copy(
                                            vF2[:, nb + qh * 1024 + j * 512:
                                                nb + qh * 1024 + (j + 1) * 512],
                                            psP),
                                        wait=[tvm] + bguard(), inc=[("V", 1)])
                            tk["psP"] = tve
                            vevs.append(tve)
                            yield
                    last_vt = None
                    for rnd in range(2):
                        pt = ptA if rnd == 0 else ptB

                        def vtr(e, rnd=rnd, pt=pt, nb=nb):
                            for jj in range(8):
                                c = rnd * 8 + jj
                                ins = nc.tensor.transpose(
                                    pt[:, jj * 128:(jj + 1) * 128],
                                    vF2[:, nb + c * 128: nb + (c + 1) * 128],
                                    id_bf[:])
                            return ins
                        ttr = ph.do("tensor", vtr,
                                    wait=vevs + wlf(tk.get("adst_r%d" % rnd)),
                                    inc=[("T", 1)])
                        tvev = ph.do("scalar", lambda e, rnd=rnd, pt=pt, nb=nb:
                                     e.copy(vt2[:, nb + rnd * 1024:
                                                nb + (rnd + 1) * 1024], pt[:]),
                                     wait=[ttr] + wlf(tk.get("attnT")),
                                     inc=[("Sc", 1)])
                        tk["pt_steal_%d" % rnd] = tvev
                        last_vt = tvev
                        yield
                    tk["vt"] = last_vt
                    tk["qk"] = qk
                return gen()

            g0 = mk_prep(0)
            for _ in g0:
                pass
            for h in range(NH):
                nxt = mk_prep(h + 1) if h + 1 < NH else None
                hh = h % 2
                nb = (h % 2) * S
                gb = ((h // 2) % 2) * S
                emit_attention(ph, h,
                               q_chunks=[(qn2[:, nb:nb + S], None),
                                         (qr2[:, gb:gb + S], hh)],
                               k_chunks=[(kn2[:, nb:nb + S], None),
                                         (krF_bf, hh)],
                               v_tok=vt2[:, nb:nb + S],
                               attn_bf2=attn_bf2, sc_sb2=sc_sb2, attnT=attnT,
                               ctx_pc=ctx_pc, sm=sm, tk=tk, prep=nxt)
                if nxt is not None:
                    for _ in nxt:
                        pass
            ph.emit()

    def emit_epilogue(name, w_dram, w_is_bf, x_in, x_out, bias_src=None,
                      lora=None, pre=None, act_scale=None):
        ph = Ph(k, name)
        with ExitStack() as es:
            act_sb = es.enter_context(nc.sbuf_tensor(name + "_act", [128, NT * D], BF16))
            wbf = es.enter_context(nc.sbuf_tensor(name + "_wbf", [128, 2 * 32 * 256], BF16))
            wst = None if w_is_bf else es.enter_context(
                nc.sbuf_tensor(name + "_wst", [128, 4 * 1024], F32))
            xsl = es.enter_context(nc.sbuf_tensor(name + "_xsl", [128, 4 * 256], F32))
            osl = es.enter_context(nc.sbuf_tensor(name + "_osl", [128, 4 * 256], F32))
            brow = es.enter_context(nc.sbuf_tensor(name + "_brow", [1, D], BF16))
            brf = es.enter_context(nc.sbuf_tensor(name + "_brf", [1, 1024], F32))
            btok = None
            if bias_src is not None:
                for c4 in range(4):
                    tl_ = ph.do("sync", lambda e, c4=c4: e.dma_start(
                        out=brf[:], in_=bias_src[None, c4 * 1024:(c4 + 1) * 1024]),
                        wait=[btok] if btok else [], inc=[("Q", 16)])
                    btok = ph.do("vector", lambda e, c4=c4: e.tensor_copy(
                        brow[0:1, c4 * 1024:(c4 + 1) * 1024], brf[:]),
                        wait=[tl_], inc=[("V", 1)])
            pre_tok = pre(ph) if pre is not None else None
            act_toks = []
            for tt in range(NT):
                act_toks.append(ph.do("gpsimd", lambda e, tt=tt: e.dma_start(
                    out=act_sb[:, tt * D:(tt + 1) * D], in_=tls[tt]),
                    wait=ph.w("G") if tt == 0 else [], inc=[("G", 16)]))
            act_tok = act_toks[-1]
            act_first = list(act_toks)
            if act_scale is not None:
                rrow = es.enter_context(
                    nc.sbuf_tensor(name + "_rrow", [1, 2048], F32))
                ones_f32 = es.enter_context(
                    nc.sbuf_tensor(name + "_o1f", [1, 128], F32))
                tof = ph.do("gpsimd", lambda e: e.memset(ones_f32[:], 1.0),
                            inc=[("Gc", 1)])
                rtk = {}
                mlast = {}

                def scale_act(tt):
                    psx = psC[:, 512:1024]
                    last = None
                    for hf in range(2):
                        hb = hf * 16
                        trd = ph.do("sync", lambda e, tt=tt, hf=hf, hb=hb:
                                    e.dma_start(
                            out=rrow[0:1, :].rearrange("q (h j) -> q h j",
                                                       h=16),
                            in_=act_scale[hb:hb + 16,
                                          tt * 128:(tt + 1) * 128]),
                            wait=[x for x in [mlast.get(0)] if x],
                            inc=[("Q", 16)])
                        for i in range(4):
                            def bmm(e, i=i, psx=psx):
                                return nc.tensor.matmul(
                                    psx, ones_f32[0:1, 0:128],
                                    rrow[0:1, i * 512:(i + 1) * 512],
                                    start=True, stop=True)
                            tb = ph.do("tensor", bmm,
                                       wait=[trd, tof] + [x for x in
                                                          [mlast.get(0)] if x],
                                       inc=[("T", 1)])

                            def bml(e, tt=tt, i=i, hf=hf, psx=psx):
                                a = act_sb[:, tt * D + hf * 2048 + i * 512:
                                           tt * D + hf * 2048 + (i + 1) * 512]
                                return e.tensor_tensor(a, a, psx, op=OP.mult)
                            mlast[0] = ph.do("vector", bml,
                                             wait=[tb, act_toks[tt]],
                                             inc=[("V", 1)])
                            last = mlast[0]
                    rtk[tt] = last
                    return last
            psl = [psA[:, 0:256], psA[:, 512:768], psB[:, 0:256], psB[:, 512:768],
                   psC[:, 0:256], psC[:, 512:768]]
            if act_scale is not None:
                psl = psl[:5]
            nps = len(psl)
            psl_free = [None] * nps
            osl_free = [None] * 4
            xtok = {}
            mm_last = {}
            pii = 0
            for nb in range(16):
                wb = (nb % 2) * 8192
                wfree = mm_last.get(nb - 2)
                load_toks = []
                if w_is_bf:
                    for sc8 in range(8):
                        r0 = sc8 * 512
                        srcw = w_dram[r0:r0 + 512, nb * 256:(nb + 1) * 256].rearrange(
                            "(dc p) c -> p dc c", p=128)
                        dstw = wbf[:, wb + sc8 * 1024: wb + (sc8 + 1) * 1024].rearrange(
                            "p (dc c) -> p dc c", dc=4)
                        load_toks.append(ph.do("sync", lambda e, srcw=srcw, dstw=dstw:
                                               e.dma_start(out=dstw, in_=srcw),
                                               wait=[wfree] if wfree else [],
                                               inc=[("Q", 16)]))
                    wready = load_toks
                else:
                    cv = None
                    for sc8 in range(8):
                        r0 = sc8 * 512
                        sb4 = (sc8 % 4) * 1024
                        srcw = w_dram[r0:r0 + 512, nb * 256:(nb + 1) * 256].rearrange(
                            "(dc p) c -> p dc c", p=128)
                        dstw = wst[:, sb4:sb4 + 1024].rearrange(
                            "p (dc c) -> p dc c", dc=4)
                        wld = [t for t in (xtok.get(("w", nb - 1, sc8 + 4)),
                                           xtok.get(("w", nb, sc8 - 4)),
                                           wfree if sc8 == 0 else None) if t]
                        tl_ = ph.do("sync", lambda e, srcw=srcw, dstw=dstw: e.dma_start(
                            out=dstw, in_=srcw), wait=wld, inc=[("Q", 16)])
                        cv = ph.do("vector", lambda e, wb=wb, sc8=sc8, sb4=sb4:
                                   e.tensor_copy(
                                       wbf[:, wb + sc8 * 1024: wb + (sc8 + 1) * 1024],
                                       wst[:, sb4:sb4 + 1024]),
                                   wait=[tl_] + ([wfree] if wfree else []),
                                   inc=[("V", 1)])
                        xtok[("w", nb, sc8)] = cv
                    wready = [cv]
                for tt in range(NT):
                    sl = psl[pii % nps]
                    xb = (pii % 4) * 256
                    ofree = osl_free[pii % 4]
                    tx = ph.do("scalar", lambda e, tt=tt, nb=nb, xb=xb: e.dma_start(
                        out=xsl[:, xb:xb + 256],
                        in_=x_in[tt * 128:(tt + 1) * 128, nb * 256:(nb + 1) * 256]),
                        wait=[t for t in (xtok.get(("x", pii - 4)),) if t],
                        inc=[("Adma", 16)])

                    def big(e, tt=tt, nb=nb, wb=wb, sl=sl):
                        n_ops = DCH + (1 if lora else 0) + (1 if bias_src is not None
                                                            else 0)
                        io = 0
                        for dc in range(DCH):
                            ins = nc.tensor.matmul(
                                sl, act_sb[:, tt * D + dc * 128: tt * D + (dc + 1) * 128],
                                wbf[:, wb + dc * 256: wb + (dc + 1) * 256],
                                start=(io == 0), stop=(io == n_ops - 1))
                            io += 1
                        if lora:
                            t_bf_, bt_sb_ = lora
                            ins = nc.tensor.matmul(
                                sl, t_bf_[:, tt * 128:(tt + 1) * 128],
                                bt_sb_[:, nb * 256:(nb + 1) * 256],
                                start=False, stop=(io == n_ops - 1))
                            io += 1
                        if bias_src is not None:
                            ins = nc.tensor.matmul(
                                sl, ones_bf[0:1, 0:128],
                                brow[0:1, nb * 256:(nb + 1) * 256],
                                start=False, stop=True)
                        return ins
                    if act_scale is not None and nb == 0:
                        act_first[tt] = scale_act(tt)
                    w = [act_tok if (tt == NT - 1 and act_scale is None)
                         else act_first[tt]] + wready + \
                        [t for t in (psl_free[pii % nps], btok, pre_tok) if t]
                    bt_ = ph.do("tensor", big, wait=w, inc=[("T", 1)])
                    mm_last[nb] = bt_
                    at_ = ph.do("vector", lambda e, xb=xb, sl=sl: e.tensor_add(
                        osl[:, xb:xb + 256], sl, xsl[:, xb:xb + 256]),
                        wait=[bt_, tx] + ([ofree] if ofree else []), inc=[("V", 1)])
                    psl_free[pii % nps] = at_
                    xtok[("x", pii)] = at_
                    st_ = ph.do("gpsimd", lambda e, tt=tt, nb=nb, xb=xb: e.dma_start(
                        out=x_out[tt * 128:(tt + 1) * 128, nb * 256:(nb + 1) * 256],
                        in_=osl[:, xb:xb + 256]), wait=[at_], inc=[("G", 16)])
                    osl_free[pii % 4] = st_
                    pii += 1
            ph.emit()

    def emit_mini_epilogue(name, zn_bf, M_bf, brow2_bf, x_in, x_out):
        ph = Ph(k, name)
        wlf = lambda *t: [x for x in t if x]
        with ExitStack() as es:
            xsl = es.enter_context(nc.sbuf_tensor(name + "_xsl", [128, 4 * 512], F32))
            osl = es.enter_context(nc.sbuf_tensor(name + "_osl", [128, 4 * 512], F32))
            psl = [psA[:, 0:512], psA[:, 512:1024], psB[:, 0:512],
                   psB[:, 512:1024], psC[:, 0:512], psC[:, 512:1024]]
            psl_free = [None] * 6
            osl_free = [None] * 4
            xtok = {}
            pii = 0
            for tt in range(NT):
                for nb in range(8):
                    sl = psl[pii % 6]
                    xb = (pii % 4) * 512
                    eng, q = (("sync", "Q") if pii % 2 == 0
                              else ("scalar", "Adma"))
                    tx = ph.do(eng, lambda e, tt=tt, nb=nb, xb=xb: e.dma_start(
                        out=xsl[:, xb:xb + 512],
                        in_=x_in[tt * 128:(tt + 1) * 128,
                                 nb * 512:(nb + 1) * 512]),
                        wait=wlf(xtok.get(pii - 4)), inc=[(q, 16)])

                    def big(e, tt=tt, nb=nb, sl=sl):
                        nc.tensor.matmul(sl, zn_bf[:, tt * 128:(tt + 1) * 128],
                                         M_bf[:, nb * 512:(nb + 1) * 512],
                                         start=True, stop=False)
                        return nc.tensor.matmul(
                            sl, ones_bf[0:1, 0:128],
                            brow2_bf[0:1, nb * 512:(nb + 1) * 512],
                            start=False, stop=True)
                    bt_ = ph.do("tensor", big, wait=wlf(psl_free[pii % 6]),
                                inc=[("T", 1)])
                    at_ = ph.do("vector", lambda e, xb=xb, sl=sl: e.tensor_add(
                        osl[:, xb:xb + 512], sl, xsl[:, xb:xb + 512]),
                        wait=wlf(bt_, tx, osl_free[pii % 4]), inc=[("V", 1)])
                    psl_free[pii % 6] = at_
                    xtok[pii] = at_
                    st_ = ph.do("gpsimd", lambda e, tt=tt, nb=nb, xb=xb:
                                e.dma_start(
                                    out=x_out[tt * 128:(tt + 1) * 128,
                                              nb * 512:(nb + 1) * 512],
                                    in_=osl[:, xb:xb + 512]),
                                wait=[at_], inc=[("G", 16)])
                    osl_free[pii % 4] = st_
                    pii += 1
            ph.emit()

    def make_f1(tag, x_in, t_bf, routing_vec):
        def f1():
            ph = Ph(k, tag)
            with ExitStack() as es:
                xt2 = es.enter_context(nc.sbuf_tensor(tag + "_xt", [128, 2 * D], F32))

                abuf = es.enter_context(nc.sbuf_tensor(tag + "_abuf", [128, D], BF16))
                a_f32 = es.enter_context(nc.sbuf_tensor(tag + "_af", [128, D], F32))
                at_sb = es.enter_context(nc.sbuf_tensor(tag + "_at", [128, D], BF16))
                xln_bf = es.enter_context(nc.sbuf_tensor(tag + "_xln", [128, 2 * D],
                                                          BF16))
                junk_bf = xln_bf
                xlt2 = es.enter_context(nc.sbuf_tensor(tag + "_xlt", [128, 2 * D], BF16))
                t_f32 = es.enter_context(nc.sbuf_tensor(tag + "_tf32", [128, S], F32))
                abuf2_dbg = es.enter_context(nc.sbuf_tensor(tag + "_ab2", [128, D],
                                                            BF16)) if dev_dump == "p4" \
                    else None
                rsum = es.enter_context(nc.sbuf_tensor(tag + "_rsum", [128, 32], F32))
                rtmp = es.enter_context(nc.sbuf_tensor(tag + "_rtmp", [128, 32], F32))
                rm_bf = es.enter_context(nc.sbuf_tensor(tag + "_rmbf", [128, 32], BF16))
                rw_f32 = es.enter_context(nc.sbuf_tensor(tag + "_rw", [8, D], F32))
                rw_bf = es.enter_context(nc.sbuf_tensor(tag + "_rwb", [8, D], BF16))
                rwt = es.enter_context(nc.sbuf_tensor(tag + "_rwt", [128, 256], BF16))
                sm8 = es.enter_context(nc.sbuf_tensor(tag + "_sm8", [1, 8], F32))
                r8col = es.enter_context(nc.sbuf_tensor(tag + "_r8", [8, 1], F32))
                r8bf = es.enter_context(nc.sbuf_tensor(tag + "_r8b", [8, 1], BF16))
                e8_sb = es.enter_context(nc.sbuf_tensor(tag + "_e8", [8, 128], BF16))
                nm8 = es.enter_context(nc.sbuf_tensor(tag + "_nm8", [1, 1], F32))
                dd8 = es.enter_context(nc.sbuf_tensor(tag + "_dd8", [1, 1], F32))
                rb8 = es.enter_context(nc.sbuf_tensor(tag + "_rb8", [1, 8], F32))
                stat = new_stat(tag, es)
                ph.do("sync", lambda e: e.dma_start(out=a_f32[:], in_=fA),
                      inc=[("Q", 16)])
                if dev_dump == "p4" and tag == "p4":
                    ph.do("vector", lambda e: e.memset(abuf[:], 777.0), inc=[("V", 1)])
                ph.do("vector", lambda e: e.tensor_copy(abuf[:], a_f32[:]),
                      wait=ph.w("Q", "V"), inc=[("V", 1)])
                if dev_dump == "p4" and tag == "p4":
                    d10 = nc.dram_tensor("dbg_abuf0", [128, D], BF16,
                                         kind="ExternalOutput").ap()
                    ph.do("gpsimd", lambda e: e.dma_start(out=d10, in_=abuf[:]),
                          wait=ph.w("V"), inc=[("G", 16)])
                    d11 = nc.dram_tensor("dbg_abuf2", [128, D], BF16,
                                         kind="ExternalOutput").ap()
                    def spacer(e):
                        for _ in range(10):
                            ins = e.tensor_scalar_mul(junk_bf[:, 0:2048],
                                                      junk_bf[:, 0:2048], 1.0)
                        return ins
                    ph.do("vector", spacer, wait=ph.w("V"), inc=[("V", 1)])
                    ph.do("vector", lambda e: e.tensor_copy(junk_bf[:, 0:D] if False
                                                            else abuf2_dbg[:], a_f32[:]),
                          wait=ph.w("V"), inc=[("V", 1)])
                    ph.do("gpsimd", lambda e: e.dma_start(out=d11, in_=abuf2_dbg[:]),
                          wait=ph.w("V"), inc=[("G", 16)])

                def adst(e, rnd, pt, lo, hi):
                    return e.copy(at_sb[:, lo * 128:hi * 128], pt[:, 0:(hi - lo) * 128])
                tr16(ph, abuf, 32, adst)
                if dev_dump == "p4" and tag == "p4":
                    d6 = nc.dram_tensor("dbg_at_early", [128, D], BF16,
                                        kind="ExternalOutput").ap()
                    ph.do("gpsimd", lambda e: e.dma_start(out=d6, in_=at_sb[:]),
                          wait=ph.w("Sc", "V", "T"), inc=[("G", 16)])
                    d7 = nc.dram_tensor("dbg_abuf", [128, D], BF16,
                                        kind="ExternalOutput").ap()
                    ph.do("gpsimd", lambda e: e.dma_start(out=d7, in_=abuf[:]),
                          wait=ph.w("Sc", "V", "T"), inc=[("G", 16)])
                    d8 = nc.dram_tensor("dbg_af32", [128, D], F32,
                                        kind="ExternalOutput").ap()
                    ph.do("gpsimd", lambda e: e.dma_start(out=d8, in_=a_f32[:]),
                          wait=ph.w("Sc", "V", "T"), inc=[("G", 16)])
                    d9 = nc.dram_tensor("dbg_af32b", [128, D], F32,
                                        kind="ExternalOutput").ap()
                    ph.do("gpsimd", lambda e: e.dma_start(out=d9, in_=fA),
                          wait=ph.w("Sc", "V", "T"), inc=[("G", 16)])
                ph.do("vector", lambda e: e.memset(rsum[:], 0.0), inc=[("V", 1)])
                ph.do("sync", lambda e: e.dma_start(out=e8_sb[:], in_=c_e8), inc=[("Q", 16)])
                ph.do("sync", lambda e: e.dma_start(out=rw_f32[:], in_=frW), inc=[("Q", 16)])
                ph.do("sync", lambda e: e.dma_start(out=rb8[:], in_=frb[None, :]),
                      inc=[("Q", 16)])
                ph.do("vector", lambda e: e.tensor_copy(rw_bf[:], rw_f32[:]),
                      wait=ph.w("Q"), inc=[("V", 1)])
                for rnd in range(4):
                    pt = ptA if rnd % 2 == 0 else ptB

                    def tr8(e, rnd=rnd, pt=pt):
                        for j in range(8):
                            c = rnd * 8 + j
                            ins = nc.tensor.transpose(pt[:, j * 8:(j + 1) * 8],
                                                      rw_bf[:, c * 128:(c + 1) * 128],
                                                      id_bf[0:8, 0:8])
                        return ins
                    ph.do("tensor", tr8, wait=ph.w("V", "Sc"), inc=[("T", 1)])
                    ph.do("scalar", lambda e, rnd=rnd, pt=pt: e.copy(
                        rwt[:, rnd * 64:(rnd + 1) * 64], pt[:, 0:64]),
                        wait=ph.w("T"), inc=[("Sc", 1)])
                at_ready = ph.w("Sc", "T", "V")
                f1_stats = [new_stat(tag + "s0", es), new_stat(tag + "s1", es)]
                pt_free = {0: at_ready[0], 1: at_ready[0]}
                wtg = wt_gen(ph, es, pt_free) if tag == "p4" else None

                def adv(n):
                    if wtg is not None:
                        for _ in range(n):
                            next(wtg, None)
                ln_prev = [{}, {}]
                slot_free = {}
                tev_prev = None
                for tt in range(NT):
                    tb = (tt % 2) * D
                    par = tt % 2
                    prev = dict(ln_prev[par])
                    prev["dst"] = slot_free.get(par)
                    prev["xln"] = slot_free.get(("xln", par))
                    prev["stat"] = slot_free.get(("stat", par))

                    def mk_dst(e, rnd, pt, tb=tb):
                        return e.copy(xlt2[:, tb + rnd * 1024: tb + (rnd + 1) * 1024],
                                      pt[:])
                    r = ln_tile_tok(ph, x_in, tt, xt2[:, tb:tb + D], f1_stats[par],
                                    xln_bf[:, tb:tb + D], mk_dst, pt_free, prev)
                    ln_prev[par] = {"xt": r["sq"]}
                    slot_free[("xln", par)] = r["xln"]
                    slot_free[("stat", par)] = r["sq"]
                    tg = ph.do("gpsimd", lambda e, tt=tt, tb=tb: e.dma_start(
                        out=tls[tt], in_=xlt2[:, tb:tb + D]), wait=[r["evlast"]],
                        inc=[("G", 16)])

                    def tmm(e, tb=tb):
                        for dc in range(DCH):
                            ins = nc.tensor.matmul(
                                psC[:, 0:128], at_sb[:, dc * 128:(dc + 1) * 128],
                                xlt2[:, tb + dc * 128: tb + (dc + 1) * 128],
                                start=(dc == 0), stop=(dc == DCH - 1))
                        return ins
                    w = [r["evlast"]] + ([tev_prev] if tev_prev else []) + \
                        (at_ready if tt == 0 else [])
                    tmt = ph.do("tensor", tmm, wait=w, inc=[("T", 1)])
                    tev = ph.do("vector", lambda e, tt=tt: e.tensor_copy(
                        t_f32[:, tt * 128:(tt + 1) * 128], psC[:, 0:128]),
                        wait=[tmt], inc=[("V", 1)])
                    tev_prev = tev
                    slots = [tg, tmt]
                    lo = max(Q_LO, tt * 128) - tt * 128
                    hi = min(Q_HI, (tt + 1) * 128) - tt * 128
                    if hi > lo:
                        def rsel(e, tb=tb, lo=lo, hi=hi):
                            src = xlt2[:, tb:tb + D].rearrange(
                                "p (dc j) -> p dc j", dc=DCH)[:, :, lo:hi]
                            return e.reduce_sum(rtmp[:], src, axis=AX.X)
                        trs_ = ph.do("vector", rsel, wait=[r["evlast"]],
                                     inc=[("V", 1)])
                        ph.do("vector", lambda e: e.tensor_add(rsum[:], rsum[:],
                                                               rtmp[:]),
                              wait=[trs_], inc=[("V", 1)])
                        slots.append(trs_)
                    slot_free[par] = slots
                    adv(14)
                if wtg is not None:
                    for _ in wtg:
                        pass
                ph.do("vector", lambda e: e.tensor_scalar_mul(
                    rtmp[:], rsum[:], 1.0 / ROUTER_CNT), wait=ph.w("V"), inc=[("V", 1)])
                ph.do("vector", lambda e: e.tensor_copy(rm_bf[:], rtmp[:]),
                      wait=ph.w("V"), inc=[("V", 1)])

                def lmm(e):
                    for dc in range(DCH):
                        ins = nc.tensor.matmul(psC[0:8, 0:1], rwt[:, dc * 8:(dc + 1) * 8],
                                               rm_bf[:, dc:dc + 1],
                                               start=(dc == 0), stop=(dc == DCH - 1))
                    return ins
                ph.do("tensor", lmm, wait=ph.w("V", "Sc"), inc=[("T", 1)])
                ph.do("vector", lambda e: e.tensor_copy(r8col[:], psC[0:8, 0:1]),
                      wait=ph.w("T"), inc=[("V", 1)])
                ph.do("tensor", lambda e: nc.tensor.transpose(psC[0:1, 512:520], r8col[:],
                                                              id_f32[0:8, 0:8]),
                      wait=ph.w("V"), inc=[("T", 1)])

                ph.do("vector", lambda e: e.tensor_add(sm8[:], psC[0:1, 512:520],
                                                       rb8[:]),
                      wait=ph.w("T", "Q", "V"), inc=[("V", 1)])
                ph.do("vector", lambda e: e.reduce_max(nm8[:], sm8[:], axis=AX.X,
                                                       negate=True),
                      wait=ph.w("V"), inc=[("V", 1)])
                ph.do("scalar", lambda e: e.activation(sm8[:], sm8[:], AF.Exp, bias=nm8[:],
                                                       accum_out=dd8[:]),
                      wait=ph.w("V"), inc=[("Sc", 1)])

                ph.do("vector", lambda e: e.reciprocal(dd8[:], dd8[:]),
                      wait=ph.w("Sc", "V"), inc=[("V", 1)])
                ph.do("vector", lambda e: e.tensor_scalar(sm8[:], sm8[:], dd8[:], 2.0,
                                                          op0=OP.mult, op1=OP.mult),
                      wait=ph.w("V"), inc=[("V", 1)])
                ph.do("tensor", lambda e: nc.tensor.transpose(psC[0:8, 640:641], sm8[:],
                                                              id_f32[0:1, 0:1]),
                      wait=ph.w("V"), inc=[("T", 1)])
                ph.do("vector", lambda e: e.tensor_copy(r8bf[:], psC[0:8, 640:641]),
                      wait=ph.w("T"), inc=[("V", 1)])
                ph.do("tensor", lambda e: nc.tensor.matmul(psC[:, 896:897], e8_sb[:],
                                                           r8bf[:], start=True, stop=True),
                      wait=ph.w("V"), inc=[("T", 1)])
                ph.do("vector", lambda e: e.tensor_copy(routing_vec[:], psC[:, 896:897]),
                      wait=ph.w("T"), inc=[("V", 1)])
                ph.do("vector", lambda e: e.tensor_scalar_mul(t_bf[:], t_f32[:],
                                                              routing_vec[:]),
                      wait=ph.w("V"), inc=[("V", 1)])
                if dev_dump == "p4" and tag == "p4":
                    d1 = nc.dram_tensor("dbg_t", [128, S], F32, kind="ExternalOutput").ap()
                    d2 = nc.dram_tensor("dbg_tbf", [128, S], BF16,
                                        kind="ExternalOutput").ap()
                    d3 = nc.dram_tensor("dbg_rv", [128, 1], F32,
                                        kind="ExternalOutput").ap()
                    d4 = nc.dram_tensor("dbg_rsum", [128, 32], F32,
                                        kind="ExternalOutput").ap()
                    ph.do("gpsimd", lambda e: e.dma_start(out=d1, in_=t_f32[:]),
                          wait=ph.w("V", "Sc"), inc=[("G", 16)])
                    ph.do("gpsimd", lambda e: e.dma_start(out=d2, in_=t_bf[:]),
                          wait=ph.w("V", "Sc"), inc=[("G", 16)])
                    ph.do("gpsimd", lambda e: e.dma_start(out=d3, in_=routing_vec[:]),
                          wait=ph.w("V", "Sc"), inc=[("G", 16)])
                    ph.do("gpsimd", lambda e: e.dma_start(out=d4, in_=rsum[:]),
                          wait=ph.w("V", "Sc"), inc=[("G", 16)])
                    d5 = nc.dram_tensor("dbg_at", [128, D], BF16,
                                        kind="ExternalOutput").ap()
                    ph.do("gpsimd", lambda e: e.dma_start(out=d5, in_=at_sb[:]),
                          wait=ph.w("V", "Sc"), inc=[("G", 16)])
                ph.emit()
        return f1

    def make_f2_pre(bt_sb, uniq=""):
        def f2_pre(ph):
            with ExitStack() as es:
                b_f32 = es.enter_context(nc.sbuf_tensor("btb_f32" + uniq, [128, 512], F32))
                b_bf = es.enter_context(nc.sbuf_tensor("btb_bf" + uniq, [128, 2 * 512], BF16))
                btst = es.enter_context(nc.sbuf_tensor("btb_st" + uniq, [128, 32], BF16))
                for pg in range(4):
                    for ei in range(2):
                        e_idx = pg * 2 + ei
                        src = fB[e_idx].rearrange("(ob p) r -> p ob r", p=128)
                        dstb = b_f32.rearrange("p (ob r) -> p ob r", ob=32)
                        for q4 in range(4):
                            ph.do("sync", lambda e, src=src, dstb=dstb, q4=q4:
                                  e.dma_start(out=dstb[:, q4 * 8:(q4 + 1) * 8, :],
                                              in_=src[:, q4 * 8:(q4 + 1) * 8, :]),
                                  wait=ph.w("V"), inc=[("Q", 16)])
                        ph.do("vector", lambda e, ei=ei: e.tensor_copy(
                            b_bf[:, ei * 512:(ei + 1) * 512], b_f32[:]),
                            wait=ph.w("Q", "T"), inc=[("V", 1)])
                    for rnd in range(4):
                        pt = ptA if rnd % 2 == 0 else ptB
                        for j in range(8):
                            ob = rnd * 8 + j

                            def st2(e, ob=ob):
                                e.tensor_copy(btst[:, 0:16],
                                              b_bf[:, ob * 16:(ob + 1) * 16])
                                return e.tensor_copy(
                                    btst[:, 16:32],
                                    b_bf[:, 512 + ob * 16: 512 + (ob + 1) * 16])
                            ph.do("vector", st2, wait=ph.w("T", "V"), inc=[("V", 1)])
                            ph.do("tensor", lambda e, pt=pt, j=j: nc.tensor.transpose(
                                pt[0:32, j * 128:(j + 1) * 128], btst[:], id_bf[:]),
                                wait=ph.w("V", "Sc"), inc=[("T", 1)])

                        def bev(e, pg=pg, rnd=rnd, pt=pt):
                            return e.copy(bt_sb[pg * 32:(pg + 1) * 32,
                                                rnd * 1024:(rnd + 1) * 1024], pt[0:32, :])
                        bev_tok = ph.do("scalar", bev, wait=ph.w("T"), inc=[("Sc", 1)])
                if dev_dump == "p4":
                    db = nc.dram_tensor("dbg_bt", [128, D], BF16,
                                        kind="ExternalOutput").ap()
                    ph.do("gpsimd", lambda e: e.dma_start(out=db, in_=bt_sb[:]),
                          wait=ph.w("Sc", "V"), inc=[("G", 16)])
                return bev_tok
        return f2_pre

    def p6(qpack, ckv_pack, cq_bf, ckv_bf, bvu_sb, wvu5_all, ckv_tok):
        ph = Ph(k, "p6")
        with ExitStack() as es:
            xt2 = es.enter_context(nc.sbuf_tensor("p6_xt", [128, 2 * D], F32))
            xln_bf = es.enter_context(nc.sbuf_tensor("p6_xln", [128, 2 * D], BF16))
            junk_bf = xln_bf
            xlt2 = es.enter_context(nc.sbuf_tensor("p6_xlt", [128, 2 * D], BF16))
            wq_f32 = es.enter_context(nc.sbuf_tensor("p6_wqf", [128, 32 * 12], F32))
            wq_bf = es.enter_context(nc.sbuf_tensor("p6_wqb", [128, 32 * 12], BF16))
            wk_f32 = es.enter_context(nc.sbuf_tensor("p6_wkf", [128, 32 * 4], F32))
            wk_bf = es.enter_context(nc.sbuf_tensor("p6_wkb", [128, 32 * 4], BF16))
            b12 = es.enter_context(nc.sbuf_tensor("p6_b12", [1, 12], F32))
            b4 = es.enter_context(nc.sbuf_tensor("p6_b4", [1, 4], F32))
            bqd_col = es.enter_context(nc.sbuf_tensor("p6_bqd", [12, 1], F32))
            bkvd_col = es.enter_context(nc.sbuf_tensor("p6_bkvd", [4, 1], F32))
            wqk_f32 = es.enter_context(nc.sbuf_tensor("p6_wqk", [12, 128], F32))
            wqk_bf = es.enter_context(nc.sbuf_tensor("p6_wqkb", [12, 128], BF16))
            wqk_pack = es.enter_context(nc.sbuf_tensor("p6_wqkp", [12, 8 * 128], BF16))
            bqk_row = es.enter_context(nc.sbuf_tensor("p6_bqk", [1, 128], F32))
            bqk_col = es.enter_context(nc.sbuf_tensor("p6_bqkc", [128, 1], F32))
            bqk_packed = es.enter_context(nc.sbuf_tensor("p6_bqkp", [128, 1], F32))
            bvu_rows = es.enter_context(nc.sbuf_tensor("p6_bvur", [32, 128], F32))
            stat = new_stat("p6", es)
            for dc in range(DCH):
                ph.do("sync", lambda e, dc=dc: e.dma_start(
                    out=wq_f32[:, dc * 12:(dc + 1) * 12],
                    in_=bWqd[dc * 128:(dc + 1) * 128, :]), inc=[("Q", 16)])
                ph.do("sync", lambda e, dc=dc: e.dma_start(
                    out=wk_f32[:, dc * 4:(dc + 1) * 4],
                    in_=bWkvd[dc * 128:(dc + 1) * 128, :]), inc=[("Q", 16)])
            ph.do("vector", lambda e: e.tensor_copy(wq_bf[:], wq_f32[:]),
                  wait=ph.w("Q"), inc=[("V", 1)])
            ph.do("vector", lambda e: e.tensor_copy(wk_bf[:], wk_f32[:]), inc=[("V", 1)])
            ph.do("sync", lambda e: e.dma_start(out=b12[:], in_=bbqd[None, :]),
                  inc=[("Q", 16)])
            ph.do("sync", lambda e: e.dma_start(out=b4[:], in_=bbkvd[None, :]),
                  inc=[("Q", 16)])
            ph.do("sync", lambda e: e.dma_start(out=wqk_f32[:], in_=bWqk), inc=[("Q", 16)])
            ph.do("sync", lambda e: e.dma_start(out=bqk_row[:], in_=bbqk[None, :]),
                  inc=[("Q", 16)])
            ph.do("sync", lambda e: e.dma_start(
                out=bvu_rows[:], in_=bbvu.rearrange("(h j) -> h j", h=32)),
                inc=[("Q", 16)])
            ph.do("vector", lambda e: e.tensor_copy(wqk_bf[:], wqk_f32[:]),
                  wait=ph.w("Q"), inc=[("V", 1)])
            ph.do("tensor", lambda e: nc.tensor.transpose(psC[0:12, 0:1], b12[:],
                                                          id_f32[0:1, 0:1]),
                  wait=ph.w("Q"), inc=[("T", 1)])
            ph.do("vector", lambda e: e.tensor_copy(bqd_col[:], psC[0:12, 0:1]),
                  wait=ph.w("T"), inc=[("V", 1)])
            ph.do("tensor", lambda e: nc.tensor.transpose(psC[0:4, 512:513], b4[:],
                                                          id_f32[0:1, 0:1]),
                  wait=ph.w("V"), inc=[("T", 1)])
            ph.do("vector", lambda e: e.tensor_copy(bkvd_col[:], psC[0:4, 512:513]),
                  wait=ph.w("T"), inc=[("V", 1)])
            ph.do("tensor", lambda e: nc.tensor.transpose(psC[:, 640:641], bqk_row[:],
                                                          id_f32[0:1, 0:1]),
                  wait=ph.w("V"), inc=[("T", 1)])
            ph.do("vector", lambda e: e.tensor_scalar(bqk_col[:], psC[:, 640:641], 0.5,
                                                      None, op0=OP.mult),
                  wait=ph.w("T"), inc=[("V", 1)])
            ph.do("tensor", lambda e: nc.tensor.transpose(psC[:, 768:800], bvu_rows[:],
                                                          id_f32[0:32, 0:32]),
                  wait=ph.w("V"), inc=[("T", 1)])
            tbvu = ph.do("vector", lambda e: e.tensor_copy(bvu_sb[:],
                                                           psC[:, 768:800]),
                         wait=ph.w("T"), inc=[("V", 1)])
            ph.do("gpsimd", lambda e: e.memset(wqk_pack[:], 0.0), inc=[("Gc", 1)])
            ph.do("gpsimd", lambda e: e.memset(ckv_pack[:], 0.0), inc=[("Gc", 1)])
            ph.do("gpsimd", lambda e: e.memset(bqk_packed[:], 0.0), inc=[("Gc", 1)])

            def wqkp(e):
                for g in range(8):
                    for s_ in range(4):
                        hsrc = (4 * g + s_) * 4
                        ins = e.tensor_copy(
                            wqk_pack[:, g * 128 + 32 * s_: g * 128 + 32 * s_ + 4],
                            wqk_bf[:, hsrc:hsrc + 4])
                return ins
            ph.do("vector", wqkp, wait=ph.w("Gc", "V"), inc=[("V", 1)])
            w_ready = ph.w("Sc", "V", "T")
            p6_stats = [new_stat("p6s0", es), new_stat("p6s1", es)]
            pt_free = {}
            ln_prev = [{}, {}]
            slot_free = {}
            cev_tok = {}
            for tt in range(NT):
                tb = (tt % 2) * D
                par = tt % 2
                prev = dict(ln_prev[par])
                prev["dst"] = slot_free.get(par)
                prev["xln"] = slot_free.get(("xln", par))
                prev["stat"] = slot_free.get(("stat", par))

                def mk_dst(e, rnd, pt, tb=tb):
                    return e.copy(xlt2[:, tb + rnd * 1024: tb + (rnd + 1) * 1024],
                                  pt[:])
                r = ln_tile_tok(ph, xs_b, tt, xt2[:, tb:tb + D], p6_stats[par],
                                xln_bf[:, tb:tb + D], mk_dst, pt_free, prev)
                ln_prev[par] = {"xt": r["sq"]}
                slot_free[("xln", par)] = r["xln"]
                slot_free[("stat", par)] = r["sq"]
                ps = psA if tt % 2 == 0 else psB

                def cmm(e, tb=tb, ps=ps):
                    for dc in range(DCH):
                        nc.tensor.matmul(ps[0:12, 0:128], wq_bf[:, dc * 12:(dc + 1) * 12],
                                         xlt2[:, tb + dc * 128: tb + (dc + 1) * 128],
                                         start=(dc == 0), stop=(dc == DCH - 1))
                    for dc in range(DCH):
                        ins = nc.tensor.matmul(ps[0:4, 512:640],
                                               wk_bf[:, dc * 4:(dc + 1) * 4],
                                               xlt2[:, tb + dc * 128: tb + (dc + 1) * 128],
                                               start=(dc == 0), stop=(dc == DCH - 1))
                    return ins
                w = [r["evlast"]] + ([cev_tok[tt - 2]] if tt >= 2 else w_ready)
                tmc = ph.do("tensor", cmm, wait=w, inc=[("T", 1)])

                def cev(e, tt=tt, ps=ps):
                    e.tensor_scalar(cq_bf[:, tt * 128:(tt + 1) * 128], ps[0:12, 0:128],
                                    1.0, bqd_col[0:12, :], op0=OP.mult, op1=OP.add)
                    return e.tensor_scalar(ckv_bf[:, tt * 128:(tt + 1) * 128],
                                           ps[0:4, 512:640], 1.0, bkvd_col[0:4, :],
                                           op0=OP.mult, op1=OP.add)
                cev_tok[tt] = ph.do("vector", cev, wait=[tmc], inc=[("V", 1)])
                slot_free[par] = [tmc]

            def ckvp(e):
                for s_ in range(4):
                    ins = e.tensor_copy(ckv_pack[32 * s_:32 * s_ + 4, :], ckv_bf[:])
                return ins
            ph.do("vector", ckvp, wait=ph.w("Gc", "V"), inc=[("V", 1)])
            cq_ready = ph.w("V")
            vev = None
            for g in range(8):
                gts = []
                for s_ in range(4):
                    src_p = 4 * (4 * g + s_)
                    gts.append(ph.do("gpsimd", lambda e, s_=s_, src_p=src_p:
                                     e.dma_start(
                                         out=bqk_packed[32 * s_:32 * s_ + 4, :],
                                         in_=bqk_col[src_p:src_p + 4, :]),
                                     wait=([vev] if vev else ph.w("V")),
                                     inc=[("G", 16)]))
                for qh in range(2):
                    tmq = None
                    for j in range(2):
                        tmq = ph.do("tensor", lambda e, g=g, qh=qh, j=j:
                                    nc.tensor.matmul(
                                        psC[:, j * 512:(j + 1) * 512],
                                        wqk_pack[:, g * 128:(g + 1) * 128],
                                        cq_bf[:, qh * 1024 + j * 512:
                                              qh * 1024 + (j + 1) * 512],
                                        start=True, stop=True),
                                    wait=([vev] if vev else cq_ready) if j == 0
                                    else [], inc=[("T", 1)])
                    vev = ph.do("vector", lambda e, g=g, qh=qh: e.tensor_scalar(
                        qpack[:, g * S + qh * 1024: g * S + (qh + 1) * 1024],
                        psC[:, 0:1024], 0.5, bqk_packed[:], op0=OP.mult, op1=OP.add),
                        wait=[tmq] + gts, inc=[("V", 1)])

            wvu_f32 = es.enter_context(nc.sbuf_tensor("p6_wvf", [4, D], F32))
            wvu_bf = es.enter_context(nc.sbuf_tensor("p6_wvb", [4, D], BF16))
            wlf2 = lambda *t: [x for x in t if x]
            tw = ph.do("sync", lambda e: e.dma_start(out=wvu_f32[:], in_=bWvu),
                       inc=[("Q", 16)])
            twc = ph.do("vector", lambda e: e.tensor_copy(wvu_bf[:], wvu_f32[:]),
                        wait=[tw], inc=[("V", 1)])
            pt_ev = {}
            for r in range(4):
                pt = ptA if r % 2 == 0 else ptB

                def wtr(e, r=r, pt=pt):
                    for j in range(8):
                        h_ = r * 8 + j
                        ins = nc.tensor.transpose(
                            pt[:, j * 128:j * 128 + 4],
                            wvu_bf[0:4, h_ * 128:(h_ + 1) * 128],
                            id_bf[0:4, 0:4])
                    return ins
                ttr = ph.do("tensor", wtr, wait=wlf2(twc, pt_ev.get(r - 2)),
                            inc=[("T", 1)])

                def wev(e, r=r, pt=pt):
                    dst = wvu5_all.rearrange("p (h f) -> p h f", f=5)[
                        :, r * 8:(r + 1) * 8, 0:4]
                    src = pt.rearrange("p (h f) -> p h f", f=128)[:, 0:8, 0:4]
                    return e.tensor_copy(dst, src)
                pt_ev[r] = ph.do("vector", wev, wait=[ttr], inc=[("V", 1)])
            ph.do("vector", lambda e: e.tensor_copy(
                wvu5_all.rearrange("p (h f) -> p h f", f=5)[:, :, 4:5],
                bvu_sb.rearrange("p (h f) -> p h f", f=1)),
                wait=[tbvu], inc=[("V", 1)])
            for r in range(2):
                pt = ptA if r % 2 == 0 else ptB

                def ctr(e, r=r, pt=pt):
                    for j in range(8):
                        c = r * 8 + j
                        ins = nc.tensor.transpose(
                            pt[:, j * 128:j * 128 + 4],
                            ckv_bf[0:4, c * 128:(c + 1) * 128],
                            id_bf[0:4, 0:4])
                    return ins
                tct = ph.do("tensor", ctr,
                            wait=wlf2(cev_tok[r * 8 + 7], pt_ev.get(r + 2)),
                            inc=[("T", 1)])

                def cevk(e, r=r, pt=pt):
                    dst = ckv_tok.rearrange("p (h f) -> p h f", f=4)[
                        :, r * 8:(r + 1) * 8, :]
                    src = pt.rearrange("p (h f) -> p h f", f=128)[:, 0:8, 0:4]
                    return e.tensor_copy(dst, src)
                ph.do("vector", cevk, wait=[tct], inc=[("V", 1)])
            ph.emit()

    def p7(qpack, ckv_pack, wvu5_all, ckv_tok, zn_bf, M_bf, bvw_parts,
           brow2_bf):
        ph = Ph(k, "p7")
        wlf = lambda *t: [x for x in t if x]
        with ExitStack() as es:
            wo_f32 = es.enter_context(nc.sbuf_tensor("p7_wof", [128, 2048], F32))
            wo_bf = es.enter_context(nc.sbuf_tensor("p7_wob", [128, 2 * D], BF16))
            mstg = es.enter_context(nc.sbuf_tensor("p7_mst", [64, 1024], BF16))
            zstg = es.enter_context(nc.sbuf_tensor("p7_zst", [4, 1024], F32))
            zn_f32 = es.enter_context(nc.sbuf_tensor("p7_znf", [128, S], F32))
            rcp_all = es.enter_context(nc.sbuf_tensor("p7_rca", [128, 512], F32))
            RCPE = es.enter_context(nc.sbuf_tensor("p7_rce", [128, S], F32))
            rstg = es.enter_context(nc.sbuf_tensor("p7_rst", [16, 256], F32))
            bo_f32 = es.enter_context(nc.sbuf_tensor("p7_bo", [1, D], F32))
            brow2_f32 = es.enter_context(nc.sbuf_tensor("p7_br2", [1, D], F32))
            ones32 = es.enter_context(nc.sbuf_tensor("p7_o32", [32, 1], BF16))
            attn_bf2 = es.enter_context(nc.sbuf_tensor("p7_at2", [128, 2 * S], BF16))
            sc_sb2 = es.enter_context(nc.sbuf_tensor("p7_scb", [128, 2 * S], F32))
            attnT = es.enter_context(nc.sbuf_tensor("p7_aT", [128, 16 * 512], BF16))
            sm = {"mx": es.enter_context(nc.sbuf_tensor("p7_mx", [128, 4], F32)),
                  "nmx": es.enter_context(nc.sbuf_tensor("p7_nmx", [128, 2], F32)),
                  "dd": es.enter_context(nc.sbuf_tensor("p7_dd", [128, 4], F32)),
                  "rcp": es.enter_context(nc.sbuf_tensor("p7_rcp", [128, 2], F32))}
            tk = {}

            def mk_prep(h):
                wb = (h % 2) * D

                def gen():
                    casts = []
                    for c4 in range(4):
                        slot = c4 % 2

                        def wld(e, h=h, c4=c4, slot=slot):
                            return e.dma_start(
                                out=wo_f32[:, slot * 1024:(slot + 1) * 1024],
                                in_=bWo[h * 128:(h + 1) * 128,
                                        c4 * 1024:(c4 + 1) * 1024])
                        tl_ = ph.do("sync", wld,
                                    wait=wlf(casts[c4 - 2] if c4 >= 2 else
                                             tk.get("wof%d" % slot)),
                                    inc=[("Q", 16)])
                        tc = ph.do("vector", lambda e, c4=c4, slot=slot, wb=wb:
                                   e.tensor_copy(
                                       wo_bf[:, wb + c4 * 1024:
                                             wb + (c4 + 1) * 1024],
                                       wo_f32[:, slot * 1024:(slot + 1) * 1024]),
                                   wait=wlf(tl_, tk.get("wo_used_%d" % (h % 2))
                                            if c4 == 0 else None),
                                   inc=[("V", 1)])
                        casts.append(tc)
                        tk["wof%d" % slot] = tc
                        yield
                    mm_last = None
                    ev_prev = None
                    dma_prev = {}
                    ev_hist = {}
                    for nb in range(8):
                        slot = nb % 2
                        pb = slot * 32

                        def mmm(e, h=h, nb=nb, wb=wb, pb=pb):
                            return nc.tensor.matmul(
                                psC[pb:pb + 5, 512:1024],
                                wvu5_all[:, h * 5:(h + 1) * 5],
                                wo_bf[:, wb + nb * 512: wb + (nb + 1) * 512],
                                start=True, stop=True)
                        tm = ph.do("tensor", mmm,
                                   wait=wlf(casts[nb // 2],
                                            ev_hist.get(slot,
                                                        tk.get("mps%d" % slot))),
                                   inc=[("T", 1)])
                        mm_last = tm

                        def mev(e, slot=slot, pb=pb):
                            return e.tensor_copy(
                                mstg[pb:pb + 5, slot * 512:(slot + 1) * 512],
                                psC[pb:pb + 5, 512:1024])
                        ev = ph.do("vector", mev,
                                   wait=wlf(tm) + dma_prev.get(slot, []),
                                   inc=[("V", 1)])
                        ev_hist[slot] = ev
                        tk["mps%d" % slot] = ev
                        ev_prev = ev
                        d1 = ph.do("scalar", lambda e, h=h, nb=nb, slot=slot,
                                   pb=pb: e.dma_start(
                                       out=M_bf[h * 4:(h + 1) * 4,
                                                nb * 512:(nb + 1) * 512],
                                       in_=mstg[pb:pb + 4,
                                                slot * 512:(slot + 1) * 512]),
                                   wait=[ev], inc=[("Adma", 16)])
                        d2 = ph.do("scalar", lambda e, h=h, nb=nb, slot=slot,
                                   pb=pb: e.dma_start(
                                       out=bvw_parts[h:h + 1,
                                                     nb * 512:(nb + 1) * 512],
                                       in_=mstg[pb + 4:pb + 5,
                                                slot * 512:(slot + 1) * 512]),
                                   wait=[ev], inc=[("Adma", 16)])
                        dma_prev[slot] = [d1, d2]
                        if nb % 2 == 1:
                            yield
                    tk["wo_used_%d" % (h % 2)] = mm_last
                    tk["mev_last"] = ev_prev
                    tk["vt"] = None
                    tk["qk"] = []
                return gen()

            preps = [mk_prep(h) for h in range(NH)] + [None]
            for _ in preps[0]:
                pass

            zdma = {}

            def z_sink(qg, h, gg, tmm):
                slot = qg % 2
                tz = ph.do("scalar", lambda e, slot=slot: e.copy(
                    zstg[0:4, slot * 512:(slot + 1) * 512], psC[0:4, 0:512]),
                    wait=wlf(tmm, zdma.get(slot)), inc=[("Sc", 1)])
                td = ph.do("gpsimd", lambda e, h=h, gg=gg, slot=slot:
                           e.dma_start(
                               out=zn_f32[h * 4:(h + 1) * 4,
                                          gg * 512:(gg + 1) * 512],
                               in_=zstg[0:4, slot * 512:(slot + 1) * 512]),
                           wait=[tz], inc=[("G", 16)])
                zdma[slot] = td
                tk["z_last"] = tz
                return tz

            def head_args(h):
                g, s_ = h // 4, h % 4
                return ([(qpack[:, g * S:(g + 1) * S], "strip")],
                        [(ckv_pack, "strip")], ckv_tok,
                        (32 * s_, 0), s_)
            emit_attention_stream(ph, NH, head_args, attn_bf2, sc_sb2, attnT,
                                  None, sm, tk, preps, vdw=4, z_sink=z_sink,
                                  rcp_all=rcp_all)

            rv_prev = {}
            rd_prev = {}
            for h in range(NH):
                slot = h % 2
                ttr = ph.do("tensor", lambda e, h=h: nc.tensor.transpose(
                    psC[0:16, 0:128], rcp_all[:, h * 16:(h + 1) * 16],
                    id_f32[:]),
                    wait=(ph.w("V") + wlf(tk.get("z_last"))) if h == 0
                    else wlf(rv_prev.get(1 - slot)), inc=[("T", 1)])
                rv = ph.do("vector", lambda e, slot=slot: e.tensor_copy(
                    rstg[:, slot * 128:(slot + 1) * 128], psC[0:16, 0:128]),
                    wait=[ttr] + rd_prev.get(slot, []), inc=[("V", 1)])
                rv_prev[slot] = rv
                rd_prev[slot] = []
                for j in range(4):
                    eng, q = (("gpsimd", "G"), ("sync", "Q"))[j % 2]
                    rd_prev[slot].append(ph.do(
                        eng, lambda e, h=h, j=j, slot=slot: e.dma_start(
                            out=RCPE[h * 4 + j:h * 4 + j + 1, :].rearrange(
                                "q (c p) -> q c p", c=16),
                            in_=rstg[:, slot * 128:(slot + 1) * 128]),
                        wait=[rv], inc=[(q, 16)]))
            ph.do("vector", lambda e: e.tensor_tensor(
                zn_bf[:], zn_f32[:], RCPE[:], op=OP.mult),
                wait=ph.w("Q", "G"), inc=[("V", 1)])

            tone = ph.do("gpsimd", lambda e: e.memset(ones32[:], 1.0),
                         inc=[("Gc", 1)])
            tbo = ph.do("sync", lambda e: e.dma_start(out=bo_f32[:],
                                                      in_=bbo[None, :]),
                        inc=[("Q", 16)])
            tprev = None
            for nb in range(8):
                w = wlf(tprev)
                if nb == 0:
                    w = ph.w("Adma", "Gc") + wlf(tk.get("mev_last"))
                tm = ph.do("tensor", lambda e, nb=nb: nc.tensor.matmul(
                    psC[0:1, 512:1024], ones32[:],
                    bvw_parts[:, nb * 512:(nb + 1) * 512],
                    start=True, stop=True), wait=w, inc=[("T", 1)])
                tprev = ph.do("vector", lambda e, nb=nb: e.tensor_tensor(
                    brow2_f32[0:1, nb * 512:(nb + 1) * 512],
                    psC[0:1, 512:1024],
                    bo_f32[0:1, nb * 512:(nb + 1) * 512], op=OP.add),
                    wait=wlf(tm, tbo if nb == 0 else None), inc=[("V", 1)])
            ph.do("vector", lambda e: e.tensor_copy(brow2_bf[:], brow2_f32[:]),
                  wait=[tprev], inc=[("V", 1)])
            ph.emit()

    names = ["p0", "p1", "p2", "p3", "p4", "p5", "p6", "p7", "p8", "p9", "p10"]
    n_run = len(names) if n_phases is None else n_phases
    redir = names[n_run - 1] if (dev_dump == "redir" and n_run < len(names)) else None

    def xout(name, default):
        return out if name == redir else default

    if n_run >= 1:
        p0()
    if n_run >= 2:
        with (
            nc.sbuf_tensor("q_cF", [128, 12 * S], BF16) as q_cF,
            nc.sbuf_tensor("k_cF", [128, 4 * S], BF16) as k_cF,
            nc.sbuf_tensor("krF_bf", [128, S], BF16) as krF_bf,
        ):
            p1(q_cF, k_cF, krF_bf)
            if n_run >= 3:
                p2(q_cF, k_cF, krF_bf)
    if n_run >= 4:
        emit_epilogue("p3", wo_a, False, x0, xout("p3", xs_a))
    if n_run >= 5:
        with (
            nc.sbuf_tensor("t_bf", [128, S], BF16) as t_bf,
            nc.sbuf_tensor("routing_vec", [128, 1], F32) as routing_vec,
        ):
            make_f1("p4", xs_a, t_bf, routing_vec)()
            if n_run >= 6:
                with nc.sbuf_tensor("bt_sb", [128, D], BF16) as bt_sb:
                    if DEV_P5_STRIP == 1:
                        emit_epilogue("p5", wt_scr, True, xs_a, xout("p5", xs_b))
                    elif DEV_P5_STRIP == 2:
                        emit_epilogue("p5", wt_scr, True, xs_a, xout("p5", xs_b),
                                      pre=make_f2_pre(bt_sb))
                    elif DEV_P5_STRIP == 3:
                        emit_epilogue("p5", wt_scr, True, xs_a, xout("p5", xs_b),
                                      bias_src=fb, pre=make_f2_pre(bt_sb))
                    else:
                        emit_epilogue("p5", wt_scr, True, xs_a, xout("p5", xs_b),
                                      bias_src=fb, lora=(t_bf, bt_sb),
                                      pre=make_f2_pre(bt_sb))
    if n_run >= 7:
        with (
            nc.sbuf_tensor("qpack", [128, 8 * S], BF16) as qpack,
            nc.sbuf_tensor("ckv_pack", [128, S], BF16) as ckv_pack,
            nc.sbuf_tensor("cq_bf", [12, S], BF16) as cq_bf,
            nc.sbuf_tensor("ckv_bf", [4, S], BF16) as ckv_bf,
            nc.sbuf_tensor("bvu_sb", [128, 32], F32) as bvu_sb,
            nc.sbuf_tensor("wvu5_all", [128, 160], BF16) as wvu5_all,
            nc.sbuf_tensor("ckv_tok", [128, 64], BF16) as ckv_tok,
            nc.sbuf_tensor("zn_bf", [128, S], BF16) as zn_bf,
            nc.sbuf_tensor("M_bf", [128, D], BF16) as M_bf,
            nc.sbuf_tensor("bvw_parts", [32, D], BF16) as bvw_parts,
            nc.sbuf_tensor("brow2_bf", [1, D], BF16) as brow2_bf,
        ):
            p6(qpack, ckv_pack, cq_bf, ckv_bf, bvu_sb, wvu5_all, ckv_tok)
            if n_run >= 8:
                p7(qpack, ckv_pack, wvu5_all, ckv_tok, zn_bf, M_bf,
                   bvw_parts, brow2_bf)
            if n_run >= 9:
                emit_mini_epilogue("p8", zn_bf, M_bf, brow2_bf, xs_b,
                                   xout("p8", xs_a))
    if n_run >= 10:
        with (
            nc.sbuf_tensor("t_bfb", [128, S], BF16) as t_bf,
            nc.sbuf_tensor("routing_vecb", [128, 1], F32) as routing_vec,
        ):
            make_f1("p9", xs_a, t_bf, routing_vec)()
            if n_run >= 11:
                with nc.sbuf_tensor("bt_sbb", [128, D], BF16) as bt_sb:
                    emit_epilogue("p10", wt_scr, True, xs_a, out,
                                  bias_src=fb, lora=(t_bf, bt_sb),
                                  pre=make_f2_pre(bt_sb, "b"))

    if dev_dump == "victim":
        vbig = nc.dram_tensor("vbig", [128, 32768], F32, kind="ExternalInput").ap()
        d_v = nc.dram_tensor("d_v", [128, 2048], F32, kind="ExternalOutput").ap()
        phv = Ph(k, "victim")
        with ExitStack() as es:
            vb = es.enter_context(nc.sbuf_tensor("v_b", [128, 32768], F32))
            vo = es.enter_context(nc.sbuf_tensor("v_o", [128, 2048], F32))
            slack = 0
            phv.do("sync", lambda e: e.dma_start(out=vb[:], in_=vbig), inc=[("Q", 16)])
            if slack:
                phv.do("sync", lambda e: e.dma_start(out=vo[:], in_=vbig[:, 0:2048]),
                       inc=[("Q", 16)])
            wv = phv.w("Q")
            phv.do("vector", lambda e: e.tensor_copy(vo[:], vb[:, 30720:32768]),
                   wait=wv, inc=[("V", 1)])
            phv.do("gpsimd", lambda e: e.dma_start(out=d_v, in_=vo[:]),
                   wait=phv.w("V"), inc=[("G", 16)])
            phv.emit()

    if dev_dump == "tls":
        ph = Ph(k, "dump_tls")
        dt_ = nc.dram_tensor("dbg_tls", [NT, 128, D], BF16, kind="ExternalOutput").ap()
        for tt in range(NT):
            ph.do("gpsimd", lambda e, tt=tt: e.dma_start(out=dt_[tt], in_=tls[tt]),
                  wait=ph.w("G", "Sc", "V", "T"), inc=[("G", 16)])
        ph.emit()

    ph = Ph(k, "fin")
    ph.do("sync", lambda e: e.nop(),
          wait=ph.w("Q", "Adma", "G", "V", "Sc", "T", "Gc"))
    ph.do("gpsimd", lambda e: e.nop(),
          wait=ph.w("Q", "Adma", "G", "V", "Sc", "T", "Gc"))
    ph.emit()
    return nc


def _consts():
    t = np.arange(S, dtype=np.float64)
    inv = 1.0 / (10000.0 ** (np.arange(0, RD, 2, dtype=np.float64) / RD))
    f = np.concatenate([inv, inv])
    ang = np.outer(f, t)
    cos64 = np.cos(ang)
    sin64 = np.sin(ang)
    sinm = np.concatenate([-sin64[0:32], sin64[32:64]], axis=0)
    cos128 = np.concatenate([cos64, cos64], axis=0).astype(np.float32)
    sinm128 = np.concatenate([sinm, sinm], axis=0).astype(np.float32)
    e8 = np.zeros((8, 128), dtype=np.float32)
    for e in range(8):
        e8[e, e * 16:(e + 1) * 16] = 1.0
    return {
        "c_id_bf": np.eye(128, dtype=np.float32).astype(ml_dtypes.bfloat16),
        "c_id_f32": np.eye(128, dtype=np.float32),
        "c_cos": cos128,
        "c_sinm": sinm128,
        "c_ones_bf": np.ones((1, 128), dtype=np.float32).astype(ml_dtypes.bfloat16),
        "c_e8": e8.astype(ml_dtypes.bfloat16),
    }


def host_inputs(inputs):
    consts = _consts()
    common = dict(
        wqa=inputs["L0_attn_a_w_qa"], gqa=inputs["L0_attn_a_g_qa"],
        wqb=inputs["L0_attn_a_w_qb"], wkva=inputs["L0_attn_a_w_kva"],
        gkva=inputs["L0_attn_a_g_kva"], wkvb=inputs["L0_attn_a_w_kvb"],
        wo_a=inputs["L0_attn_a_w_o"],
        bWqd=inputs["L0_attn_b_Wq_d"], bbqd=inputs["L0_attn_b_bq_d"],
        bWqk=inputs["L0_attn_b_W_qk"], bbqk=inputs["L0_attn_b_b_qk"],
        bWkvd=inputs["L0_attn_b_Wkv_d"], bbkvd=inputs["L0_attn_b_bkv_d"],
        bWvu=inputs["L0_attn_b_Wv_u"], bbvu=inputs["L0_attn_b_bv_u"],
        bWo=inputs["L0_attn_b_Wo"], bbo=inputs["L0_attn_b_bo"],
        fW=inputs["L0_ffn_W"], fb=inputs["L0_ffn_b"],
        fA=np.ascontiguousarray(inputs["L0_ffn_A"], dtype=np.float32).reshape(128, D),
        fB=inputs["L0_ffn_B"], frW=inputs["L0_ffn_router_W"],
        frb=inputs["L0_ffn_router_b"],
        **consts,
    )
    return {kk: np.ascontiguousarray(v) for kk, v in common.items()}


def kernel(**inputs):
    x = np.ascontiguousarray(inputs["L0_attn_a_hidden_states"], dtype=np.float32)
    common = host_inputs(inputs)
    nc = build_nc(PHASE_LIMIT, DEV_DUMP)
    in_maps = [dict(common, x=np.ascontiguousarray(x[b])) for b in range(8)]
    res = run_bass_kernel_spmd(nc, in_maps, core_ids=list(range(8)))
    return np.stack([res.results[b]["out"] for b in range(8)], axis=0)


# version 21 (final)
# speedup vs baseline: 1.1629x; 1.0167x over previous
import numpy as np
import ml_dtypes
from contextlib import ExitStack
import concourse.bass as bass
import concourse.mybir as mybir
from concourse.bass_utils import run_bass_kernel_spmd

F32 = mybir.dt.float32
BF16 = mybir.dt.bfloat16
AF = mybir.ActivationFunctionType
AX = mybir.AxisListType
OP = mybir.AluOpType

S = 2048
D = 4096
NT = 16
DCH = 32
NH = 32
QR = 1536
KVR = 512
RD = 64
QK_SCALE = float(192.0 ** -0.5)
LN_EPS = 1e-5
RMS_EPS = 1e-6
Q_LO, Q_HI = 611, 2047
ROUTER_CNT = float(Q_HI - Q_LO)

ENGINES = ("sync", "scalar", "vector", "tensor", "gpsimd")
DMA_QUEUES = ("Q", "Adma", "G")
DMA_POOL = 8

PHASE_LIMIT = None
DEV_P5_STRIP = 0
DEV_DUMP = None


class Ctx:
    def __init__(self, nc):
        self.nc = nc
        self.sems = {}
        self.cnt = {}
        self.own = {}

    def sem(self, name, owner=None):
        h = self.nc.semaphore(name).__enter__()
        self.sems[name] = h
        self.cnt[name] = 0
        if owner:
            self.own[name] = owner


class Ph:

    def __init__(self, k, name, barrier=True):
        self.k = k
        self.name = name
        self.q = {e: [] for e in ENGINES}
        self.last_wait = {e: {} for e in ENGINES}
        self.bar = {s: k.cnt[s] for s in k.cnt} if barrier else None

    def do(self, eng, fn, wait=(), inc=()):
        k = self.k
        wait = list(wait)
        inc2 = []
        for s, n in inc:
            if s in DMA_QUEUES:
                j = k.rr[s]
                k.rr[s] = (j + 1) % DMA_POOL
                ps = f"{s}{j}"
                wait.append((ps, k.cnt[ps]))
                inc2.append((ps, n))
            else:
                inc2.append((s, n))
        inc = inc2
        w = []
        for s, v in wait:
            assert v <= k.cnt[s], f"{self.name}: wait {s}>={v} > planned {k.cnt[s]}"
            if self.last_wait[eng].get(s, -1) >= v:
                continue
            self.last_wait[eng][s] = v
            w.append((s, v))
        ii = []
        tok = None
        for s, n in inc:
            k.cnt[s] += n
            ii.append((s, n))
            if tok is None:
                tok = (s, k.cnt[s])
        self.q[eng].append((fn, w, ii))
        return tok

    def w(self, *sems):
        out = []
        for s in sems:
            if s in DMA_QUEUES:
                for j in range(DMA_POOL):
                    ps = f"{s}{j}"
                    out.append((ps, self.k.cnt[ps]))
            else:
                out.append((s, self.k.cnt[s]))
        return out

    def emit(self):
        k = self.k
        with k.nc.Block(self.name) as blk:
            for e in ENGINES:
                steps = self.q[e]
                if not steps:
                    continue

                def run(eng, steps=steps, e_name=e):
                    if self.bar is not None:
                        for s, v in self.bar.items():
                            if v > 0 and self.last_wait[e_name].get(s, -1) < v:
                                self.last_wait[e_name][s] = v
                                eng.wait_ge(k.sems[s], v)
                    for fn, waits, incs in steps:
                        for s, v in waits:
                            eng.wait_ge(k.sems[s], v)
                        ins = fn(eng)
                        for s, n in incs:
                            ins.then_inc(k.sems[s], n)
                getattr(blk, e)(run)


def mm_cols(n):
    out, c = [], 0
    while c < n:
        w = min(512, n - c)
        out.append((c, w))
        c += w
    return out


def build_nc(n_phases=None, dev_dump=None):
    nc = bass.Bass("TRN2", target_bir_lowering=False, debug=False)
    k = Ctx(nc)

    def din(name, shape, dt=F32):
        return nc.dram_tensor(name, shape, dt, kind="ExternalInput").ap()

    x0 = din("x", [S, D])
    wqa = din("wqa", [D, QR])
    gqa = din("gqa", [QR])
    wqb = din("wqb", [QR, NH * 192])
    wkva = din("wkva", [D, KVR + RD])
    gkva = din("gkva", [KVR])
    wkvb = din("wkvb", [KVR, NH * 256])
    wo_a = din("wo_a", [D, D])
    bWqd = din("bWqd", [D, 12])
    bbqd = din("bbqd", [12])
    bWqk = din("bWqk", [12, 128])
    bbqk = din("bbqk", [128])
    bWkvd = din("bWkvd", [D, 4])
    bbkvd = din("bbkvd", [4])
    bWvu = din("bWvu", [4, D])
    bbvu = din("bbvu", [D])
    bWo = din("bWo", [D, D])
    bbo = din("bbo", [D])
    fW = din("fW", [D, D])
    fb = din("fb", [D])
    fA = din("fA", [128, D])
    fB = din("fB", [8, D, 16])
    frW = din("frW", [8, D])
    frb = din("frb", [8])
    c_id_bf = din("c_id_bf", [128, 128], BF16)
    c_id_f32 = din("c_id_f32", [128, 128], F32)
    c_cos = din("c_cos", [128, S])
    c_sinm = din("c_sinm", [128, S])
    c_ones_bf = din("c_ones_bf", [1, 128], BF16)
    c_e8 = din("c_e8", [8, 128], BF16)

    out = nc.dram_tensor("out", [S, D], F32, kind="ExternalOutput").ap()

    xs_a = nc.dram_tensor("xs_a", [S, D], F32, kind="Internal").ap()
    xs_b = nc.dram_tensor("xs_b", [S, D], F32, kind="Internal").ap()
    tls = nc.dram_tensor("tls", [NT, 128, D], BF16, kind="Internal").ap()
    wt_scr = nc.dram_tensor("wt_scr", [D, D], BF16, kind="Internal").ap()

    for q in DMA_QUEUES:
        for j in range(DMA_POOL):
            k.sem(f"{q}{j}")
    for s_name, owner in [("V", "vector"), ("Sc", "scalar"), ("T", "tensor"),
                          ("Gc", "gpsimd")]:
        k.sem(s_name, owner)
    k.rr = {q: 0 for q in DMA_QUEUES}
    nc.clear_and_free_semaphores(list(k.sems.values()))
    nc.all_engine_barrier()

    id_bf = nc.alloc_sbuf_tensor("id_bf", [128, 128], BF16)
    id_f32 = nc.alloc_sbuf_tensor("id_f32", [128, 128], F32)
    ones_bf = nc.alloc_sbuf_tensor("ones_bf", [1, 128], BF16)
    rcpT_a = nc.dram_tensor("rcp_scr", [32, S], F32, kind="Internal").ap()

    psA = nc.alloc_psum_tensor("psA", [128, 1024], F32)
    psB = nc.alloc_psum_tensor("psB", [128, 1024], F32)
    psC = nc.alloc_psum_tensor("psC", [128, 1024], F32)
    ptA = nc.alloc_psum_tensor("ptA", [128, 1024], BF16)
    ptB = nc.alloc_psum_tensor("ptB", [128, 1024], BF16)

    def tr16(ph, src_bf, n, dst_fn, wait_extra=()):
        for rnd in range((n + 7) // 8):
            pt = ptA if rnd % 2 == 0 else ptB
            lo, hi = rnd * 8, min(rnd * 8 + 8, n)

            def tr(e, lo=lo, hi=hi, pt=pt):
                for j in range(hi - lo):
                    c = lo + j
                    ins = nc.tensor.transpose(pt[:, j * 128:(j + 1) * 128],
                                              src_bf[:, c * 128:(c + 1) * 128], id_bf[:])
                return ins
            ph.do("tensor", tr, wait=ph.w("V", "Sc") + list(wait_extra), inc=[("T", 1)])
            ph.do("scalar", lambda e, rnd=rnd, pt=pt, lo=lo, hi=hi: dst_fn(e, rnd, pt, lo, hi),
                  wait=ph.w("T"), inc=[("Sc", 1)])

    def new_stat(prefix, es):
        names = ["ssq", "sm", "mean", "var", "var2", "std", "rstd", "nmr"]
        return {n: es.enter_context(nc.sbuf_tensor(f"{prefix}_{n}", [128, 1], F32))
                for n in names}

    def emit_ln_tile(ph, x_dram, tt, xt_f32, junk_bf, stat, xln_bf, xlt_dst):
        ph.do("sync", lambda e: e.dma_start(
            out=xt_f32, in_=x_dram[tt * 128:(tt + 1) * 128, :]),
            wait=ph.w("V", "Sc"), inc=[("Q", 16)])
        ph.do("scalar", lambda e: e.activation(junk_bf[:], xt_f32, AF.Square,
                                               accum_out=stat["ssq"][:]),
              wait=ph.w("Q", "V", "T"), inc=[("Sc", 1)])
        ph.do("vector", lambda e: e.reduce_sum(stat["sm"][:], xt_f32, axis=AX.X),
              wait=ph.w("Q", "Sc"), inc=[("V", 1)])

        def s1(e):
            e.tensor_scalar_mul(stat["mean"][:], stat["sm"][:], 1.0 / D)
            return e.tensor_scalar(stat["var2"][:], stat["ssq"][:], 1.0 / D, None,
                                   op0=OP.mult)
        ph.do("vector", s1, wait=ph.w("Sc", "V"), inc=[("V", 1)])
        ph.do("vector", lambda e: e.tensor_mul(stat["var"][:], stat["mean"][:],
                                               stat["mean"][:]),
              wait=ph.w("V"), inc=[("V", 1)])
        ph.do("vector", lambda e: e.tensor_sub(stat["var"][:], stat["var2"][:],
                                               stat["var"][:]),
              wait=ph.w("V"), inc=[("V", 1)])
        ph.do("vector", lambda e: e.tensor_scalar_add(stat["var"][:], stat["var"][:],
                                                      LN_EPS),
              wait=ph.w("V"), inc=[("V", 1)])
        ph.do("scalar", lambda e: e.activation(stat["std"][:], stat["var"][:], AF.Sqrt),
              wait=ph.w("V"), inc=[("Sc", 1)])
        ph.do("vector", lambda e: e.reciprocal(stat["rstd"][:], stat["std"][:]),
              wait=ph.w("Sc", "V"), inc=[("V", 1)])
        ph.do("vector", lambda e: e.tensor_mul(stat["nmr"][:], stat["mean"][:],
                                               stat["rstd"][:]),
              wait=ph.w("V"), inc=[("V", 1)])
        ph.do("vector", lambda e: e.tensor_scalar_mul(stat["nmr"][:], stat["nmr"][:],
                                                      -1.0),
              wait=ph.w("V"), inc=[("V", 1)])
        ph.do("vector", lambda e: e.tensor_scalar(xln_bf[:], xt_f32, stat["rstd"][:],
                                                  stat["nmr"][:], op0=OP.mult,
                                                  op1=OP.add),
              wait=ph.w("V", "T"), inc=[("V", 1)])

        def mk_dst(e, rnd, pt, lo, hi):
            return e.copy(xlt_dst[:, lo * 128:hi * 128], pt[:, 0:(hi - lo) * 128])
        tr16(ph, xln_bf, 32, mk_dst)

    def p0():
        ph = Ph(k, "p0")
        ph.do("sync", lambda e: e.dma_start(out=id_bf[:], in_=c_id_bf), inc=[("Q", 16)])
        ph.do("sync", lambda e: e.dma_start(out=id_f32[:], in_=c_id_f32), inc=[("Q", 16)])
        ph.do("sync", lambda e: e.dma_start(out=ones_bf[:], in_=c_ones_bf), inc=[("Q", 16)])
        ph.emit()

    def wt_gen(ph, es, pt_free):
        wlf = lambda *t: [x for x in t if x]
        wrow_f32 = es.enter_context(nc.sbuf_tensor("p0_wrow", [128, 2 * D], F32))
        wrow_bf = es.enter_context(nc.sbuf_tensor("p0_wrowb", [128, 2 * D], BF16))
        wt_stage = es.enter_context(nc.sbuf_tensor("p0_wts", [128, 2 * D], BF16))
        t_conv = {}; t_tr = {}; t_ev = {}; t_st = {}
        for rb in range(DCH):
            sl = (rb % 2) * D
            tl_ = ph.do("sync", lambda e, rb=rb, sl=sl: e.dma_start(
                out=wrow_f32[:, sl:sl + D], in_=fW[rb * 128:(rb + 1) * 128, :]),
                wait=wlf(t_conv.get(rb - 2)), inc=[("Q", 16)])
            yield
            t_conv[rb] = ph.do("vector", lambda e, sl=sl: e.tensor_copy(
                wrow_bf[:, sl:sl + D], wrow_f32[:, sl:sl + D]),
                wait=wlf(tl_, t_tr.get((rb - 2, 3))), inc=[("V", 1)])
            yield
            for rnd in range(4):
                ptk = rnd % 2
                pt = ptA if ptk == 0 else ptB

                def tr(e, rnd=rnd, pt=pt, sl=sl):
                    for j in range(8):
                        c = rnd * 8 + j
                        ins = nc.tensor.transpose(
                            pt[:, j * 128:(j + 1) * 128],
                            wrow_bf[:, sl + c * 128: sl + (c + 1) * 128],
                            id_bf[:])
                    return ins
                t_tr[(rb, rnd)] = ph.do("tensor", tr, wait=wlf(
                    t_conv[rb], pt_free.get(ptk)), inc=[("T", 1)])
                t_ev[(rb, rnd)] = ph.do(
                    "scalar", lambda e, rnd=rnd, pt=pt, sl=sl:
                    e.copy(wt_stage[:, sl + rnd * 1024: sl + (rnd + 1) * 1024],
                           pt[:]),
                    wait=wlf(t_tr[(rb, rnd)]) + list(t_st.get(rb - 2, [])),
                    inc=[("Sc", 1)])
                pt_free[ptk] = t_ev[(rb, rnd)]
                yield
            dst = wt_scr[:, rb * 128:(rb + 1) * 128].rearrange(
                "(c p) j -> p c j", p=128)
            src3 = wt_stage[:, sl:sl + D].rearrange("p (c j) -> p c j", c=32)
            sts = []
            for q4 in range(4):
                sts.append(ph.do("gpsimd", lambda e, dst=dst, src3=src3, q4=q4:
                                 e.dma_start(out=dst[:, q4 * 8:(q4 + 1) * 8, :],
                                             in_=src3[:, q4 * 8:(q4 + 1) * 8, :]),
                                 wait=[t_ev[(rb, q4)]], inc=[("G", 16)]))
            t_st[rb] = sts
            yield

    def ln_tile_tok(ph, x_dram, tt, xt, stat, xln, dst_fn, pt_free, t_prev):
        wlf = lambda *t: [x for x in t if x]
        tload = ph.do("sync", lambda e: e.dma_start(
            out=xt, in_=x_dram[tt * 128:(tt + 1) * 128, :]),
            wait=wlf(t_prev.get("xt")), inc=[("Adma", 16)])
        tsq = ph.do("scalar", lambda e: e.activation(
            xln[:], xt, AF.Square, accum_out=stat["ssq"][:]),
            wait=wlf(tload, t_prev.get("xln"), t_prev.get("stat")),
            inc=[("Sc", 1)])
        tsum = ph.do("vector", lambda e: e.reduce_sum(stat["sm"][:], xt, axis=AX.X),
                     wait=wlf(tload, t_prev.get("stat")), inc=[("V", 1)])

        def s1(e):
            e.tensor_scalar_mul(stat["mean"][:], stat["sm"][:], 1.0 / D)
            return e.tensor_scalar(stat["var2"][:], stat["ssq"][:], 1.0 / D, None,
                                   op0=OP.mult)
        ts1 = ph.do("vector", s1, wait=wlf(tsum, tsq), inc=[("V", 1)])
        tva = ph.do("vector", lambda e: e.scalar_tensor_tensor(
            stat["var"][:], stat["mean"][:], stat["mean"][:], stat["var2"][:],
            op0=OP.mult, op1=OP.subtract),
            wait=[ts1], inc=[("V", 1)])
        tvb = ph.do("vector", lambda e: e.tensor_scalar(
            stat["var"][:], stat["var"][:], -1.0, LN_EPS, op0=OP.mult, op1=OP.add),
            wait=[tva], inc=[("V", 1)])
        trs = ph.do("scalar", lambda e: e.activation(
            stat["std"][:], stat["var"][:], AF.Sqrt),
            wait=[tvb], inc=[("Sc", 1)])
        trc = ph.do("vector", lambda e: e.reciprocal(
            stat["rstd"][:], stat["std"][:]),
            wait=[trs], inc=[("V", 1)])
        tnm = ph.do("vector", lambda e: e.tensor_scalar(
            stat["nmr"][:], stat["mean"][:], stat["rstd"][:], -1.0,
            op0=OP.mult, op1=OP.mult),
            wait=[trc], inc=[("V", 1)])
        txl = ph.do("vector", lambda e: e.tensor_scalar(
            xln[:], xt, stat["rstd"][:], stat["nmr"][:], op0=OP.mult, op1=OP.add),
            wait=wlf(tnm, tsq), inc=[("V", 1)])
        trlast, evlast = None, None
        for rnd in range(4):
            ptk = rnd % 2
            pt = ptA if ptk == 0 else ptB

            def tr(e, rnd=rnd, pt=pt):
                for j in range(8):
                    c = rnd * 8 + j
                    ins = nc.tensor.transpose(pt[:, j * 128:(j + 1) * 128],
                                              xln[:, c * 128:(c + 1) * 128], id_bf[:])
                return ins
            trlast = ph.do("tensor", tr,
                           wait=wlf(txl, pt_free.get(ptk)), inc=[("T", 1)])
            dfree = t_prev.get("dst") if rnd == 0 else None
            dfree = dfree if isinstance(dfree, list) else ([dfree] if dfree else [])
            evlast = ph.do("scalar", lambda e, rnd=rnd, pt=pt: dst_fn(e, rnd, pt),
                           wait=wlf(trlast) + dfree,
                           inc=[("Sc", 1)])
            pt_free[ptk] = evlast
        return {"load": tload, "xln": trlast, "sq": txl, "evlast": evlast}

    def p1(q_cF, k_cF, krF_bf):
        ph = Ph(k, "p1")
        es_out = ExitStack()
        with es_out:
            krF_f32 = es_out.enter_context(nc.sbuf_tensor("p1_krf", [64, S], BF16))
            gqa_bc = es_out.enter_context(nc.sbuf_tensor("p1_gqa", [128, QR], BF16))
            gkva_bc = es_out.enter_context(nc.sbuf_tensor("p1_gkva", [128, KVR], BF16))
            pha = Ph(k, "p1a")
            with ExitStack() as es:
                grow = es.enter_context(nc.sbuf_tensor("p1_grow", [1, QR], F32))
                grow_bf = es.enter_context(nc.sbuf_tensor("p1_growb", [1, QR], BF16))

                def bcast_gain(vecap, n, dst):
                    pha.do("sync", lambda e: e.dma_start(out=grow[0:1, 0:n],
                                                         in_=vecap[None, :]),
                           wait=pha.w("V"), inc=[("Q", 16)])
                    pha.do("vector", lambda e: e.tensor_copy(grow_bf[0:1, 0:n],
                                                             grow[0:1, 0:n]),
                           wait=pha.w("Q"), inc=[("V", 1)])
                    for c0, w in mm_cols(n):
                        pha.do("tensor", lambda e, c0=c0, w=w: nc.tensor.matmul(
                            psA[:, 0:w], ones_bf[0:1, 0:128], grow_bf[0:1, c0:c0 + w],
                            start=True, stop=True), wait=pha.w("V", "Sc"), inc=[("T", 1)])
                        pha.do("scalar", lambda e, c0=c0, w=w, dst=dst: e.copy(
                            dst[:, c0:c0 + w], psA[:, 0:w]), wait=pha.w("T"),
                            inc=[("Sc", 1)])
                bcast_gain(gqa, QR, gqa_bc)
                bcast_gain(gkva, KVR, gkva_bc)
                pha.emit()
            p1_main(ph, q_cF, k_cF, krF_f32, gqa_bc, gkva_bc)
            ph2 = Ph(k, "p1b")
            with ExitStack() as es:
                c_cs = es.enter_context(nc.sbuf_tensor("p1_cs", [64, 2 * S], F32))
                kr_t1 = es.enter_context(nc.sbuf_tensor("p1_kt1", [64, S], F32))
                kr_t2 = es.enter_context(nc.sbuf_tensor("p1_kt2", [64, S], F32))
                kr_sw = es.enter_context(nc.sbuf_tensor("p1_ksw", [64, S], F32))
                ph2.do("sync", lambda e: e.dma_start(out=c_cs[:, 0:S], in_=c_cos[0:64, :]),
                       inc=[("Q", 16)])
                ph2.do("sync", lambda e: e.dma_start(out=c_cs[:, S:2 * S],
                                                     in_=c_sinm[0:64, :]),
                       inc=[("Q", 16)])

                def kr1(e):
                    e.tensor_copy(kr_sw[0:32, :], krF_f32[32:64, :])
                    e.tensor_copy(kr_sw[32:64, :], krF_f32[0:32, :])
                    return e.tensor_copy(kr_t1[:], krF_f32[:])
                ph2.do("vector", kr1, wait=ph2.w("Q", "Sc", "V"), inc=[("V", 1)])

                def kr2(e):
                    e.tensor_mul(kr_t1[:], kr_t1[:], c_cs[:, 0:S])
                    return e.tensor_mul(kr_t2[:], kr_sw[:], c_cs[:, S:2 * S])
                ph2.do("vector", kr2, wait=ph2.w("V"), inc=[("V", 1)])
                ph2.do("vector", lambda e: e.tensor_add(krF_bf[0:64, :], kr_t1[:],
                                                        kr_t2[:]),
                       wait=ph2.w("V"), inc=[("V", 1)])
                ph2.do("vector", lambda e: e.tensor_copy(krF_bf[64:128, :],
                                                         krF_bf[0:64, :]),
                       wait=ph2.w("V"), inc=[("V", 1)])
                if dev_dump == "p1":
                    dq = nc.dram_tensor("dbg_qcf", [128, 12 * S], BF16,
                                        kind="ExternalOutput").ap()
                    dk = nc.dram_tensor("dbg_kcf", [128, 4 * S], BF16,
                                        kind="ExternalOutput").ap()
                    dr = nc.dram_tensor("dbg_krf", [64, S], BF16,
                                        kind="ExternalOutput").ap()
                    ph2.do("gpsimd", lambda e: e.dma_start(out=dq, in_=q_cF[:]),
                           wait=ph2.w("Sc", "V"), inc=[("G", 16)])
                    ph2.do("gpsimd", lambda e: e.dma_start(out=dk, in_=k_cF[:]),
                           wait=ph2.w("Sc", "V"), inc=[("G", 16)])
                    ph2.do("gpsimd", lambda e: e.dma_start(out=dr, in_=krF_bf[0:64, :]),
                           wait=ph2.w("Sc", "V"), inc=[("G", 16)])
                ph2.emit()

    def p1_main(ph, q_cF, k_cF, krF_f32, gqa_bc, gkva_bc):
        wlf = lambda *t: [x for x in t if x]
        with ExitStack() as es:
            pt_free = {}
            xlh = es.enter_context(nc.sbuf_tensor("p1_xlh", [128, 4 * D], BF16))
            xt_f32 = es.enter_context(nc.sbuf_tensor("p1_xt", [128, 2 * D], F32))
            xln_bf = es.enter_context(nc.sbuf_tensor("p1_xln", [128, 2 * D], BF16))
            wpc_f32 = es.enter_context(nc.sbuf_tensor("p1_wpf", [128, 4 * 384], F32))
            wpc_bf = es.enter_context(nc.sbuf_tensor("p1_wpb", [128, 32 * 384], BF16))
            qstage = es.enter_context(nc.sbuf_tensor("p1_qst", [128, 4 * QR], BF16))
            q_ssq = es.enter_context(nc.sbuf_tensor("p1_qssq", [128, 4], F32))
            sspart = es.enter_context(nc.sbuf_tensor("p1_ssp", [128, 2], F32))
            rms_r = es.enter_context(nc.sbuf_tensor("p1_rr", [128, 8], F32))
            kc_f32 = es.enter_context(nc.sbuf_tensor("p1_kc", [128, 4 * KVR], BF16))
            kc_bf = es.enter_context(nc.sbuf_tensor("p1_kcb", [128, KVR], BF16))
            krtmp = es.enter_context(nc.sbuf_tensor("p1_krt", [128, RD], F32))
            qn_bf = es.enter_context(nc.sbuf_tensor("p1_qnb", [128, QR], BF16))
            stats = [new_stat("p1s%d" % i, es) for i in range(2)]

            ln_prev = [{}, {}]
            dst_free = {}
            mm_tok = {}
            ev_tok = {}
            stat_free = {}
            pieces = [("qa", 0, 384), ("qa", 384, 384), ("qa", 768, 384),
                      ("qa", 1152, 384), ("kv", 0, 288), ("kv", 288, 288)]
            NP = len(pieces)
            ln_done = {}
            kr_done = []
            cv_prev = {}

            def emit_ln(quarter):
                for tl, tt in enumerate(range(4 * quarter, 4 * quarter + 4)):
                    par = tl % 2
                    prev = dict(ln_prev[par])
                    prev["dst"] = dst_free.get(tl)
                    prev["stat"] = stat_free.get(par)
                    prev["xln"] = ev_tok.get(("xln", par))

                    def mk_dst(e, rnd, pt, tl=tl):
                        return e.copy(xlh[:, tl * D + rnd * 1024:
                                          tl * D + (rnd + 1) * 1024], pt[:])
                    r = ln_tile_tok(ph, x0, tt, xt_f32[:, par * D:(par + 1) * D],
                                    stats[par], xln_bf[:, par * D:(par + 1) * D],
                                    mk_dst, pt_free, prev)
                    ln_prev[par] = {"xt": r["sq"]}
                    ev_tok[("xln", par)] = r["xln"]
                    stat_free[par] = r["sq"]
                    ln_done[tl] = r["evlast"]

            emit_ln(0)
            for quarter in range(4):
                for pi, (pname, c0, pw) in enumerate(pieces):
                    wsrc = wqa if pname == "qa" else wkva
                    qname = "Q" if pname == "qa" else "G"
                    prev_pi = pi - 1 if pi > 0 else NP - 1
                    prev_pw = pieces[prev_pi][2]
                    cv = {-4: cv_prev.get(28), -3: cv_prev.get(29),
                          -2: cv_prev.get(30), -1: cv_prev.get(31)}
                    for dc in range(DCH):
                        b0 = (dc % 4) * 384
                        if qname == "Q":
                            eng, qq = (("sync", "Q") if dc % 2 == 0
                                       else ("scalar", "Adma"))
                        else:
                            eng, qq = "gpsimd", "G"
                        tl_ = ph.do(eng,
                                    lambda e, dc=dc, b0=b0, wsrc=wsrc, c0=c0, pw=pw:
                                    e.dma_start(
                                        out=wpc_f32[:, b0:b0 + pw],
                                        in_=wsrc[dc * 128:(dc + 1) * 128,
                                                 c0:c0 + pw]),
                                    wait=wlf(cv.get(dc - 4)), inc=[(qq, 16)])
                        g_req = (dc + 1) * pw + 4 * prev_pw - 1
                        g_req = min(7, max(0, g_req // (4 * prev_pw) - 1))
                        wfree = mm_tok.get((prev_pi, 3))
                        wfree = wfree[g_req] if wfree else None
                        cv[dc] = ph.do("vector",
                                       lambda e, dc=dc, b0=b0, pw=pw: e.tensor_copy(
                                           wpc_bf[:, dc * pw:dc * pw + pw],
                                           wpc_f32[:, b0:b0 + pw]),
                                       wait=wlf(tl_, wfree), inc=[("V", 1)])
                    for tl in range(4):
                        ps = psA if tl % 2 == 0 else psB

                        def mmg(e, tl, pw, g8):
                            pps = psA if tl % 2 == 0 else psB
                            for dc in range(g8 * 4, g8 * 4 + 4):
                                lhs = xlh[:, tl * D + dc * 128:
                                          tl * D + (dc + 1) * 128]
                                ins = nc.tensor.matmul(
                                    pps[:, 0:pw], lhs,
                                    wpc_bf[:, dc * pw:(dc + 1) * pw],
                                    start=(dc == 0), stop=(dc == DCH - 1))
                            return ins
                        subs = []
                        for g8 in range(8):
                            w = wlf(cv[g8 * 4 + 3])
                            if g8 == 0:
                                w += wlf(ln_done.get(tl),
                                         ev_tok.get(("psum", tl % 2)))
                            subs.append(ph.do(
                                "tensor",
                                lambda e, tl=tl, pw=pw, g8=g8: mmg(e, tl, pw, g8),
                                wait=w, inc=[("T", 1)]))
                        tB = subs[-1]
                        mm_tok[(pi, tl)] = subs
                        if tl == 0:
                            cv_prev = cv
                        tt = 4 * quarter + tl
                        if pname == "qa":
                            ppi = c0 // 384
                            tsq = ph.do("scalar", lambda e, ps=ps, pw=pw, tl=tl:
                                        e.activation(
                                            qn_bf[:, 0:pw], ps[:, 0:pw],
                                            AF.Square,
                                            accum_out=sspart[:, tl % 2:tl % 2 + 1]),
                                        wait=wlf(tB, ev_tok.get(("sq", tl % 2)),
                                                 ev_tok.get(("qn",))),
                                        inc=[("Sc", 1)])

                            def qacc(e, tl=tl, ppi=ppi, c0=c0, pw=pw, ps=ps):
                                if ppi == 0:
                                    e.tensor_copy(q_ssq[:, tl:tl + 1],
                                                  sspart[:, tl % 2:tl % 2 + 1])
                                else:
                                    e.tensor_add(q_ssq[:, tl:tl + 1],
                                                 q_ssq[:, tl:tl + 1],
                                                 sspart[:, tl % 2:tl % 2 + 1])
                                return e.tensor_copy(
                                    qstage[:, tl * QR + c0: tl * QR + c0 + pw],
                                    ps[:, 0:pw])
                            tq = ph.do("vector", qacc,
                                       wait=wlf(tsq, ev_tok.get(("qa", tl))),
                                       inc=[("V", 1)])
                            ev_tok[("psum", tl % 2)] = tq
                            ev_tok[("sq", tl % 2)] = tq
                            ev_tok[("qa", tl)] = tq
                        elif c0 == 0:
                            te = ph.do("scalar", lambda e, ps=ps, tl=tl: e.copy(
                                kc_f32[:, tl * KVR:tl * KVR + 288], ps[:, 0:288]),
                                wait=wlf(tB, ev_tok.get(("kc", tl))),
                                inc=[("Sc", 1)])
                            ev_tok[("psum", tl % 2)] = te
                            ev_tok[("kv1", tl)] = te
                        else:
                            def kev0(e, ps=ps, tl=tl):
                                e.copy(kc_f32[:, tl * KVR + 288:(tl + 1) * KVR],
                                       ps[:, 0:KVR - 288])
                                return e.copy(krtmp[:], ps[:, KVR - 288:KVR - 288 + RD])
                            te = ph.do("scalar", kev0,
                                       wait=wlf(tB, ev_tok.get(("kc", tl)),
                                                kr_done[-1] if kr_done else None),
                                       inc=[("Sc", 1)])
                            ev_tok[("psum", tl % 2)] = te
                            ttr = ph.do("tensor", lambda e: nc.tensor.transpose(
                                psC[0:64, 0:128], krtmp[:], id_f32[:]),
                                wait=wlf(te, kr_done[-1] if kr_done else None),
                                inc=[("T", 1)])
                            tkr = ph.do("vector", lambda e, tt=tt: e.tensor_copy(
                                krF_f32[:, tt * 128:(tt + 1) * 128],
                                psC[0:64, 0:128]), wait=[ttr], inc=[("V", 1)])
                            kr_done.append(tkr)
                            tsq = ph.do("scalar", lambda e, tl=tl:
                                        e.activation(
                                            qn_bf[:, 0:KVR],
                                            kc_f32[:, tl * KVR:(tl + 1) * KVR],
                                            AF.Square,
                                            accum_out=sspart[:, tl % 2:tl % 2 + 1]),
                                        wait=wlf(te, ev_tok.get(("kv1", tl)),
                                                 ev_tok.get(("sq", tl % 2)),
                                                 ev_tok.get(("qn",))),
                                        inc=[("Sc", 1)])
                            rr = rms_r[:, 4 + tl:5 + tl]
                            t1 = ph.do("vector", lambda e, rr=rr, tl=tl:
                                       e.tensor_scalar_mul(
                                           rr, sspart[:, tl % 2:tl % 2 + 1],
                                           1.0 / KVR),
                                       wait=[tsq], inc=[("V", 1)])
                            ev_tok[("sq", tl % 2)] = t1
                            t2 = ph.do("scalar", lambda e, rr=rr: e.activation(
                                rr, rr, AF.Sqrt), wait=[t1], inc=[("Sc", 1)])
                            t3 = ph.do("vector", lambda e, rr=rr: e.tensor_scalar(
                                rr, rr, 1.0, RMS_EPS, op0=OP.mult, op1=OP.add),
                                wait=[t2], inc=[("V", 1)])
                            t4 = ph.do("vector", lambda e, rr=rr: e.reciprocal(
                                rr, rr), wait=[t3], inc=[("V", 1)])
                            t5 = ph.do("vector", lambda e, rr=rr, tl=tl:
                                       e.scalar_tensor_tensor(
                                           kc_bf[:], kc_f32[:, tl * KVR:(tl + 1) * KVR],
                                           rr, gkva_bc[:], op0=OP.mult, op1=OP.mult),
                                       wait=wlf(t4, ev_tok.get(("kcb",))),
                                       inc=[("V", 1)])
                            ev_tok[("kc", tl)] = t5
                            ptk = tl % 2
                            pt = ptA if ptk == 0 else ptB

                            def ktr(e, pt=pt):
                                for j in range(4):
                                    ins = nc.tensor.transpose(
                                        pt[:, j * 128:(j + 1) * 128],
                                        kc_bf[:, j * 128:(j + 1) * 128], id_bf[:])
                                return ins
                            trl = ph.do("tensor", ktr,
                                        wait=wlf(t5, pt_free.get(ptk)),
                                        inc=[("T", 1)])
                            ev_tok[("kcb",)] = trl

                            def kdst(e, pt=pt, tt=tt):
                                ins = None
                                for j in range(4):
                                    ins = e.copy(
                                        k_cF[:, j * S + tt * 128:
                                             j * S + (tt + 1) * 128],
                                        pt[:, j * 128:(j + 1) * 128])
                                return ins
                            evl = ph.do("scalar", kdst, wait=[trl], inc=[("Sc", 1)])
                            pt_free[ptk] = evl
                    if pname == "qa" and c0 == 1152:
                        prev_trl = ev_tok.get(("qn",))
                        for tl in range(4):
                            rr = rms_r[:, tl:tl + 1]
                            tq0 = ev_tok[("qa", tl)]
                            t1 = ph.do("vector", lambda e, rr=rr, tl=tl:
                                       e.tensor_scalar_mul(rr, q_ssq[:, tl:tl + 1],
                                                           1.0 / QR),
                                       wait=[tq0], inc=[("V", 1)])
                            t2 = ph.do("scalar", lambda e, rr=rr: e.activation(
                                rr, rr, AF.Sqrt), wait=[t1], inc=[("Sc", 1)])
                            t3 = ph.do("vector", lambda e, rr=rr: e.tensor_scalar(
                                rr, rr, 1.0, RMS_EPS, op0=OP.mult, op1=OP.add),
                                wait=[t2], inc=[("V", 1)])
                            t4 = ph.do("vector", lambda e, rr=rr: e.reciprocal(rr, rr),
                                       wait=[t3], inc=[("V", 1)])
                            t5 = ph.do("vector", lambda e, rr=rr: e.tensor_scalar_mul(
                                rr, rr, QK_SCALE), wait=[t4], inc=[("V", 1)])
                            t6 = ph.do("vector", lambda e, rr=rr, tl=tl:
                                       e.scalar_tensor_tensor(
                                           qn_bf[:], qstage[:, tl * QR:(tl + 1) * QR],
                                           rr, gqa_bc[:], op0=OP.mult, op1=OP.mult),
                                       wait=wlf(t5, prev_trl,
                                                ev_tok.get(("sq", 0)),
                                                ev_tok.get(("sq", 1))),
                                       inc=[("V", 1)])
                            tt = 4 * quarter + tl
                            trl = None
                            for rnd in range(2):
                                ptk = rnd % 2
                                pt = ptA if ptk == 0 else ptB

                                def qtr(e, rnd=rnd, pt=pt):
                                    for j in range(6):
                                        c = rnd * 6 + j
                                        ins = nc.tensor.transpose(
                                            pt[:, j * 128:(j + 1) * 128],
                                            qn_bf[:, c * 128:(c + 1) * 128],
                                            id_bf[:])
                                    return ins
                                trl = ph.do("tensor", qtr,
                                            wait=wlf(t6, pt_free.get(ptk)),
                                            inc=[("T", 1)])

                                def qdst(e, rnd=rnd, pt=pt, tt=tt):
                                    ins = None
                                    for j in range(6):
                                        c = rnd * 6 + j
                                        ins = e.copy(
                                            q_cF[:, c * S + tt * 128:
                                                 c * S + (tt + 1) * 128],
                                            pt[:, j * 128:(j + 1) * 128])
                                    return ins
                                evl = ph.do("scalar", qdst, wait=[trl],
                                            inc=[("Sc", 1)])
                                pt_free[ptk] = evl
                            prev_trl = trl
                            ev_tok[("qn",)] = trl
                    if pname == "kv" and c0 == 288:
                        for tl in range(4):
                            dst_free[tl] = mm_tok[(pi, tl)][-1]
                        ln_done = {}
                        if quarter < 3:
                            emit_ln(quarter + 1)
            ph.emit()

    def emit_attention(ph, h, q_chunks, k_chunks, v_tok, attn_bf2, sc_sb2,
                       attnT, ctx_pc, sm, tk, tile_pos=None, strip=None,
                       prep=None, rcp_all=None):
        t_evm = {}; t_comb = {}; t_exp = {}; t_dd = {}; t_recip = {}
        t_tn = {}; t_atr = {}; t_adst = {}; t_avmm = {}; t_cev = {}
        vt_tok = tk.get("vt")
        qk_list = list(tk.get("qk", []))

        def wl(*toks):
            return [t for t in toks if t]

        def stage_A(qt):
            par = qt % 2
            sc_buf = sc_sb2[:, par * 2048:(par + 1) * 2048]
            t_exp.setdefault(qt, {})
            for kh in range(2):
                ps = psA if kh == 0 else psB

                def smm(e, qt=qt, kh=kh, ps=ps):
                    nchunks = len(q_chunks)
                    for ci, ((qsb, qsel), (ksb, ksel)) in enumerate(
                            zip(q_chunks, k_chunks)):
                        if qsel == "strip":
                            lhs = qsb[strip * 32:(strip + 1) * 32,
                                      qt * 128:(qt + 1) * 128]
                        elif qsel is None:
                            lhs = qsb[:, qt * 128:(qt + 1) * 128]
                        else:
                            lhs = qsb[qsel * 64:(qsel + 1) * 64,
                                      qt * 128:(qt + 1) * 128]
                        for j in range(2):
                            c0 = kh * 1024 + j * 512
                            if ksel == "strip":
                                rhs = ksb[strip * 32:(strip + 1) * 32, c0:c0 + 512]
                            elif ksel is None:
                                rhs = ksb[:, c0:c0 + 512]
                            else:
                                rhs = ksb[ksel * 64:(ksel + 1) * 64, c0:c0 + 512]
                            ins = nc.tensor.matmul(
                                ps[:, j * 512:(j + 1) * 512], lhs, rhs,
                                start=(ci == 0), stop=(ci == nchunks - 1),
                                tile_position=tile_pos)
                    return ins
                pk = "evm%d" % kh
                prev = tk.get(pk) if qt == 0 else t_evm[qt - 1][kh]
                w = wl(prev) + (qk_list if qt == 0 else [])
                tT = ph.do("tensor", smm, wait=w, inc=[("T", 1)])

                def evm(e, ps=ps, kh=kh, sc_buf=sc_buf, par=par):
                    return e.tensor_scalar(
                        sc_buf[:, kh * 1024:(kh + 1) * 1024], ps[:], 0.0, None,
                        op0=OP.add, op1=OP.max,
                        accum_out=sm["mx"][:, 2 * par + kh:2 * par + kh + 1])
                scfree = (t_exp[qt - 2][1] if qt >= 2
                          else tk.get("exp_p%d" % par))
                t_evm.setdefault(qt, {})[kh] = ph.do(
                    "vector", evm, wait=wl(tT, scfree), inc=[("V", 1)])
                tk["evm%d" % kh] = t_evm[qt][kh]

            def comb(e, par=par):
                return e.tensor_scalar(
                    sm["nmx"][:, par:par + 1], sm["mx"][:, 2 * par:2 * par + 1],
                    sm["mx"][:, 2 * par + 1:2 * par + 2], -1.0,
                    op0=OP.max, op1=OP.mult)
            scfree = t_exp[qt - 2][1] if qt >= 2 else tk.get("exp_p%d" % par)
            t_comb[qt] = ph.do("vector", comb,
                               wait=wl(t_evm[qt][0], t_evm[qt][1], scfree),
                               inc=[("V", 1)])

        def stage_B(qt):
            par = qt % 2
            sc_buf = sc_sb2[:, par * 2048:(par + 1) * 2048]
            ab = attn_bf2[:, par * 2048:(par + 1) * 2048]
            abfree = t_atr[qt - 2][1] if qt >= 2 else tk.get("atr_p%d" % par)
            ddfree = t_recip[qt - 2] if qt >= 2 else tk.get("dd_p%d" % par)
            for kh in range(2):
                def expo(e, kh=kh, sc_buf=sc_buf, ab=ab, par=par):
                    return e.activation(
                        ab[:, kh * 1024:(kh + 1) * 1024],
                        sc_buf[:, kh * 1024:(kh + 1) * 1024], AF.Exp,
                        bias=sm["nmx"][:, par:par + 1],
                        accum_out=sm["dd"][:, 2 * par + kh:2 * par + kh + 1])
                t_exp[qt][kh] = ph.do("scalar", expo,
                                      wait=wl(t_comb[qt], abfree, ddfree),
                                      inc=[("Sc", 1)])
            t_dd[qt] = ph.do("vector", lambda e, par=par: e.tensor_tensor(
                sm["dd"][:, 2 * par:2 * par + 1], sm["dd"][:, 2 * par:2 * par + 1],
                sm["dd"][:, 2 * par + 1:2 * par + 2], op=OP.add),
                wait=wl(t_exp[qt][0], t_exp[qt][1]), inc=[("V", 1)])
            rdst = (sm["rcp"][:, par:par + 1] if rcp_all is None
                    else rcp_all[:, h * 16 + qt:h * 16 + qt + 1])
            t_recip[qt] = ph.do("vector", lambda e, rdst=rdst, par=par:
                                e.reciprocal(rdst, sm["dd"][:, 2 * par:2 * par + 1]),
                                wait=wl(t_dd[qt]), inc=[("V", 1)])
            if rcp_all is None:
                t_tn[qt] = ph.do("vector", lambda e, ab=ab, par=par:
                                 e.tensor_scalar_mul(ab[:], ab[:],
                                                     sm["rcp"][:, par:par + 1]),
                                 wait=wl(t_recip[qt]), inc=[("V", 1)])

        def stage_C(qt):
            par = qt % 2
            ab = attn_bf2[:, par * 2048:(par + 1) * 2048]
            t_atr.setdefault(qt, {})
            t_adst.setdefault(qt, {})
            for rnd in range(2):
                pt = ptA if rnd == 0 else ptB

                def atr(e, rnd=rnd, pt=pt, ab=ab):
                    for j in range(8):
                        c = rnd * 8 + j
                        ins = nc.tensor.transpose(pt[:, j * 128:(j + 1) * 128],
                                                  ab[:, c * 128:(c + 1) * 128],
                                                  id_bf[:])
                    return ins
                ptfree = t_adst[qt - 1][rnd] if qt >= 1 else tk.get("adst_r%d" % rnd)
                steal = tk.pop("pt_steal_%d" % rnd, None)
                src_ok = t_tn[qt] if rcp_all is None else t_exp[qt][rnd]
                t_atr[qt][rnd] = ph.do(
                    "tensor", atr,
                    wait=wl(src_ok, ptfree, steal), inc=[("T", 1)])

                aeng = "scalar" if (rnd == 0 or rcp_all is None) else "vector"
                asem = "Sc" if aeng == "scalar" else "V"

                def adst(e, rnd=rnd, pt=pt, qt=qt, aeng=aeng):
                    ql = qt % 4
                    dst = attnT.rearrange("p (t c) -> p t c", t=16)[
                        :, rnd * 8:(rnd + 1) * 8, ql * 128:(ql + 1) * 128]
                    srct = pt.rearrange("p (t c) -> p t c", t=8)
                    return (e.copy(dst, srct) if aeng == "scalar"
                            else e.tensor_copy(dst, srct))
                atfree = t_avmm[qt // 4 - 1] if qt >= 4 else tk.get("attnT")
                t_adst[qt][rnd] = ph.do(
                    aeng, adst, wait=wl(t_atr[qt][rnd], atfree),
                    inc=[(asem, 1)])
                tk["adst_r%d" % rnd] = t_adst[qt][rnd]

        def stage_D(qg):
            sl = psC[:, 0:512]

            def avmm(e, sl=sl):
                for kt in range(16):
                    ins = nc.tensor.matmul(
                        sl, v_tok[:, kt * 128:(kt + 1) * 128],
                        attnT[:, kt * 512:(kt + 1) * 512],
                        start=(kt == 0), stop=(kt == 15))
                return ins
            pcfree = t_cev[qg - 1] if qg >= 1 else tk.get("avc")
            t_avmm[qg] = ph.do(
                "tensor", avmm,
                wait=wl(t_adst[qg * 4 + 3][0], t_adst[qg * 4 + 3][1],
                        vt_tok, pcfree),
                inc=[("T", 1)])

            def cev(e, sl=sl):
                return e.copy(ctx_pc[:], sl)
            cxfree = tk.get("ctx")
            t_cev[qg] = ph.do("scalar", cev, wait=wl(t_avmm[qg], cxfree),
                              inc=[("Sc", 1)])
            tk["avc"] = t_cev[qg]
            dst = tls[qg * 4:(qg + 1) * 4, :, h * 128:(h + 1) * 128].rearrange(
                "t p j -> p t j")
            srcx = ctx_pc.rearrange("p (t j) -> p t j", t=4)
            tk["ctx"] = ph.do("gpsimd", lambda e, dst=dst, srcx=srcx:
                              e.dma_start(out=dst, in_=srcx),
                              wait=[t_cev[qg]], inc=[("G", 16)])

        for i in range(20):
            if i < 16:
                stage_A(i)
            if 4 <= i and (i - 4) % 4 == 3:
                stage_D((i - 4) // 4)
            if 3 <= i < 19:
                stage_C(i - 3)
            if 1 <= i < 17:
                stage_B(i - 1)
            if prep is not None and i >= 1:
                next(prep, None)
        tk["evm0"] = t_evm[15][0]
        tk["evm1"] = t_evm[15][1]
        tk["psA"] = t_evm[15][0]
        tk["psB"] = t_evm[15][1]
        tk["exp_p0"] = t_exp[14][1]
        tk["exp_p1"] = t_exp[15][1]
        tk["atr_p0"] = t_atr[14][1]
        tk["atr_p1"] = t_atr[15][1]
        tk["adst_r0"] = t_adst[15][0]
        tk["adst_r1"] = t_adst[15][1]
        tk["dd_p0"] = t_recip[14]
        tk["dd_p1"] = t_recip[15]
        tk["attnT"] = t_avmm[3]

    def emit_attention_stream(ph, nheads, head_args, attn_bf2, sc_sb2,
                              attnT, ctx_pc, sm, tk, preps, vdw=128,
                              z_sink=None, rcp_all=None):
        wl = lambda *t: [x for x in t if x]
        t_evm = {}; t_comb = {}; t_exp = {}; t_dd = {}; t_recip = {}
        t_tn = {}; t_atr = {}; t_adst = {}; t_avmm = {}; t_cev = {}
        head_state = {}
        NQT = nheads * 16

        def stage_A(qt):
            h = qt // 16
            if qt % 16 == 0:
                head_state[h] = {"qk": list(tk.get("qk", [])),
                                 "vt": tk.get("vt")}
            q_chunks, k_chunks, v_tok, tile_pos, strip = head_args(h)
            par = qt % 2
            ql = qt % 16
            sc_buf = sc_sb2[:, par * 2048:(par + 1) * 2048]
            t_exp.setdefault(qt, {})
            for kh in range(2):
                ps = psA if kh == 0 else psB

                def smm(e, ql=ql, kh=kh, ps=ps, q_chunks=q_chunks,
                        k_chunks=k_chunks, tile_pos=tile_pos, strip=strip):
                    nchunks = len(q_chunks)
                    for ci, ((qsb, qsel), (ksb, ksel)) in enumerate(
                            zip(q_chunks, k_chunks)):
                        if qsel == "strip":
                            lhs = qsb[strip * 32:(strip + 1) * 32,
                                      ql * 128:(ql + 1) * 128]
                        elif qsel is None:
                            lhs = qsb[:, ql * 128:(ql + 1) * 128]
                        else:
                            lhs = qsb[qsel * 64:(qsel + 1) * 64,
                                      ql * 128:(ql + 1) * 128]
                        for j in range(2):
                            c0 = kh * 1024 + j * 512
                            if ksel == "strip":
                                rhs = ksb[strip * 32:(strip + 1) * 32, c0:c0 + 512]
                            elif ksel is None:
                                rhs = ksb[:, c0:c0 + 512]
                            else:
                                rhs = ksb[ksel * 64:(ksel + 1) * 64, c0:c0 + 512]
                            ins = nc.tensor.matmul(
                                ps[:, j * 512:(j + 1) * 512], lhs, rhs,
                                start=(ci == 0), stop=(ci == nchunks - 1),
                                tile_position=tile_pos)
                    return ins
                prev = t_evm[qt - 1][kh] if qt >= 1 else None
                w = wl(prev) + (head_state[h]["qk"] if ql == 0 else [])
                tT = ph.do("tensor", smm, wait=w, inc=[("T", 1)])

                def evm(e, ps=ps, kh=kh, sc_buf=sc_buf, par=par):
                    return e.tensor_scalar(
                        sc_buf[:, kh * 1024:(kh + 1) * 1024], ps[:], 0.0, None,
                        op0=OP.add, op1=OP.max,
                        accum_out=sm["mx"][:, 2 * par + kh:2 * par + kh + 1])
                scfree = t_exp[qt - 2][1] if qt >= 2 else None
                t_evm.setdefault(qt, {})[kh] = ph.do(
                    "vector", evm, wait=wl(tT, scfree), inc=[("V", 1)])
                tk["evm%d" % kh] = t_evm[qt][kh]

            def comb(e, par=par):
                return e.tensor_scalar(
                    sm["nmx"][:, par:par + 1], sm["mx"][:, 2 * par:2 * par + 1],
                    sm["mx"][:, 2 * par + 1:2 * par + 2], -1.0,
                    op0=OP.max, op1=OP.mult)
            scfree = t_exp[qt - 2][1] if qt >= 2 else None
            t_comb[qt] = ph.do("vector", comb,
                               wait=wl(t_evm[qt][0], t_evm[qt][1], scfree),
                               inc=[("V", 1)])

        def stage_B(qt):
            par = qt % 2
            sc_buf = sc_sb2[:, par * 2048:(par + 1) * 2048]
            ab = attn_bf2[:, par * 2048:(par + 1) * 2048]
            abfree = t_atr[qt - 2][1] if qt >= 2 else None
            ddfree = t_recip[qt - 2] if qt >= 2 else None
            for kh in range(2):
                def expo(e, kh=kh, sc_buf=sc_buf, ab=ab, par=par):
                    return e.activation(
                        ab[:, kh * 1024:(kh + 1) * 1024],
                        sc_buf[:, kh * 1024:(kh + 1) * 1024], AF.Exp,
                        bias=sm["nmx"][:, par:par + 1],
                        accum_out=sm["dd"][:, 2 * par + kh:2 * par + kh + 1])
                t_exp[qt][kh] = ph.do("scalar", expo,
                                      wait=wl(t_comb[qt], abfree, ddfree),
                                      inc=[("Sc", 1)])
            t_dd[qt] = ph.do("vector", lambda e, par=par: e.tensor_tensor(
                sm["dd"][:, 2 * par:2 * par + 1], sm["dd"][:, 2 * par:2 * par + 1],
                sm["dd"][:, 2 * par + 1:2 * par + 2], op=OP.add),
                wait=wl(t_exp[qt][0], t_exp[qt][1]), inc=[("V", 1)])
            rdst = (sm["rcp"][:, par:par + 1] if rcp_all is None
                    else rcp_all[:, qt:qt + 1])
            t_recip[qt] = ph.do("vector", lambda e, rdst=rdst, par=par:
                                e.reciprocal(rdst, sm["dd"][:, 2 * par:2 * par + 1]),
                                wait=wl(t_dd[qt]), inc=[("V", 1)])
            if rcp_all is None:
                t_tn[qt] = ph.do("vector", lambda e, ab=ab, par=par:
                                 e.tensor_scalar_mul(ab[:], ab[:],
                                                     sm["rcp"][:, par:par + 1]),
                                 wait=wl(t_recip[qt]), inc=[("V", 1)])

        def stage_C(qt):
            par = qt % 2
            ab = attn_bf2[:, par * 2048:(par + 1) * 2048]
            t_atr.setdefault(qt, {})
            t_adst.setdefault(qt, {})
            for rnd in range(2):
                pt = ptA if rnd == 0 else ptB

                def atr(e, rnd=rnd, pt=pt, ab=ab):
                    for j in range(8):
                        c = rnd * 8 + j
                        ins = nc.tensor.transpose(pt[:, j * 128:(j + 1) * 128],
                                                  ab[:, c * 128:(c + 1) * 128],
                                                  id_bf[:])
                    return ins
                ptfree = t_adst[qt - 1][rnd] if qt >= 1 else None
                steal = tk.pop("pt_steal_%d" % rnd, None)
                src_ok = t_tn[qt] if rcp_all is None else t_exp[qt][rnd]
                t_atr[qt][rnd] = ph.do(
                    "tensor", atr,
                    wait=wl(src_ok, ptfree, steal), inc=[("T", 1)])

                aeng = "scalar" if (rnd == 0 or rcp_all is None) else "vector"
                asem = "Sc" if aeng == "scalar" else "V"

                def adst(e, rnd=rnd, pt=pt, qt=qt, aeng=aeng):
                    ql = qt % 4
                    dst = attnT.rearrange("p (t c) -> p t c", t=16)[
                        :, rnd * 8:(rnd + 1) * 8, ql * 128:(ql + 1) * 128]
                    srct = pt.rearrange("p (t c) -> p t c", t=8)
                    return (e.copy(dst, srct) if aeng == "scalar"
                            else e.tensor_copy(dst, srct))
                atfree = t_avmm[qt // 4 - 1] if qt >= 4 else None
                t_adst[qt][rnd] = ph.do(
                    aeng, adst, wait=wl(t_atr[qt][rnd], atfree),
                    inc=[(asem, 1)])
                tk["adst_r%d" % rnd] = t_adst[qt][rnd]

        def stage_D(qg):
            h = qg // 4
            v_tok = head_args(h)[2]
            sl = psC[:, 0:512] if vdw == 128 else psC[0:vdw, 0:512]

            def avmm(e, sl=sl, v_tok=v_tok):
                for kt in range(16):
                    ins = nc.tensor.matmul(
                        sl, v_tok[:, kt * vdw:(kt + 1) * vdw],
                        attnT[:, kt * 512:(kt + 1) * 512],
                        start=(kt == 0), stop=(kt == 15))
                return ins
            pcfree = t_cev[qg - 1] if qg >= 1 else None
            t_avmm[qg] = ph.do(
                "tensor", avmm,
                wait=wl(t_adst[qg * 4 + 3][0], t_adst[qg * 4 + 3][1],
                        head_state[h]["vt"], pcfree),
                inc=[("T", 1)])
            tk["attnT"] = t_avmm[qg]
            gg = qg % 4
            if z_sink is not None:
                t_cev[qg] = z_sink(qg, h, gg, t_avmm[qg])
                return

            def cev(e, sl=sl):
                return e.copy(ctx_pc[:], sl)
            cxfree = tk.get("ctx")
            t_cev[qg] = ph.do("scalar", cev, wait=wl(t_avmm[qg], cxfree),
                              inc=[("Sc", 1)])
            tk["avc"] = t_cev[qg]
            dst = tls[gg * 4:(gg + 1) * 4, :, h * 128:(h + 1) * 128].rearrange(
                "t p j -> p t j")
            srcx = ctx_pc.rearrange("p (t j) -> p t j", t=4)
            tk["ctx"] = ph.do("gpsimd", lambda e, dst=dst, srcx=srcx:
                              e.dma_start(out=dst, in_=srcx),
                              wait=[t_cev[qg]], inc=[("G", 16)])

        for i in range(NQT + 4):
            if i < NQT:
                stage_A(i)
            if 4 <= i and (i - 4) % 4 == 3:
                stage_D((i - 4) // 4)
            if 3 <= i < NQT + 3:
                stage_C(i - 3)
            if 1 <= i < NQT + 1:
                stage_B(i - 1)
            hnext = i // 16 + 1
            if hnext < len(preps) and preps[hnext] is not None:
                next(preps[hnext], None)

    def p2(q_cF, k_cF, krF_bf):
        ph = Ph(k, "p2")
        wlf = lambda *t: [x for x in t if x]
        with ExitStack() as es:
            c_cs2 = es.enter_context(nc.sbuf_tensor("p2_cs", [128, 2 * S], F32))
            wqb_f32 = es.enter_context(nc.sbuf_tensor("p2_wqf", [128, 6 * 384], F32))
            wqb_bf = es.enter_context(nc.sbuf_tensor("p2_wqb", [128, 12 * 384], BF16))
            wqbr_bf = es.enter_context(nc.sbuf_tensor("p2_wqr", [128, 12 * 128], BF16))
            wkvb_f32 = es.enter_context(nc.sbuf_tensor("p2_wkf", [128, 4 * 256], F32))
            wkvb_bf = es.enter_context(nc.sbuf_tensor("p2_wkb", [128, 4 * 256], BF16))
            qn2 = es.enter_context(nc.sbuf_tensor("p2_qn", [128, 2 * S], BF16))
            kn2 = es.enter_context(nc.sbuf_tensor("p2_kn", [128, 2 * S], BF16))
            vF2 = es.enter_context(nc.sbuf_tensor("p2_vF", [128, 2 * S], BF16))
            vt2 = es.enter_context(nc.sbuf_tensor("p2_vt", [128, 2 * S], BF16))
            qr2 = es.enter_context(nc.sbuf_tensor("p2_qr", [128, 2 * S], BF16))
            rp_raw = es.enter_context(nc.sbuf_tensor("p2_rraw", [128, 1024], F32))
            rp_sw = es.enter_context(nc.sbuf_tensor("p2_rsw", [128, 1024], F32))
            attn_bf2 = es.enter_context(nc.sbuf_tensor("p2_at2", [128, 2 * S], BF16))
            sc_sb2 = es.enter_context(nc.sbuf_tensor("p2_scb", [128, 2 * S], F32))
            rstg2 = es.enter_context(nc.sbuf_tensor("p2_rst", [16, 256], F32))
            rcp_all_a = es.enter_context(nc.sbuf_tensor("p2_rca", [128, 512], F32))
            attnT = es.enter_context(nc.sbuf_tensor("p2_aT", [128, 16 * 512], BF16))
            ctx_pc = es.enter_context(nc.sbuf_tensor("p2_cx", [128, 512], BF16))
            sm = {"mx": es.enter_context(nc.sbuf_tensor("p2_mx", [128, 4], F32)),
                  "nmx": es.enter_context(nc.sbuf_tensor("p2_nmx", [128, 2], F32)),
                  "dd": es.enter_context(nc.sbuf_tensor("p2_dd", [128, 4], F32)),
                  "rcp": es.enter_context(nc.sbuf_tensor("p2_rcp", [128, 2], F32))}
            psP = psC[:, 512:1024]
            ph.do("sync", lambda e: e.dma_start(out=c_cs2[:, 0:S], in_=c_cos),
                  inc=[("Q", 16)])
            ph.do("sync", lambda e: e.dma_start(out=c_cs2[:, S:2 * S], in_=c_sinm),
                  inc=[("Q", 16)])
            tk = {}

            def mk_prep(h):
                hh = h % 2
                nb = (h % 2) * S
                gb = ((h // 2) % 2) * S

                def bguard():
                    return wlf(tk.get("evm1"))

                def gen():
                    qk = []
                    if hh == 0:
                        pair = h // 2
                        srcw = wqb[:, pair * 384:(pair + 1) * 384].rearrange(
                            "(dc p) c -> p dc c", p=128)
                        dstw = wqb_f32.rearrange("p (dc c) -> p dc c", dc=6)
                        tconv = None
                        for hf in range(2):
                            tdma = ph.do("sync", lambda e, srcw=srcw, dstw=dstw,
                                         hf=hf: e.dma_start(
                                             out=dstw,
                                             in_=srcw[:, hf * 6:(hf + 1) * 6, :]),
                                         wait=wlf(tconv if hf else
                                                  tk.get("wqf_rd")),
                                         inc=[("Q", 16)])
                            yield
                            tconv = ph.do("vector", lambda e, hf=hf:
                                          e.tensor_copy(
                                              wqb_bf[:, hf * 2304:(hf + 1) * 2304],
                                              wqb_f32[:]),
                                          wait=[tdma] + wlf(tk.get("wqb_rd")
                                                            if hf == 0 else None),
                                          inc=[("V", 1)])
                        tk["wqf_rd"] = tconv
                        tk["wqbconv"] = tconv
                        yield

                        def rstage(e):
                            for dc in range(12):
                                e.tensor_copy(wqbr_bf[:, dc * 128:dc * 128 + 64],
                                              wqb_bf[:, dc * 384 + 128:
                                                     dc * 384 + 192])
                                ins = e.tensor_copy(
                                    wqbr_bf[:, dc * 128 + 64:dc * 128 + 128],
                                    wqb_bf[:, dc * 384 + 320:dc * 384 + 384])
                            return ins
                        trs = ph.do("vector", rstage,
                                    wait=[tconv] + wlf(tk.get("wqr_rd")),
                                    inc=[("V", 1)])
                        tk["rstage"] = trs
                        yield
                    srck = wkvb[:, h * 256:(h + 1) * 256].rearrange(
                        "(dc p) c -> p dc c", p=128)
                    dstk = wkvb_f32.rearrange("p (dc c) -> p dc c", dc=4)
                    tkv = ph.do("scalar", lambda e, srck=srck, dstk=dstk:
                                e.dma_start(out=dstk, in_=srck),
                                wait=wlf(tk.get("wkf_rd")), inc=[("Adma", 16)])
                    tkc = ph.do("vector", lambda e: e.tensor_copy(wkvb_bf[:],
                                                                  wkvb_f32[:]),
                                wait=[tkv] + wlf(tk.get("wkb_rd")), inc=[("V", 1)])
                    tk["wkf_rd"] = tkc
                    yield
                    if hh == 0:
                        for qh in range(2):
                            evs = []
                            for j in range(2):
                                def qrmm(e, qh=qh, j=j):
                                    for dc in range(12):
                                        ins = nc.tensor.matmul(
                                            psP,
                                            wqbr_bf[:, dc * 128:(dc + 1) * 128],
                                            q_cF[:, dc * S + qh * 1024 + j * 512:
                                                 dc * S + qh * 1024 +
                                                 (j + 1) * 512],
                                            start=(dc == 0), stop=(dc == 11))
                                    return ins
                                tq = ph.do("tensor", qrmm,
                                           wait=[tk["rstage"]] +
                                           wlf(tk.get("psP")), inc=[("T", 1)])
                                tk["wqr_rd"] = tq
                                tev = ph.do("vector", lambda e, j=j:
                                            e.tensor_copy(
                                                rp_raw[:, j * 512:(j + 1) * 512],
                                                psP),
                                            wait=[tq] + wlf(tk.get("rp_rd")
                                                            if j == 0 else None),
                                            inc=[("V", 1)])
                                tk["psP"] = tev
                                evs.append(tev)
                                yield

                            def swp(e):
                                for g, sw_ in ((0, 32), (32, 0), (64, 96),
                                               (96, 64)):
                                    ins = e.tensor_copy(rp_sw[g:g + 32, :],
                                                        rp_raw[sw_:sw_ + 32, :])
                                return ins
                            tsw = ph.do("vector", swp, wait=evs, inc=[("V", 1)])
                            tm1 = ph.do("vector", lambda e, qh=qh: e.tensor_mul(
                                rp_raw[:], rp_raw[:],
                                c_cs2[:, qh * 1024:(qh + 1) * 1024]),
                                wait=[tsw], inc=[("V", 1)])
                            tm2 = ph.do("vector", lambda e, qh=qh: e.tensor_mul(
                                rp_sw[:], rp_sw[:],
                                c_cs2[:, S + qh * 1024: S + (qh + 1) * 1024]),
                                wait=[tsw], inc=[("V", 1)])
                            tro = ph.do("vector", lambda e, qh=qh, gb=gb:
                                        e.tensor_add(
                                            qr2[:, gb + qh * 1024:
                                                gb + (qh + 1) * 1024],
                                            rp_raw[:], rp_sw[:]),
                                        wait=[tm1, tm2] + bguard(),
                                        inc=[("V", 1)])
                            tk["rp_rd"] = tro
                            tk["qr_ready"] = tro
                            yield
                    qk.append(tk["qr_ready"])
                    last_ev = None
                    for qh in range(2):
                        for j in range(2):
                            def qnmm(e, qh=qh, j=j, hh=hh):
                                for dc in range(12):
                                    ins = nc.tensor.matmul(
                                        psP,
                                        wqb_bf[:, dc * 384 + hh * 192:
                                               dc * 384 + hh * 192 + 128],
                                        q_cF[:, dc * S + qh * 1024 + j * 512:
                                             dc * S + qh * 1024 + (j + 1) * 512],
                                        start=(dc == 0), stop=(dc == 11))
                                return ins
                            tqn = ph.do("tensor", qnmm,
                                        wait=[tk["wqbconv"]] + wlf(tk.get("psP")),
                                        inc=[("T", 1)])
                            tk["wqb_rd"] = tqn
                            last_ev = ph.do("scalar", lambda e, qh=qh, j=j, nb=nb:
                                            e.copy(qn2[:, nb + qh * 1024 + j * 512:
                                                       nb + qh * 1024 +
                                                       (j + 1) * 512], psP),
                                            wait=[tqn] + bguard(), inc=[("Sc", 1)])
                            tk["psP"] = last_ev
                            yield
                    for qh in range(2):
                        for j in range(2):
                            def knmm(e, qh=qh, j=j):
                                for dc in range(4):
                                    ins = nc.tensor.matmul(
                                        psP,
                                        wkvb_bf[:, dc * 256:dc * 256 + 128],
                                        k_cF[:, dc * S + qh * 1024 + j * 512:
                                             dc * S + qh * 1024 + (j + 1) * 512],
                                        start=(dc == 0), stop=(dc == 3))
                                return ins
                            tkn = ph.do("tensor", knmm,
                                        wait=[tkc] + wlf(tk.get("psP")),
                                        inc=[("T", 1)])
                            tk["wkb_rd"] = tkn
                            last_ev = ph.do("scalar", lambda e, qh=qh, j=j, nb=nb:
                                            e.copy(kn2[:, nb + qh * 1024 + j * 512:
                                                       nb + qh * 1024 +
                                                       (j + 1) * 512], psP),
                                            wait=[tkn] + bguard(), inc=[("Sc", 1)])
                            tk["psP"] = last_ev
                            yield
                    qk.append(last_ev)
                    vevs = []
                    for qh in range(2):
                        for j in range(2):
                            def vmm(e, qh=qh, j=j):
                                for dc in range(4):
                                    ins = nc.tensor.matmul(
                                        psP,
                                        wkvb_bf[:, dc * 256 + 128:dc * 256 + 256],
                                        k_cF[:, dc * S + qh * 1024 + j * 512:
                                             dc * S + qh * 1024 + (j + 1) * 512],
                                        start=(dc == 0), stop=(dc == 3))
                                return ins
                            tvm = ph.do("tensor", vmm,
                                        wait=[tkc] + wlf(tk.get("psP")),
                                        inc=[("T", 1)])
                            tk["wkb_rd"] = tvm
                            tve = ph.do("vector", lambda e, qh=qh, j=j, nb=nb:
                                        e.tensor_copy(
                                            vF2[:, nb + qh * 1024 + j * 512:
                                                nb + qh * 1024 + (j + 1) * 512],
                                            psP),
                                        wait=[tvm] + bguard(), inc=[("V", 1)])
                            tk["psP"] = tve
                            vevs.append(tve)
                            yield
                    last_vt = None
                    for rnd in range(2):
                        pt = ptA if rnd == 0 else ptB

                        def vtr(e, rnd=rnd, pt=pt, nb=nb):
                            for jj in range(8):
                                c = rnd * 8 + jj
                                ins = nc.tensor.transpose(
                                    pt[:, jj * 128:(jj + 1) * 128],
                                    vF2[:, nb + c * 128: nb + (c + 1) * 128],
                                    id_bf[:])
                            return ins
                        ttr = ph.do("tensor", vtr,
                                    wait=vevs + wlf(tk.get("adst_r%d" % rnd)),
                                    inc=[("T", 1)])
                        tvev = ph.do("scalar", lambda e, rnd=rnd, pt=pt, nb=nb:
                                     e.copy(vt2[:, nb + rnd * 1024:
                                                nb + (rnd + 1) * 1024], pt[:]),
                                     wait=[ttr] + wlf(tk.get("attnT")),
                                     inc=[("Sc", 1)])
                        tk["pt_steal_%d" % rnd] = tvev
                        last_vt = tvev
                        yield
                    tk["vt"] = last_vt
                    tk["qk"] = qk
                return gen()

            g0 = mk_prep(0)
            for _ in g0:
                pass
            for h in range(NH):
                nxt = mk_prep(h + 1) if h + 1 < NH else None
                hh = h % 2
                nb = (h % 2) * S
                gb = ((h // 2) % 2) * S
                emit_attention(ph, h,
                               q_chunks=[(qn2[:, nb:nb + S], None),
                                         (qr2[:, gb:gb + S], hh)],
                               k_chunks=[(kn2[:, nb:nb + S], None),
                                         (krF_bf, hh)],
                               v_tok=vt2[:, nb:nb + S],
                               attn_bf2=attn_bf2, sc_sb2=sc_sb2, attnT=attnT,
                               ctx_pc=ctx_pc, sm=sm, tk=tk, prep=nxt)
                if nxt is not None:
                    for _ in nxt:
                        pass
            ph.emit()

    def emit_epilogue(name, w_dram, w_is_bf, x_in, x_out, bias_src=None,
                      lora=None, pre=None, act_scale=None):
        ph = Ph(k, name)
        with ExitStack() as es:
            act_sb = es.enter_context(nc.sbuf_tensor(name + "_act", [128, NT * D], BF16))
            wbf = es.enter_context(nc.sbuf_tensor(name + "_wbf", [128, 2 * 32 * 256], BF16))
            wst = None if w_is_bf else es.enter_context(
                nc.sbuf_tensor(name + "_wst", [128, 4 * 1024], F32))
            xsl = es.enter_context(nc.sbuf_tensor(name + "_xsl", [128, 4 * 256], F32))
            osl = es.enter_context(nc.sbuf_tensor(name + "_osl", [128, 4 * 256], F32))
            brow = es.enter_context(nc.sbuf_tensor(name + "_brow", [1, D], BF16))
            brf = es.enter_context(nc.sbuf_tensor(name + "_brf", [1, 1024], F32))
            btok = None
            if bias_src is not None:
                for c4 in range(4):
                    tl_ = ph.do("sync", lambda e, c4=c4: e.dma_start(
                        out=brf[:], in_=bias_src[None, c4 * 1024:(c4 + 1) * 1024]),
                        wait=[btok] if btok else [], inc=[("Q", 16)])
                    btok = ph.do("vector", lambda e, c4=c4: e.tensor_copy(
                        brow[0:1, c4 * 1024:(c4 + 1) * 1024], brf[:]),
                        wait=[tl_], inc=[("V", 1)])
            pre_tok = pre(ph) if pre is not None else None
            act_toks = []
            for tt in range(NT):
                act_toks.append(ph.do("gpsimd", lambda e, tt=tt: e.dma_start(
                    out=act_sb[:, tt * D:(tt + 1) * D], in_=tls[tt]),
                    wait=ph.w("G") if tt == 0 else [], inc=[("G", 16)]))
            act_tok = act_toks[-1]
            act_first = list(act_toks)
            if act_scale is not None:
                rrow = es.enter_context(
                    nc.sbuf_tensor(name + "_rrow", [1, 2048], F32))
                ones_f32 = es.enter_context(
                    nc.sbuf_tensor(name + "_o1f", [1, 128], F32))
                tof = ph.do("gpsimd", lambda e: e.memset(ones_f32[:], 1.0),
                            inc=[("Gc", 1)])
                rtk = {}
                mlast = {}

                def scale_act(tt):
                    psx = psC[:, 512:1024]
                    last = None
                    for hf in range(2):
                        hb = hf * 16
                        trd = ph.do("sync", lambda e, tt=tt, hf=hf, hb=hb:
                                    e.dma_start(
                            out=rrow[0:1, :].rearrange("q (h j) -> q h j",
                                                       h=16),
                            in_=act_scale[hb:hb + 16,
                                          tt * 128:(tt + 1) * 128]),
                            wait=[x for x in [mlast.get(0)] if x],
                            inc=[("Q", 16)])
                        for i in range(4):
                            def bmm(e, i=i, psx=psx):
                                return nc.tensor.matmul(
                                    psx, ones_f32[0:1, 0:128],
                                    rrow[0:1, i * 512:(i + 1) * 512],
                                    start=True, stop=True)
                            tb = ph.do("tensor", bmm,
                                       wait=[trd, tof] + [x for x in
                                                          [mlast.get(0)] if x],
                                       inc=[("T", 1)])

                            def bml(e, tt=tt, i=i, hf=hf, psx=psx):
                                a = act_sb[:, tt * D + hf * 2048 + i * 512:
                                           tt * D + hf * 2048 + (i + 1) * 512]
                                return e.tensor_tensor(a, a, psx, op=OP.mult)
                            mlast[0] = ph.do("vector", bml,
                                             wait=[tb, act_toks[tt]],
                                             inc=[("V", 1)])
                            last = mlast[0]
                    rtk[tt] = last
                    return last
            psl = [psA[:, 0:256], psA[:, 512:768], psB[:, 0:256], psB[:, 512:768],
                   psC[:, 0:256], psC[:, 512:768]]
            if act_scale is not None:
                psl = psl[:5]
            nps = len(psl)
            psl_free = [None] * nps
            osl_free = [None] * 4
            xtok = {}
            mm_last = {}
            pii = 0
            for nb in range(16):
                wb = (nb % 2) * 8192
                wfree = mm_last.get(nb - 2)
                load_toks = []
                if w_is_bf:
                    for sc8 in range(8):
                        r0 = sc8 * 512
                        srcw = w_dram[r0:r0 + 512, nb * 256:(nb + 1) * 256].rearrange(
                            "(dc p) c -> p dc c", p=128)
                        dstw = wbf[:, wb + sc8 * 1024: wb + (sc8 + 1) * 1024].rearrange(
                            "p (dc c) -> p dc c", dc=4)
                        load_toks.append(ph.do("sync", lambda e, srcw=srcw, dstw=dstw:
                                               e.dma_start(out=dstw, in_=srcw),
                                               wait=[wfree] if wfree else [],
                                               inc=[("Q", 16)]))
                    wready = load_toks
                else:
                    cv = None
                    for sc8 in range(8):
                        r0 = sc8 * 512
                        sb4 = (sc8 % 4) * 1024
                        srcw = w_dram[r0:r0 + 512, nb * 256:(nb + 1) * 256].rearrange(
                            "(dc p) c -> p dc c", p=128)
                        dstw = wst[:, sb4:sb4 + 1024].rearrange(
                            "p (dc c) -> p dc c", dc=4)
                        wld = [t for t in (xtok.get(("w", nb - 1, sc8 + 4)),
                                           xtok.get(("w", nb, sc8 - 4)),
                                           wfree if sc8 == 0 else None) if t]
                        tl_ = ph.do("sync", lambda e, srcw=srcw, dstw=dstw: e.dma_start(
                            out=dstw, in_=srcw), wait=wld, inc=[("Q", 16)])
                        cv = ph.do("vector", lambda e, wb=wb, sc8=sc8, sb4=sb4:
                                   e.tensor_copy(
                                       wbf[:, wb + sc8 * 1024: wb + (sc8 + 1) * 1024],
                                       wst[:, sb4:sb4 + 1024]),
                                   wait=[tl_] + ([wfree] if wfree else []),
                                   inc=[("V", 1)])
                        xtok[("w", nb, sc8)] = cv
                    wready = [cv]
                for tt in range(NT):
                    sl = psl[pii % nps]
                    xb = (pii % 4) * 256
                    ofree = osl_free[pii % 4]
                    tx = ph.do("scalar", lambda e, tt=tt, nb=nb, xb=xb: e.dma_start(
                        out=xsl[:, xb:xb + 256],
                        in_=x_in[tt * 128:(tt + 1) * 128, nb * 256:(nb + 1) * 256]),
                        wait=[t for t in (xtok.get(("x", pii - 4)),) if t],
                        inc=[("Adma", 16)])

                    def big(e, tt=tt, nb=nb, wb=wb, sl=sl):
                        n_ops = DCH + (1 if lora else 0) + (1 if bias_src is not None
                                                            else 0)
                        io = 0
                        for dc in range(DCH):
                            ins = nc.tensor.matmul(
                                sl, act_sb[:, tt * D + dc * 128: tt * D + (dc + 1) * 128],
                                wbf[:, wb + dc * 256: wb + (dc + 1) * 256],
                                start=(io == 0), stop=(io == n_ops - 1))
                            io += 1
                        if lora:
                            t_bf_, bt_sb_ = lora
                            ins = nc.tensor.matmul(
                                sl, t_bf_[:, tt * 128:(tt + 1) * 128],
                                bt_sb_[:, nb * 256:(nb + 1) * 256],
                                start=False, stop=(io == n_ops - 1))
                            io += 1
                        if bias_src is not None:
                            ins = nc.tensor.matmul(
                                sl, ones_bf[0:1, 0:128],
                                brow[0:1, nb * 256:(nb + 1) * 256],
                                start=False, stop=True)
                        return ins
                    if act_scale is not None and nb == 0:
                        act_first[tt] = scale_act(tt)
                    w = [act_tok if (tt == NT - 1 and act_scale is None)
                         else act_first[tt]] + wready + \
                        [t for t in (psl_free[pii % nps], btok, pre_tok) if t]
                    bt_ = ph.do("tensor", big, wait=w, inc=[("T", 1)])
                    mm_last[nb] = bt_
                    at_ = ph.do("vector", lambda e, xb=xb, sl=sl: e.tensor_add(
                        osl[:, xb:xb + 256], sl, xsl[:, xb:xb + 256]),
                        wait=[bt_, tx] + ([ofree] if ofree else []), inc=[("V", 1)])
                    psl_free[pii % nps] = at_
                    xtok[("x", pii)] = at_
                    st_ = ph.do("gpsimd", lambda e, tt=tt, nb=nb, xb=xb: e.dma_start(
                        out=x_out[tt * 128:(tt + 1) * 128, nb * 256:(nb + 1) * 256],
                        in_=osl[:, xb:xb + 256]), wait=[at_], inc=[("G", 16)])
                    osl_free[pii % 4] = st_
                    pii += 1
            ph.emit()

    def emit_mini_epilogue(name, zn_bf, M_bf, brow2_bf, x_in, x_out):
        ph = Ph(k, name)
        wlf = lambda *t: [x for x in t if x]
        with ExitStack() as es:
            xsl = es.enter_context(nc.sbuf_tensor(name + "_xsl", [128, 4 * 512], F32))
            osl = es.enter_context(nc.sbuf_tensor(name + "_osl", [128, 4 * 512], F32))
            psl = [psA[:, 0:512], psA[:, 512:1024], psB[:, 0:512],
                   psB[:, 512:1024], psC[:, 0:512], psC[:, 512:1024]]
            psl_free = [None] * 6
            osl_free = [None] * 4
            xtok = {}
            pii = 0
            for tt in range(NT):
                for nb in range(8):
                    sl = psl[pii % 6]
                    xb = (pii % 4) * 512
                    eng, q = (("sync", "Q") if pii % 2 == 0
                              else ("scalar", "Adma"))
                    tx = ph.do(eng, lambda e, tt=tt, nb=nb, xb=xb: e.dma_start(
                        out=xsl[:, xb:xb + 512],
                        in_=x_in[tt * 128:(tt + 1) * 128,
                                 nb * 512:(nb + 1) * 512]),
                        wait=wlf(xtok.get(pii - 4)), inc=[(q, 16)])

                    def big(e, tt=tt, nb=nb, sl=sl):
                        nc.tensor.matmul(sl, zn_bf[:, tt * 128:(tt + 1) * 128],
                                         M_bf[:, nb * 512:(nb + 1) * 512],
                                         start=True, stop=False)
                        return nc.tensor.matmul(
                            sl, ones_bf[0:1, 0:128],
                            brow2_bf[0:1, nb * 512:(nb + 1) * 512],
                            start=False, stop=True)
                    bt_ = ph.do("tensor", big, wait=wlf(psl_free[pii % 6]),
                                inc=[("T", 1)])
                    at_ = ph.do("vector", lambda e, xb=xb, sl=sl: e.tensor_add(
                        osl[:, xb:xb + 512], sl, xsl[:, xb:xb + 512]),
                        wait=wlf(bt_, tx, osl_free[pii % 4]), inc=[("V", 1)])
                    psl_free[pii % 6] = at_
                    xtok[pii] = at_
                    st_ = ph.do("gpsimd", lambda e, tt=tt, nb=nb, xb=xb:
                                e.dma_start(
                                    out=x_out[tt * 128:(tt + 1) * 128,
                                              nb * 512:(nb + 1) * 512],
                                    in_=osl[:, xb:xb + 512]),
                                wait=[at_], inc=[("G", 16)])
                    osl_free[pii % 4] = st_
                    pii += 1
            ph.emit()

    def make_f1(tag, x_in, t_bf, routing_vec):
        def f1():
            ph = Ph(k, tag)
            with ExitStack() as es:
                xt2 = es.enter_context(nc.sbuf_tensor(tag + "_xt", [128, 2 * D], F32))

                abuf = es.enter_context(nc.sbuf_tensor(tag + "_abuf", [128, D], BF16))
                a_f32 = es.enter_context(nc.sbuf_tensor(tag + "_af", [128, D], F32))
                at_sb = es.enter_context(nc.sbuf_tensor(tag + "_at", [128, D], BF16))
                xln_bf = es.enter_context(nc.sbuf_tensor(tag + "_xln", [128, 2 * D],
                                                          BF16))
                junk_bf = xln_bf
                xlt2 = es.enter_context(nc.sbuf_tensor(tag + "_xlt", [128, 2 * D], BF16))
                t_f32 = es.enter_context(nc.sbuf_tensor(tag + "_tf32", [128, S], F32))
                abuf2_dbg = es.enter_context(nc.sbuf_tensor(tag + "_ab2", [128, D],
                                                            BF16)) if dev_dump == "p4" \
                    else None
                rsum = es.enter_context(nc.sbuf_tensor(tag + "_rsum", [128, 32], F32))
                rtmp = es.enter_context(nc.sbuf_tensor(tag + "_rtmp", [128, 32], F32))
                rm_bf = es.enter_context(nc.sbuf_tensor(tag + "_rmbf", [128, 32], BF16))
                rw_f32 = es.enter_context(nc.sbuf_tensor(tag + "_rw", [8, D], F32))
                rw_bf = es.enter_context(nc.sbuf_tensor(tag + "_rwb", [8, D], BF16))
                rwt = es.enter_context(nc.sbuf_tensor(tag + "_rwt", [128, 256], BF16))
                sm8 = es.enter_context(nc.sbuf_tensor(tag + "_sm8", [1, 8], F32))
                r8col = es.enter_context(nc.sbuf_tensor(tag + "_r8", [8, 1], F32))
                r8bf = es.enter_context(nc.sbuf_tensor(tag + "_r8b", [8, 1], BF16))
                e8_sb = es.enter_context(nc.sbuf_tensor(tag + "_e8", [8, 128], BF16))
                nm8 = es.enter_context(nc.sbuf_tensor(tag + "_nm8", [1, 1], F32))
                dd8 = es.enter_context(nc.sbuf_tensor(tag + "_dd8", [1, 1], F32))
                rb8 = es.enter_context(nc.sbuf_tensor(tag + "_rb8", [1, 8], F32))
                stat = new_stat(tag, es)
                ph.do("sync", lambda e: e.dma_start(out=a_f32[:], in_=fA),
                      inc=[("Q", 16)])
                if dev_dump == "p4" and tag == "p4":
                    ph.do("vector", lambda e: e.memset(abuf[:], 777.0), inc=[("V", 1)])
                ph.do("vector", lambda e: e.tensor_copy(abuf[:], a_f32[:]),
                      wait=ph.w("Q", "V"), inc=[("V", 1)])
                if dev_dump == "p4" and tag == "p4":
                    d10 = nc.dram_tensor("dbg_abuf0", [128, D], BF16,
                                         kind="ExternalOutput").ap()
                    ph.do("gpsimd", lambda e: e.dma_start(out=d10, in_=abuf[:]),
                          wait=ph.w("V"), inc=[("G", 16)])
                    d11 = nc.dram_tensor("dbg_abuf2", [128, D], BF16,
                                         kind="ExternalOutput").ap()
                    def spacer(e):
                        for _ in range(10):
                            ins = e.tensor_scalar_mul(junk_bf[:, 0:2048],
                                                      junk_bf[:, 0:2048], 1.0)
                        return ins
                    ph.do("vector", spacer, wait=ph.w("V"), inc=[("V", 1)])
                    ph.do("vector", lambda e: e.tensor_copy(junk_bf[:, 0:D] if False
                                                            else abuf2_dbg[:], a_f32[:]),
                          wait=ph.w("V"), inc=[("V", 1)])
                    ph.do("gpsimd", lambda e: e.dma_start(out=d11, in_=abuf2_dbg[:]),
                          wait=ph.w("V"), inc=[("G", 16)])

                def adst(e, rnd, pt, lo, hi):
                    return e.copy(at_sb[:, lo * 128:hi * 128], pt[:, 0:(hi - lo) * 128])
                tr16(ph, abuf, 32, adst)
                if dev_dump == "p4" and tag == "p4":
                    d6 = nc.dram_tensor("dbg_at_early", [128, D], BF16,
                                        kind="ExternalOutput").ap()
                    ph.do("gpsimd", lambda e: e.dma_start(out=d6, in_=at_sb[:]),
                          wait=ph.w("Sc", "V", "T"), inc=[("G", 16)])
                    d7 = nc.dram_tensor("dbg_abuf", [128, D], BF16,
                                        kind="ExternalOutput").ap()
                    ph.do("gpsimd", lambda e: e.dma_start(out=d7, in_=abuf[:]),
                          wait=ph.w("Sc", "V", "T"), inc=[("G", 16)])
                    d8 = nc.dram_tensor("dbg_af32", [128, D], F32,
                                        kind="ExternalOutput").ap()
                    ph.do("gpsimd", lambda e: e.dma_start(out=d8, in_=a_f32[:]),
                          wait=ph.w("Sc", "V", "T"), inc=[("G", 16)])
                    d9 = nc.dram_tensor("dbg_af32b", [128, D], F32,
                                        kind="ExternalOutput").ap()
                    ph.do("gpsimd", lambda e: e.dma_start(out=d9, in_=fA),
                          wait=ph.w("Sc", "V", "T"), inc=[("G", 16)])
                ph.do("vector", lambda e: e.memset(rsum[:], 0.0), inc=[("V", 1)])
                ph.do("sync", lambda e: e.dma_start(out=e8_sb[:], in_=c_e8), inc=[("Q", 16)])
                ph.do("sync", lambda e: e.dma_start(out=rw_f32[:], in_=frW), inc=[("Q", 16)])
                ph.do("sync", lambda e: e.dma_start(out=rb8[:], in_=frb[None, :]),
                      inc=[("Q", 16)])
                ph.do("vector", lambda e: e.tensor_copy(rw_bf[:], rw_f32[:]),
                      wait=ph.w("Q"), inc=[("V", 1)])
                for rnd in range(4):
                    pt = ptA if rnd % 2 == 0 else ptB

                    def tr8(e, rnd=rnd, pt=pt):
                        for j in range(8):
                            c = rnd * 8 + j
                            ins = nc.tensor.transpose(pt[:, j * 8:(j + 1) * 8],
                                                      rw_bf[:, c * 128:(c + 1) * 128],
                                                      id_bf[0:8, 0:8])
                        return ins
                    ph.do("tensor", tr8, wait=ph.w("V", "Sc"), inc=[("T", 1)])
                    ph.do("scalar", lambda e, rnd=rnd, pt=pt: e.copy(
                        rwt[:, rnd * 64:(rnd + 1) * 64], pt[:, 0:64]),
                        wait=ph.w("T"), inc=[("Sc", 1)])
                at_ready = ph.w("Sc", "T", "V")
                f1_stats = [new_stat(tag + "s0", es), new_stat(tag + "s1", es)]
                pt_free = {0: at_ready[0], 1: at_ready[0]}
                wtg = wt_gen(ph, es, pt_free) if tag == "p4" else None

                def adv(n):
                    if wtg is not None:
                        for _ in range(n):
                            next(wtg, None)
                ln_prev = [{}, {}]
                slot_free = {}
                tev_prev = None
                for tt in range(NT):
                    tb = (tt % 2) * D
                    par = tt % 2
                    prev = dict(ln_prev[par])
                    prev["dst"] = slot_free.get(par)
                    prev["xln"] = slot_free.get(("xln", par))
                    prev["stat"] = slot_free.get(("stat", par))

                    def mk_dst(e, rnd, pt, tb=tb):
                        return e.copy(xlt2[:, tb + rnd * 1024: tb + (rnd + 1) * 1024],
                                      pt[:])
                    r = ln_tile_tok(ph, x_in, tt, xt2[:, tb:tb + D], f1_stats[par],
                                    xln_bf[:, tb:tb + D], mk_dst, pt_free, prev)
                    ln_prev[par] = {"xt": r["sq"]}
                    slot_free[("xln", par)] = r["xln"]
                    slot_free[("stat", par)] = r["sq"]
                    tg = ph.do("gpsimd", lambda e, tt=tt, tb=tb: e.dma_start(
                        out=tls[tt], in_=xlt2[:, tb:tb + D]), wait=[r["evlast"]],
                        inc=[("G", 16)])

                    def tmm(e, tb=tb):
                        for dc in range(DCH):
                            ins = nc.tensor.matmul(
                                psC[:, 0:128], at_sb[:, dc * 128:(dc + 1) * 128],
                                xlt2[:, tb + dc * 128: tb + (dc + 1) * 128],
                                start=(dc == 0), stop=(dc == DCH - 1))
                        return ins
                    w = [r["evlast"]] + ([tev_prev] if tev_prev else []) + \
                        (at_ready if tt == 0 else [])
                    tmt = ph.do("tensor", tmm, wait=w, inc=[("T", 1)])
                    tev = ph.do("vector", lambda e, tt=tt: e.tensor_copy(
                        t_f32[:, tt * 128:(tt + 1) * 128], psC[:, 0:128]),
                        wait=[tmt], inc=[("V", 1)])
                    tev_prev = tev
                    slots = [tg, tmt]
                    lo = max(Q_LO, tt * 128) - tt * 128
                    hi = min(Q_HI, (tt + 1) * 128) - tt * 128
                    if hi > lo:
                        def rsel(e, tb=tb, lo=lo, hi=hi):
                            src = xlt2[:, tb:tb + D].rearrange(
                                "p (dc j) -> p dc j", dc=DCH)[:, :, lo:hi]
                            return e.reduce_sum(rtmp[:], src, axis=AX.X)
                        trs_ = ph.do("vector", rsel, wait=[r["evlast"]],
                                     inc=[("V", 1)])
                        ph.do("vector", lambda e: e.tensor_add(rsum[:], rsum[:],
                                                               rtmp[:]),
                              wait=[trs_], inc=[("V", 1)])
                        slots.append(trs_)
                    slot_free[par] = slots
                    adv(14)
                if wtg is not None:
                    for _ in wtg:
                        pass
                ph.do("vector", lambda e: e.tensor_scalar_mul(
                    rtmp[:], rsum[:], 1.0 / ROUTER_CNT), wait=ph.w("V"), inc=[("V", 1)])
                ph.do("vector", lambda e: e.tensor_copy(rm_bf[:], rtmp[:]),
                      wait=ph.w("V"), inc=[("V", 1)])

                def lmm(e):
                    for dc in range(DCH):
                        ins = nc.tensor.matmul(psC[0:8, 0:1], rwt[:, dc * 8:(dc + 1) * 8],
                                               rm_bf[:, dc:dc + 1],
                                               start=(dc == 0), stop=(dc == DCH - 1))
                    return ins
                ph.do("tensor", lmm, wait=ph.w("V", "Sc"), inc=[("T", 1)])
                ph.do("vector", lambda e: e.tensor_copy(r8col[:], psC[0:8, 0:1]),
                      wait=ph.w("T"), inc=[("V", 1)])
                ph.do("tensor", lambda e: nc.tensor.transpose(psC[0:1, 512:520], r8col[:],
                                                              id_f32[0:8, 0:8]),
                      wait=ph.w("V"), inc=[("T", 1)])

                ph.do("vector", lambda e: e.tensor_add(sm8[:], psC[0:1, 512:520],
                                                       rb8[:]),
                      wait=ph.w("T", "Q", "V"), inc=[("V", 1)])
                ph.do("vector", lambda e: e.reduce_max(nm8[:], sm8[:], axis=AX.X,
                                                       negate=True),
                      wait=ph.w("V"), inc=[("V", 1)])
                ph.do("scalar", lambda e: e.activation(sm8[:], sm8[:], AF.Exp, bias=nm8[:],
                                                       accum_out=dd8[:]),
                      wait=ph.w("V"), inc=[("Sc", 1)])

                ph.do("vector", lambda e: e.reciprocal(dd8[:], dd8[:]),
                      wait=ph.w("Sc", "V"), inc=[("V", 1)])
                ph.do("vector", lambda e: e.tensor_scalar(sm8[:], sm8[:], dd8[:], 2.0,
                                                          op0=OP.mult, op1=OP.mult),
                      wait=ph.w("V"), inc=[("V", 1)])
                ph.do("tensor", lambda e: nc.tensor.transpose(psC[0:8, 640:641], sm8[:],
                                                              id_f32[0:1, 0:1]),
                      wait=ph.w("V"), inc=[("T", 1)])
                ph.do("vector", lambda e: e.tensor_copy(r8bf[:], psC[0:8, 640:641]),
                      wait=ph.w("T"), inc=[("V", 1)])
                ph.do("tensor", lambda e: nc.tensor.matmul(psC[:, 896:897], e8_sb[:],
                                                           r8bf[:], start=True, stop=True),
                      wait=ph.w("V"), inc=[("T", 1)])
                ph.do("vector", lambda e: e.tensor_copy(routing_vec[:], psC[:, 896:897]),
                      wait=ph.w("T"), inc=[("V", 1)])
                ph.do("vector", lambda e: e.tensor_scalar_mul(t_bf[:], t_f32[:],
                                                              routing_vec[:]),
                      wait=ph.w("V"), inc=[("V", 1)])
                if dev_dump == "p4" and tag == "p4":
                    d1 = nc.dram_tensor("dbg_t", [128, S], F32, kind="ExternalOutput").ap()
                    d2 = nc.dram_tensor("dbg_tbf", [128, S], BF16,
                                        kind="ExternalOutput").ap()
                    d3 = nc.dram_tensor("dbg_rv", [128, 1], F32,
                                        kind="ExternalOutput").ap()
                    d4 = nc.dram_tensor("dbg_rsum", [128, 32], F32,
                                        kind="ExternalOutput").ap()
                    ph.do("gpsimd", lambda e: e.dma_start(out=d1, in_=t_f32[:]),
                          wait=ph.w("V", "Sc"), inc=[("G", 16)])
                    ph.do("gpsimd", lambda e: e.dma_start(out=d2, in_=t_bf[:]),
                          wait=ph.w("V", "Sc"), inc=[("G", 16)])
                    ph.do("gpsimd", lambda e: e.dma_start(out=d3, in_=routing_vec[:]),
                          wait=ph.w("V", "Sc"), inc=[("G", 16)])
                    ph.do("gpsimd", lambda e: e.dma_start(out=d4, in_=rsum[:]),
                          wait=ph.w("V", "Sc"), inc=[("G", 16)])
                    d5 = nc.dram_tensor("dbg_at", [128, D], BF16,
                                        kind="ExternalOutput").ap()
                    ph.do("gpsimd", lambda e: e.dma_start(out=d5, in_=at_sb[:]),
                          wait=ph.w("V", "Sc"), inc=[("G", 16)])
                ph.emit()
        return f1

    def make_f2_pre(bt_sb, uniq=""):
        def f2_pre(ph):
            with ExitStack() as es:
                b_f32 = es.enter_context(nc.sbuf_tensor("btb_f32" + uniq, [128, 512], F32))
                b_bf = es.enter_context(nc.sbuf_tensor("btb_bf" + uniq, [128, 2 * 512], BF16))
                btst = es.enter_context(nc.sbuf_tensor("btb_st" + uniq, [128, 32], BF16))
                for pg in range(4):
                    for ei in range(2):
                        e_idx = pg * 2 + ei
                        src = fB[e_idx].rearrange("(ob p) r -> p ob r", p=128)
                        dstb = b_f32.rearrange("p (ob r) -> p ob r", ob=32)
                        for q4 in range(4):
                            ph.do("sync", lambda e, src=src, dstb=dstb, q4=q4:
                                  e.dma_start(out=dstb[:, q4 * 8:(q4 + 1) * 8, :],
                                              in_=src[:, q4 * 8:(q4 + 1) * 8, :]),
                                  wait=ph.w("V"), inc=[("Q", 16)])
                        ph.do("vector", lambda e, ei=ei: e.tensor_copy(
                            b_bf[:, ei * 512:(ei + 1) * 512], b_f32[:]),
                            wait=ph.w("Q", "T"), inc=[("V", 1)])
                    for rnd in range(4):
                        pt = ptA if rnd % 2 == 0 else ptB
                        for j in range(8):
                            ob = rnd * 8 + j

                            def st2(e, ob=ob):
                                e.tensor_copy(btst[:, 0:16],
                                              b_bf[:, ob * 16:(ob + 1) * 16])
                                return e.tensor_copy(
                                    btst[:, 16:32],
                                    b_bf[:, 512 + ob * 16: 512 + (ob + 1) * 16])
                            ph.do("vector", st2, wait=ph.w("T", "V"), inc=[("V", 1)])
                            ph.do("tensor", lambda e, pt=pt, j=j: nc.tensor.transpose(
                                pt[0:32, j * 128:(j + 1) * 128], btst[:], id_bf[:]),
                                wait=ph.w("V", "Sc"), inc=[("T", 1)])

                        def bev(e, pg=pg, rnd=rnd, pt=pt):
                            return e.copy(bt_sb[pg * 32:(pg + 1) * 32,
                                                rnd * 1024:(rnd + 1) * 1024], pt[0:32, :])
                        bev_tok = ph.do("scalar", bev, wait=ph.w("T"), inc=[("Sc", 1)])
                if dev_dump == "p4":
                    db = nc.dram_tensor("dbg_bt", [128, D], BF16,
                                        kind="ExternalOutput").ap()
                    ph.do("gpsimd", lambda e: e.dma_start(out=db, in_=bt_sb[:]),
                          wait=ph.w("Sc", "V"), inc=[("G", 16)])
                return bev_tok
        return f2_pre

    def p6(qpack, ckv_pack, cq_bf, ckv_bf, bvu_sb, wvu5_all, ckv_tok):
        ph = Ph(k, "p6")
        with ExitStack() as es:
            xt2 = es.enter_context(nc.sbuf_tensor("p6_xt", [128, 2 * D], F32))
            xln_bf = es.enter_context(nc.sbuf_tensor("p6_xln", [128, 2 * D], BF16))
            junk_bf = xln_bf
            xlt2 = es.enter_context(nc.sbuf_tensor("p6_xlt", [128, 2 * D], BF16))
            wq_f32 = es.enter_context(nc.sbuf_tensor("p6_wqf", [128, 32 * 12], F32))
            wq_bf = es.enter_context(nc.sbuf_tensor("p6_wqb", [128, 32 * 12], BF16))
            wk_f32 = es.enter_context(nc.sbuf_tensor("p6_wkf", [128, 32 * 4], F32))
            wk_bf = es.enter_context(nc.sbuf_tensor("p6_wkb", [128, 32 * 4], BF16))
            b12 = es.enter_context(nc.sbuf_tensor("p6_b12", [1, 12], F32))
            b4 = es.enter_context(nc.sbuf_tensor("p6_b4", [1, 4], F32))
            bqd_col = es.enter_context(nc.sbuf_tensor("p6_bqd", [12, 1], F32))
            bkvd_col = es.enter_context(nc.sbuf_tensor("p6_bkvd", [4, 1], F32))
            wqk_f32 = es.enter_context(nc.sbuf_tensor("p6_wqk", [12, 128], F32))
            wqk_bf = es.enter_context(nc.sbuf_tensor("p6_wqkb", [12, 128], BF16))
            wqk_pack = es.enter_context(nc.sbuf_tensor("p6_wqkp", [12, 8 * 128], BF16))
            bqk_row = es.enter_context(nc.sbuf_tensor("p6_bqk", [1, 128], F32))
            bqk_col = es.enter_context(nc.sbuf_tensor("p6_bqkc", [128, 1], F32))
            bqk_packed = es.enter_context(nc.sbuf_tensor("p6_bqkp", [128, 1], F32))
            bvu_rows = es.enter_context(nc.sbuf_tensor("p6_bvur", [32, 128], F32))
            stat = new_stat("p6", es)
            for dc in range(DCH):
                ph.do("sync", lambda e, dc=dc: e.dma_start(
                    out=wq_f32[:, dc * 12:(dc + 1) * 12],
                    in_=bWqd[dc * 128:(dc + 1) * 128, :]), inc=[("Q", 16)])
                ph.do("sync", lambda e, dc=dc: e.dma_start(
                    out=wk_f32[:, dc * 4:(dc + 1) * 4],
                    in_=bWkvd[dc * 128:(dc + 1) * 128, :]), inc=[("Q", 16)])
            ph.do("vector", lambda e: e.tensor_copy(wq_bf[:], wq_f32[:]),
                  wait=ph.w("Q"), inc=[("V", 1)])
            ph.do("vector", lambda e: e.tensor_copy(wk_bf[:], wk_f32[:]), inc=[("V", 1)])
            ph.do("sync", lambda e: e.dma_start(out=b12[:], in_=bbqd[None, :]),
                  inc=[("Q", 16)])
            ph.do("sync", lambda e: e.dma_start(out=b4[:], in_=bbkvd[None, :]),
                  inc=[("Q", 16)])
            ph.do("sync", lambda e: e.dma_start(out=wqk_f32[:], in_=bWqk), inc=[("Q", 16)])
            ph.do("sync", lambda e: e.dma_start(out=bqk_row[:], in_=bbqk[None, :]),
                  inc=[("Q", 16)])
            ph.do("sync", lambda e: e.dma_start(
                out=bvu_rows[:], in_=bbvu.rearrange("(h j) -> h j", h=32)),
                inc=[("Q", 16)])
            ph.do("vector", lambda e: e.tensor_copy(wqk_bf[:], wqk_f32[:]),
                  wait=ph.w("Q"), inc=[("V", 1)])
            ph.do("tensor", lambda e: nc.tensor.transpose(psC[0:12, 0:1], b12[:],
                                                          id_f32[0:1, 0:1]),
                  wait=ph.w("Q"), inc=[("T", 1)])
            ph.do("vector", lambda e: e.tensor_copy(bqd_col[:], psC[0:12, 0:1]),
                  wait=ph.w("T"), inc=[("V", 1)])
            ph.do("tensor", lambda e: nc.tensor.transpose(psC[0:4, 512:513], b4[:],
                                                          id_f32[0:1, 0:1]),
                  wait=ph.w("V"), inc=[("T", 1)])
            ph.do("vector", lambda e: e.tensor_copy(bkvd_col[:], psC[0:4, 512:513]),
                  wait=ph.w("T"), inc=[("V", 1)])
            ph.do("tensor", lambda e: nc.tensor.transpose(psC[:, 640:641], bqk_row[:],
                                                          id_f32[0:1, 0:1]),
                  wait=ph.w("V"), inc=[("T", 1)])
            ph.do("vector", lambda e: e.tensor_scalar(bqk_col[:], psC[:, 640:641], 0.5,
                                                      None, op0=OP.mult),
                  wait=ph.w("T"), inc=[("V", 1)])
            ph.do("tensor", lambda e: nc.tensor.transpose(psC[:, 768:800], bvu_rows[:],
                                                          id_f32[0:32, 0:32]),
                  wait=ph.w("V"), inc=[("T", 1)])
            tbvu = ph.do("vector", lambda e: e.tensor_copy(bvu_sb[:],
                                                           psC[:, 768:800]),
                         wait=ph.w("T"), inc=[("V", 1)])
            ph.do("gpsimd", lambda e: e.memset(wqk_pack[:], 0.0), inc=[("Gc", 1)])
            ph.do("gpsimd", lambda e: e.memset(ckv_pack[:], 0.0), inc=[("Gc", 1)])
            ph.do("gpsimd", lambda e: e.memset(bqk_packed[:], 0.0), inc=[("Gc", 1)])

            def wqkp(e):
                for g in range(8):
                    for s_ in range(4):
                        hsrc = (4 * g + s_) * 4
                        ins = e.tensor_copy(
                            wqk_pack[:, g * 128 + 32 * s_: g * 128 + 32 * s_ + 4],
                            wqk_bf[:, hsrc:hsrc + 4])
                return ins
            ph.do("vector", wqkp, wait=ph.w("Gc", "V"), inc=[("V", 1)])
            w_ready = ph.w("Sc", "V", "T")
            p6_stats = [new_stat("p6s0", es), new_stat("p6s1", es)]
            pt_free = {}
            ln_prev = [{}, {}]
            slot_free = {}
            cev_tok = {}
            for tt in range(NT):
                tb = (tt % 2) * D
                par = tt % 2
                prev = dict(ln_prev[par])
                prev["dst"] = slot_free.get(par)
                prev["xln"] = slot_free.get(("xln", par))
                prev["stat"] = slot_free.get(("stat", par))

                def mk_dst(e, rnd, pt, tb=tb):
                    return e.copy(xlt2[:, tb + rnd * 1024: tb + (rnd + 1) * 1024],
                                  pt[:])
                r = ln_tile_tok(ph, xs_b, tt, xt2[:, tb:tb + D], p6_stats[par],
                                xln_bf[:, tb:tb + D], mk_dst, pt_free, prev)
                ln_prev[par] = {"xt": r["sq"]}
                slot_free[("xln", par)] = r["xln"]
                slot_free[("stat", par)] = r["sq"]
                ps = psA if tt % 2 == 0 else psB

                def cmm(e, tb=tb, ps=ps):
                    for dc in range(DCH):
                        nc.tensor.matmul(ps[0:12, 0:128], wq_bf[:, dc * 12:(dc + 1) * 12],
                                         xlt2[:, tb + dc * 128: tb + (dc + 1) * 128],
                                         start=(dc == 0), stop=(dc == DCH - 1))
                    for dc in range(DCH):
                        ins = nc.tensor.matmul(ps[0:4, 512:640],
                                               wk_bf[:, dc * 4:(dc + 1) * 4],
                                               xlt2[:, tb + dc * 128: tb + (dc + 1) * 128],
                                               start=(dc == 0), stop=(dc == DCH - 1))
                    return ins
                w = [r["evlast"]] + ([cev_tok[tt - 2]] if tt >= 2 else w_ready)
                tmc = ph.do("tensor", cmm, wait=w, inc=[("T", 1)])

                def cev(e, tt=tt, ps=ps):
                    e.tensor_scalar(cq_bf[:, tt * 128:(tt + 1) * 128], ps[0:12, 0:128],
                                    1.0, bqd_col[0:12, :], op0=OP.mult, op1=OP.add)
                    return e.tensor_scalar(ckv_bf[:, tt * 128:(tt + 1) * 128],
                                           ps[0:4, 512:640], 1.0, bkvd_col[0:4, :],
                                           op0=OP.mult, op1=OP.add)
                cev_tok[tt] = ph.do("vector", cev, wait=[tmc], inc=[("V", 1)])
                slot_free[par] = [tmc]

            def ckvp(e):
                for s_ in range(4):
                    ins = e.tensor_copy(ckv_pack[32 * s_:32 * s_ + 4, :], ckv_bf[:])
                return ins
            ph.do("vector", ckvp, wait=ph.w("Gc", "V"), inc=[("V", 1)])
            cq_ready = ph.w("V")
            vev = None
            for g in range(8):
                gts = []
                for s_ in range(4):
                    src_p = 4 * (4 * g + s_)
                    gts.append(ph.do("gpsimd", lambda e, s_=s_, src_p=src_p:
                                     e.dma_start(
                                         out=bqk_packed[32 * s_:32 * s_ + 4, :],
                                         in_=bqk_col[src_p:src_p + 4, :]),
                                     wait=([vev] if vev else ph.w("V")),
                                     inc=[("G", 16)]))
                for qh in range(2):
                    tmq = None
                    for j in range(2):
                        tmq = ph.do("tensor", lambda e, g=g, qh=qh, j=j:
                                    nc.tensor.matmul(
                                        psC[:, j * 512:(j + 1) * 512],
                                        wqk_pack[:, g * 128:(g + 1) * 128],
                                        cq_bf[:, qh * 1024 + j * 512:
                                              qh * 1024 + (j + 1) * 512],
                                        start=True, stop=True),
                                    wait=([vev] if vev else cq_ready) if j == 0
                                    else [], inc=[("T", 1)])
                    vev = ph.do("vector", lambda e, g=g, qh=qh: e.tensor_scalar(
                        qpack[:, g * S + qh * 1024: g * S + (qh + 1) * 1024],
                        psC[:, 0:1024], 0.5, bqk_packed[:], op0=OP.mult, op1=OP.add),
                        wait=[tmq] + gts, inc=[("V", 1)])

            wvu_f32 = es.enter_context(nc.sbuf_tensor("p6_wvf", [4, D], F32))
            wvu_bf = es.enter_context(nc.sbuf_tensor("p6_wvb", [4, D], BF16))
            wlf2 = lambda *t: [x for x in t if x]
            tw = ph.do("sync", lambda e: e.dma_start(out=wvu_f32[:], in_=bWvu),
                       inc=[("Q", 16)])
            twc = ph.do("vector", lambda e: e.tensor_copy(wvu_bf[:], wvu_f32[:]),
                        wait=[tw], inc=[("V", 1)])
            pt_ev = {}
            for r in range(4):
                pt = ptA if r % 2 == 0 else ptB

                def wtr(e, r=r, pt=pt):
                    for j in range(8):
                        h_ = r * 8 + j
                        ins = nc.tensor.transpose(
                            pt[:, j * 128:j * 128 + 4],
                            wvu_bf[0:4, h_ * 128:(h_ + 1) * 128],
                            id_bf[0:4, 0:4])
                    return ins
                ttr = ph.do("tensor", wtr, wait=wlf2(twc, pt_ev.get(r - 2)),
                            inc=[("T", 1)])

                def wev(e, r=r, pt=pt):
                    dst = wvu5_all.rearrange("p (h f) -> p h f", f=5)[
                        :, r * 8:(r + 1) * 8, 0:4]
                    src = pt.rearrange("p (h f) -> p h f", f=128)[:, 0:8, 0:4]
                    return e.tensor_copy(dst, src)
                pt_ev[r] = ph.do("vector", wev, wait=[ttr], inc=[("V", 1)])
            ph.do("vector", lambda e: e.tensor_copy(
                wvu5_all.rearrange("p (h f) -> p h f", f=5)[:, :, 4:5],
                bvu_sb.rearrange("p (h f) -> p h f", f=1)),
                wait=[tbvu], inc=[("V", 1)])
            for r in range(2):
                pt = ptA if r % 2 == 0 else ptB

                def ctr(e, r=r, pt=pt):
                    for j in range(8):
                        c = r * 8 + j
                        ins = nc.tensor.transpose(
                            pt[:, j * 128:j * 128 + 4],
                            ckv_bf[0:4, c * 128:(c + 1) * 128],
                            id_bf[0:4, 0:4])
                    return ins
                tct = ph.do("tensor", ctr,
                            wait=wlf2(cev_tok[r * 8 + 7], pt_ev.get(r + 2)),
                            inc=[("T", 1)])

                def cevk(e, r=r, pt=pt):
                    dst = ckv_tok.rearrange("p (h f) -> p h f", f=4)[
                        :, r * 8:(r + 1) * 8, :]
                    src = pt.rearrange("p (h f) -> p h f", f=128)[:, 0:8, 0:4]
                    return e.tensor_copy(dst, src)
                ph.do("vector", cevk, wait=[tct], inc=[("V", 1)])
            ph.emit()

    def p7(qpack, ckv_pack, wvu5_all, ckv_tok, zn_bf, M_bf, bvw_parts,
           brow2_bf):
        ph = Ph(k, "p7")
        wlf = lambda *t: [x for x in t if x]
        with ExitStack() as es:
            wo_f32 = es.enter_context(nc.sbuf_tensor("p7_wof", [128, 2048], F32))
            wo_bf = es.enter_context(nc.sbuf_tensor("p7_wob", [128, 2 * D], BF16))
            mstg = es.enter_context(nc.sbuf_tensor("p7_mst", [64, 1024], BF16))
            zstg = es.enter_context(nc.sbuf_tensor("p7_zst", [4, 1024], F32))
            zn_f32 = es.enter_context(nc.sbuf_tensor("p7_znf", [128, S], F32))
            rcp_all = es.enter_context(nc.sbuf_tensor("p7_rca", [128, 512], F32))
            RCPE = es.enter_context(nc.sbuf_tensor("p7_rce", [128, S], F32))
            rstg = es.enter_context(nc.sbuf_tensor("p7_rst", [16, 256], F32))
            bo_f32 = es.enter_context(nc.sbuf_tensor("p7_bo", [1, D], F32))
            brow2_f32 = es.enter_context(nc.sbuf_tensor("p7_br2", [1, D], F32))
            ones32 = es.enter_context(nc.sbuf_tensor("p7_o32", [32, 1], BF16))
            attn_bf2 = es.enter_context(nc.sbuf_tensor("p7_at2", [128, 2 * S], BF16))
            sc_sb2 = es.enter_context(nc.sbuf_tensor("p7_scb", [128, 2 * S], F32))
            attnT = es.enter_context(nc.sbuf_tensor("p7_aT", [128, 16 * 512], BF16))
            sm = {"mx": es.enter_context(nc.sbuf_tensor("p7_mx", [128, 4], F32)),
                  "nmx": es.enter_context(nc.sbuf_tensor("p7_nmx", [128, 2], F32)),
                  "dd": es.enter_context(nc.sbuf_tensor("p7_dd", [128, 4], F32)),
                  "rcp": es.enter_context(nc.sbuf_tensor("p7_rcp", [128, 2], F32))}
            tk = {}

            def mk_prep(h):
                wb = (h % 2) * D

                def gen():
                    casts = []
                    for c4 in range(4):
                        slot = c4 % 2

                        def wld(e, h=h, c4=c4, slot=slot):
                            return e.dma_start(
                                out=wo_f32[:, slot * 1024:(slot + 1) * 1024],
                                in_=bWo[h * 128:(h + 1) * 128,
                                        c4 * 1024:(c4 + 1) * 1024])
                        tl_ = ph.do("sync", wld,
                                    wait=wlf(casts[c4 - 2] if c4 >= 2 else
                                             tk.get("wof%d" % slot)),
                                    inc=[("Q", 16)])
                        tc = ph.do("vector", lambda e, c4=c4, slot=slot, wb=wb:
                                   e.tensor_copy(
                                       wo_bf[:, wb + c4 * 1024:
                                             wb + (c4 + 1) * 1024],
                                       wo_f32[:, slot * 1024:(slot + 1) * 1024]),
                                   wait=wlf(tl_, tk.get("wo_used_%d" % (h % 2))
                                            if c4 == 0 else None),
                                   inc=[("V", 1)])
                        casts.append(tc)
                        tk["wof%d" % slot] = tc
                        yield
                    mm_last = None
                    ev_prev = None
                    dma_prev = {}
                    ev_hist = {}
                    for nb in range(8):
                        slot = nb % 2
                        pb = slot * 32

                        def mmm(e, h=h, nb=nb, wb=wb, pb=pb):
                            return nc.tensor.matmul(
                                psC[pb:pb + 5, 512:1024],
                                wvu5_all[:, h * 5:(h + 1) * 5],
                                wo_bf[:, wb + nb * 512: wb + (nb + 1) * 512],
                                start=True, stop=True)
                        tm = ph.do("tensor", mmm,
                                   wait=wlf(casts[nb // 2],
                                            ev_hist.get(slot,
                                                        tk.get("mps%d" % slot))),
                                   inc=[("T", 1)])
                        mm_last = tm

                        def mev(e, slot=slot, pb=pb):
                            return e.tensor_copy(
                                mstg[pb:pb + 5, slot * 512:(slot + 1) * 512],
                                psC[pb:pb + 5, 512:1024])
                        ev = ph.do("vector", mev,
                                   wait=wlf(tm) + dma_prev.get(slot, []),
                                   inc=[("V", 1)])
                        ev_hist[slot] = ev
                        tk["mps%d" % slot] = ev
                        ev_prev = ev
                        d1 = ph.do("scalar", lambda e, h=h, nb=nb, slot=slot,
                                   pb=pb: e.dma_start(
                                       out=M_bf[h * 4:(h + 1) * 4,
                                                nb * 512:(nb + 1) * 512],
                                       in_=mstg[pb:pb + 4,
                                                slot * 512:(slot + 1) * 512]),
                                   wait=[ev], inc=[("Adma", 16)])
                        d2 = ph.do("scalar", lambda e, h=h, nb=nb, slot=slot,
                                   pb=pb: e.dma_start(
                                       out=bvw_parts[h:h + 1,
                                                     nb * 512:(nb + 1) * 512],
                                       in_=mstg[pb + 4:pb + 5,
                                                slot * 512:(slot + 1) * 512]),
                                   wait=[ev], inc=[("Adma", 16)])
                        dma_prev[slot] = [d1, d2]
                        if nb % 2 == 1:
                            yield
                    tk["wo_used_%d" % (h % 2)] = mm_last
                    tk["mev_last"] = ev_prev
                    tk["vt"] = None
                    tk["qk"] = []
                return gen()

            preps = [mk_prep(h) for h in range(NH)] + [None]
            for _ in preps[0]:
                pass

            zdma = {}

            def z_sink(qg, h, gg, tmm):
                slot = qg % 2
                tz = ph.do("scalar", lambda e, slot=slot: e.copy(
                    zstg[0:4, slot * 512:(slot + 1) * 512], psC[0:4, 0:512]),
                    wait=wlf(tmm, zdma.get(slot)), inc=[("Sc", 1)])
                td = ph.do("gpsimd", lambda e, h=h, gg=gg, slot=slot:
                           e.dma_start(
                               out=zn_f32[h * 4:(h + 1) * 4,
                                          gg * 512:(gg + 1) * 512],
                               in_=zstg[0:4, slot * 512:(slot + 1) * 512]),
                           wait=[tz], inc=[("G", 16)])
                zdma[slot] = td
                tk["z_last"] = tz
                return tz

            def head_args(h):
                g, s_ = h // 4, h % 4
                return ([(qpack[:, g * S:(g + 1) * S], "strip")],
                        [(ckv_pack, "strip")], ckv_tok,
                        (32 * s_, 0), s_)
            emit_attention_stream(ph, NH, head_args, attn_bf2, sc_sb2, attnT,
                                  None, sm, tk, preps, vdw=4, z_sink=z_sink,
                                  rcp_all=rcp_all)

            rv_prev = {}
            rd_prev = {}
            for h in range(NH):
                slot = h % 2
                ttr = ph.do("tensor", lambda e, h=h: nc.tensor.transpose(
                    psC[0:16, 0:128], rcp_all[:, h * 16:(h + 1) * 16],
                    id_f32[:]),
                    wait=(ph.w("V") + wlf(tk.get("z_last"))) if h == 0
                    else wlf(rv_prev.get(1 - slot)), inc=[("T", 1)])
                rv = ph.do("vector", lambda e, slot=slot: e.tensor_copy(
                    rstg[:, slot * 128:(slot + 1) * 128], psC[0:16, 0:128]),
                    wait=[ttr] + rd_prev.get(slot, []), inc=[("V", 1)])
                rv_prev[slot] = rv
                rd_prev[slot] = []
                for j in range(4):
                    eng, q = (("gpsimd", "G"), ("sync", "Q"))[j % 2]
                    rd_prev[slot].append(ph.do(
                        eng, lambda e, h=h, j=j, slot=slot: e.dma_start(
                            out=RCPE[h * 4 + j:h * 4 + j + 1, :].rearrange(
                                "q (c p) -> q c p", c=16),
                            in_=rstg[:, slot * 128:(slot + 1) * 128]),
                        wait=[rv], inc=[(q, 16)]))
            ph.do("vector", lambda e: e.tensor_tensor(
                zn_bf[:], zn_f32[:], RCPE[:], op=OP.mult),
                wait=ph.w("Q", "G"), inc=[("V", 1)])

            tone = ph.do("gpsimd", lambda e: e.memset(ones32[:], 1.0),
                         inc=[("Gc", 1)])
            tbo = ph.do("sync", lambda e: e.dma_start(out=bo_f32[:],
                                                      in_=bbo[None, :]),
                        inc=[("Q", 16)])
            tprev = None
            for nb in range(8):
                w = wlf(tprev)
                if nb == 0:
                    w = ph.w("Adma", "Gc") + wlf(tk.get("mev_last"))
                tm = ph.do("tensor", lambda e, nb=nb: nc.tensor.matmul(
                    psC[0:1, 512:1024], ones32[:],
                    bvw_parts[:, nb * 512:(nb + 1) * 512],
                    start=True, stop=True), wait=w, inc=[("T", 1)])
                tprev = ph.do("vector", lambda e, nb=nb: e.tensor_tensor(
                    brow2_f32[0:1, nb * 512:(nb + 1) * 512],
                    psC[0:1, 512:1024],
                    bo_f32[0:1, nb * 512:(nb + 1) * 512], op=OP.add),
                    wait=wlf(tm, tbo if nb == 0 else None), inc=[("V", 1)])
            ph.do("vector", lambda e: e.tensor_copy(brow2_bf[:], brow2_f32[:]),
                  wait=[tprev], inc=[("V", 1)])
            ph.emit()

    names = ["p0", "p1", "p2", "p3", "p4", "p5", "p6", "p7", "p8", "p9", "p10"]
    n_run = len(names) if n_phases is None else n_phases
    redir = names[n_run - 1] if (dev_dump == "redir" and n_run < len(names)) else None

    def xout(name, default):
        return out if name == redir else default

    if n_run >= 1:
        p0()
    if n_run >= 2:
        with (
            nc.sbuf_tensor("q_cF", [128, 12 * S], BF16) as q_cF,
            nc.sbuf_tensor("k_cF", [128, 4 * S], BF16) as k_cF,
            nc.sbuf_tensor("krF_bf", [128, S], BF16) as krF_bf,
        ):
            p1(q_cF, k_cF, krF_bf)
            if n_run >= 3:
                p2(q_cF, k_cF, krF_bf)
    if n_run >= 4:
        emit_epilogue("p3", wo_a, False, x0, xout("p3", xs_a))
    if n_run >= 5:
        with (
            nc.sbuf_tensor("t_bf", [128, S], BF16) as t_bf,
            nc.sbuf_tensor("routing_vec", [128, 1], F32) as routing_vec,
        ):
            make_f1("p4", xs_a, t_bf, routing_vec)()
            if n_run >= 6:
                with nc.sbuf_tensor("bt_sb", [128, D], BF16) as bt_sb:
                    if DEV_P5_STRIP == 1:
                        emit_epilogue("p5", wt_scr, True, xs_a, xout("p5", xs_b))
                    elif DEV_P5_STRIP == 2:
                        emit_epilogue("p5", wt_scr, True, xs_a, xout("p5", xs_b),
                                      pre=make_f2_pre(bt_sb))
                    elif DEV_P5_STRIP == 3:
                        emit_epilogue("p5", wt_scr, True, xs_a, xout("p5", xs_b),
                                      bias_src=fb, pre=make_f2_pre(bt_sb))
                    else:
                        emit_epilogue("p5", wt_scr, True, xs_a, xout("p5", xs_b),
                                      bias_src=fb, lora=(t_bf, bt_sb),
                                      pre=make_f2_pre(bt_sb))
    if n_run >= 7:
        with (
            nc.sbuf_tensor("qpack", [128, 8 * S], BF16) as qpack,
            nc.sbuf_tensor("ckv_pack", [128, S], BF16) as ckv_pack,
            nc.sbuf_tensor("cq_bf", [12, S], BF16) as cq_bf,
            nc.sbuf_tensor("ckv_bf", [4, S], BF16) as ckv_bf,
            nc.sbuf_tensor("bvu_sb", [128, 32], F32) as bvu_sb,
            nc.sbuf_tensor("wvu5_all", [128, 160], BF16) as wvu5_all,
            nc.sbuf_tensor("ckv_tok", [128, 64], BF16) as ckv_tok,
            nc.sbuf_tensor("zn_bf", [128, S], BF16) as zn_bf,
            nc.sbuf_tensor("M_bf", [128, D], BF16) as M_bf,
            nc.sbuf_tensor("bvw_parts", [32, D], BF16) as bvw_parts,
            nc.sbuf_tensor("brow2_bf", [1, D], BF16) as brow2_bf,
        ):
            p6(qpack, ckv_pack, cq_bf, ckv_bf, bvu_sb, wvu5_all, ckv_tok)
            if n_run >= 8:
                p7(qpack, ckv_pack, wvu5_all, ckv_tok, zn_bf, M_bf,
                   bvw_parts, brow2_bf)
            if n_run >= 9:
                emit_mini_epilogue("p8", zn_bf, M_bf, brow2_bf, xs_b,
                                   xout("p8", xs_a))
    if n_run >= 10:
        with (
            nc.sbuf_tensor("t_bfb", [128, S], BF16) as t_bf,
            nc.sbuf_tensor("routing_vecb", [128, 1], F32) as routing_vec,
        ):
            make_f1("p9", xs_a, t_bf, routing_vec)()
            if n_run >= 11:
                with nc.sbuf_tensor("bt_sbb", [128, D], BF16) as bt_sb:
                    emit_epilogue("p10", wt_scr, True, xs_a, out,
                                  bias_src=fb, lora=(t_bf, bt_sb),
                                  pre=make_f2_pre(bt_sb, "b"))

    if dev_dump == "victim":
        vbig = nc.dram_tensor("vbig", [128, 32768], F32, kind="ExternalInput").ap()
        d_v = nc.dram_tensor("d_v", [128, 2048], F32, kind="ExternalOutput").ap()
        phv = Ph(k, "victim")
        with ExitStack() as es:
            vb = es.enter_context(nc.sbuf_tensor("v_b", [128, 32768], F32))
            vo = es.enter_context(nc.sbuf_tensor("v_o", [128, 2048], F32))
            slack = 0
            phv.do("sync", lambda e: e.dma_start(out=vb[:], in_=vbig), inc=[("Q", 16)])
            if slack:
                phv.do("sync", lambda e: e.dma_start(out=vo[:], in_=vbig[:, 0:2048]),
                       inc=[("Q", 16)])
            wv = phv.w("Q")
            phv.do("vector", lambda e: e.tensor_copy(vo[:], vb[:, 30720:32768]),
                   wait=wv, inc=[("V", 1)])
            phv.do("gpsimd", lambda e: e.dma_start(out=d_v, in_=vo[:]),
                   wait=phv.w("V"), inc=[("G", 16)])
            phv.emit()

    if dev_dump == "tls":
        ph = Ph(k, "dump_tls")
        dt_ = nc.dram_tensor("dbg_tls", [NT, 128, D], BF16, kind="ExternalOutput").ap()
        for tt in range(NT):
            ph.do("gpsimd", lambda e, tt=tt: e.dma_start(out=dt_[tt], in_=tls[tt]),
                  wait=ph.w("G", "Sc", "V", "T"), inc=[("G", 16)])
        ph.emit()

    ph = Ph(k, "fin")
    ph.do("sync", lambda e: e.nop(),
          wait=ph.w("Q", "Adma", "G", "V", "Sc", "T", "Gc"))
    ph.do("gpsimd", lambda e: e.nop(),
          wait=ph.w("Q", "Adma", "G", "V", "Sc", "T", "Gc"))
    ph.emit()
    return nc


def _consts():
    t = np.arange(S, dtype=np.float64)
    inv = 1.0 / (10000.0 ** (np.arange(0, RD, 2, dtype=np.float64) / RD))
    f = np.concatenate([inv, inv])
    ang = np.outer(f, t)
    cos64 = np.cos(ang)
    sin64 = np.sin(ang)
    sinm = np.concatenate([-sin64[0:32], sin64[32:64]], axis=0)
    cos128 = np.concatenate([cos64, cos64], axis=0).astype(np.float32)
    sinm128 = np.concatenate([sinm, sinm], axis=0).astype(np.float32)
    e8 = np.zeros((8, 128), dtype=np.float32)
    for e in range(8):
        e8[e, e * 16:(e + 1) * 16] = 1.0
    return {
        "c_id_bf": np.eye(128, dtype=np.float32).astype(ml_dtypes.bfloat16),
        "c_id_f32": np.eye(128, dtype=np.float32),
        "c_cos": cos128,
        "c_sinm": sinm128,
        "c_ones_bf": np.ones((1, 128), dtype=np.float32).astype(ml_dtypes.bfloat16),
        "c_e8": e8.astype(ml_dtypes.bfloat16),
    }


def host_inputs(inputs):
    consts = _consts()
    common = dict(
        wqa=inputs["L0_attn_a_w_qa"], gqa=inputs["L0_attn_a_g_qa"],
        wqb=inputs["L0_attn_a_w_qb"], wkva=inputs["L0_attn_a_w_kva"],
        gkva=inputs["L0_attn_a_g_kva"], wkvb=inputs["L0_attn_a_w_kvb"],
        wo_a=inputs["L0_attn_a_w_o"],
        bWqd=inputs["L0_attn_b_Wq_d"], bbqd=inputs["L0_attn_b_bq_d"],
        bWqk=inputs["L0_attn_b_W_qk"], bbqk=inputs["L0_attn_b_b_qk"],
        bWkvd=inputs["L0_attn_b_Wkv_d"], bbkvd=inputs["L0_attn_b_bkv_d"],
        bWvu=inputs["L0_attn_b_Wv_u"], bbvu=inputs["L0_attn_b_bv_u"],
        bWo=inputs["L0_attn_b_Wo"], bbo=inputs["L0_attn_b_bo"],
        fW=inputs["L0_ffn_W"], fb=inputs["L0_ffn_b"],
        fA=np.ascontiguousarray(inputs["L0_ffn_A"], dtype=np.float32).reshape(128, D),
        fB=inputs["L0_ffn_B"], frW=inputs["L0_ffn_router_W"],
        frb=inputs["L0_ffn_router_b"],
        **consts,
    )
    return {kk: np.ascontiguousarray(v) for kk, v in common.items()}


def kernel(**inputs):
    x = np.ascontiguousarray(inputs["L0_attn_a_hidden_states"], dtype=np.float32)
    common = host_inputs(inputs)
    nc = build_nc(PHASE_LIMIT, DEV_DUMP)
    in_maps = [dict(common, x=np.ascontiguousarray(x[b])) for b in range(8)]
    res = run_bass_kernel_spmd(nc, in_maps, core_ids=list(range(8)))
    return np.stack([res.results[b]["out"] for b in range(8)], axis=0)
